# version 07
# baseline (speedup 1.0000x reference)
import math
from contextlib import ExitStack

import numpy as np
import concourse.bass as bass
import concourse.mybir as mybir

F32 = mybir.dt.float32
BF16 = mybir.dt.bfloat16
I32 = mybir.dt.int32
F16 = mybir.dt.float16
AF = mybir.ActivationFunctionType
ALU = mybir.AluOpType
AX = mybir.AxisListType

H = 2048
NCH = H // 128

CV_EPS_LN = 0
CV_ONE = 1
CV_EPS_RMS = 2
CV_HALF = 3
CV_LR = 4


class Sem:
    def __init__(self, prog, name):
        self.h = prog.ctx.enter_context(prog.nc.semaphore(name))
        self.n = 0

    def inc(self, inst, k=1):
        inst.then_inc(self.h, k)
        self.n += k
        return self.n

    def wait(self, eng, val=None):
        v = self.n if val is None else val
        if v > 0:
            eng.wait_ge(self.h, v)


class DStream:

    def __init__(self, p, name, nslots=2):
        self.sems = [Sem(p, f"{name}{j}") for j in range(nslots)]
        self.counts = [0] * nslots
        self.n = nslots
        self.k = 0
        self.use_map = []

    def issue_slot(self, slot, inst):
        self.sems[slot].inc(inst, 16)
        self.counts[slot] += 1
        self.use_map.append((slot, self.counts[slot]))
        self.k += 1
        return self.k - 1

    def issue(self, inst):
        return self.issue_slot(self.k % self.n, inst)

    def wait(self, eng, use):
        slot, cnt = self.use_map[use]
        eng.wait_ge(self.sems[slot].h, 16 * cnt)

    def wait_idx(self, eng, use):
        eng.wait_ge(self.sems[use % self.n].h, 16 * (use // self.n + 1))

    def wait_all(self, eng, thru=None):
        t = self.k if thru is None else thru + 1
        seen = {}
        for u in range(t):
            slot, cnt = self.use_map[u]
            seen[slot] = max(seen.get(slot, 0), cnt)
        for slot, cnt in seen.items():
            eng.wait_ge(self.sems[slot].h, 16 * cnt)


class Prog:
    def __init__(self):
        self.nc = bass.Bass("TRN2", target_bir_lowering=False)
        self.ctx = ExitStack()
        self.block = self.ctx.enter_context(self.nc.Block())
        self.sb_bytes = 0
        for r in ["mm", "ev", "ac", "x1", "chv", "chs", "chg"]:
            setattr(self, r, Sem(self, r))
        self.streams = {}

    def chain(self, eng, inst, which="v"):
        s = {"v": self.chv, "s": self.chs, "g": self.chg}[which]
        s.inc(inst)
        eng.wait_ge(s.h, s.n)

    def stream(self, name, nslots=2):
        if name not in self.streams:
            self.streams[name] = DStream(self, name, nslots)
        assert self.streams[name].n == nslots
        return self.streams[name]

    def named_buf(self, name, shape, dtype):
        if not hasattr(self, "_nbufs"):
            self._nbufs = {}
        if name in self._nbufs:
            t, sh, dt_ = self._nbufs[name]
            assert sh == tuple(shape) and dt_ == dtype, (name, sh, shape)
            return t
        t = self.sbuf(name, shape, dtype)
        self._nbufs[name] = (t, tuple(shape), dtype)
        return t

    def sbuf(self, name, shape, dtype):
        esz = {F32: 4, BF16: 2, I32: 4, F16: 2}[dtype]
        self.sb_bytes += shape[1] * esz
        assert self.sb_bytes < 206 * 1024, f"SBUF over budget at {name}: {self.sb_bytes}"
        return self.ctx.enter_context(self.nc.sbuf_tensor(name, shape, dtype))

    def psum_pool(self):
        return [self.ctx.enter_context(self.nc.psum_tensor(f"pp{i}", [128, 512], F32))
                for i in range(8)]

    def dram(self, name, shape, dtype, **kw):
        return self.nc.dram_tensor(name, shape, dtype, **kw)

    def finish(self):
        self.ctx.close()
        return self.nc


def rearr_rows(ap_2d, p=128):
    return ap_2d.rearrange("(a p) c -> p a c", p=p)


def emit_consts(p):
    ones_bf = p.sbuf("ones_bf", [128, 128], BF16)
    ones_f = p.sbuf("ones_f", [128, 128], F32)
    cvals = p.sbuf("cvals", [128, 8], F32)

    @p.block.vector
    def _(v):
        v.memset(ones_bf[:], 1.0)
        v.memset(cvals[:, CV_EPS_LN:CV_EPS_LN + 1], 1e-5)
        v.memset(cvals[:, CV_ONE:CV_ONE + 1], 1.0)
        v.memset(cvals[:, CV_EPS_RMS:CV_EPS_RMS + 1], 1e-6)
        v.memset(cvals[:, CV_HALF:CV_HALF + 1], 0.5)
        p.x1.inc(v.memset(ones_f[:], 1.0))
    p.cvals = cvals
    return ones_bf, ones_f, p.x1.n


def emit_weight_prep(p, specs, tag, stage, ready_map=None, piece=None):
    cst = p.stream(f"wpcast{tag}", 2)
    trs = p.stream(f"wptr{tag}", 2)
    wbs = p.stream(f"wpwb{tag}", 2)
    cast_done = {}
    tr_done = {}
    bwid = stage[0].shape[1] if piece is None else piece

    @p.block.gpsimd
    def _(g):
        for j, (w_ap, wbf, wT, o, i) in enumerate(specs):
            if w_ap is None:
                continue
            rows = max(128, min(o, (1 << 21) // max(i, 1)))
            r0 = 0
            while r0 < o:
                r1 = min(o, r0 + rows)
                cst.issue(g.dma_start(wbf.ap()[r0:r1, :],
                                      w_ap[r0:r1, :]))
                r0 = r1
            cast_done[j] = cst.k - 1

    @p.block.sync
    def _(s):
        for j, (w_ap, wbf, wT, o, i) in enumerate(specs):
            if wT is None:
                tr_done[j] = wbs.k - 1
                continue
            if w_ap is None:
                rdy = (ready_map or {}).get(j)
                if rdy is not None:
                    if isinstance(rdy, list):
                        for rs, ru in rdy:
                            rs.wait_all(s, ru)
                    else:
                        rdy[0].wait_all(s, rdy[1])
            else:
                cst.wait_all(s, cast_done[j])
            for ic in range(i // 128):
                for c0 in range(0, o, bwid):
                    c1 = min(o, c0 + bwid)
                    b = stage[trs.k % 2]
                    if trs.k >= 2:
                        wbs.wait(s, wbs.k - 2)
                    u = trs.issue(s.dma_start(
                        b[:, 0:c1 - c0],
                        wbf.ap()[c0:c1, ic * 128:(ic + 1) * 128],
                        transpose=True))
                    trs.wait(s, u)
                    wbs.issue(s.dma_start(
                        wT.ap()[ic * 128:(ic + 1) * 128, c0:c1],
                        b[:, 0:c1 - c0]))
            tr_done[j] = wbs.k - 1
    return cast_done, tr_done, wbs


def emit_ln_fm(p, x_dram, xln_sb, TOK, ps, ones_bf, ones_f, tag, t0=0,
               G=None, entry_ev=0, xln_dram=None, entry_stream=None):
    G = G or min(2048, TOK)
    nt = G // 512
    xbuf = p.named_buf("lnx", [128, 2 * G], F32)
    xbuf2 = xbuf
    p.lnx = xbuf
    xb = p.named_buf("lnxb", [128, 2 * G], BF16)
    xsq = p.named_buf("lnsq", [128, 2 * G], BF16)
    rowsT = p.named_buf("lnrows", [128, 2 * G], F32)
    row = rowsT[0:1, :]
    srow = rowsT[32:33, :]
    p.lnrows = rowsT
    mub = p.named_buf("lnmu", [128, G], F32)
    rb = p.named_buf("lnrb", [128, G], F32)
    p.lnmu = mub
    p.lnsq = xsq
    p.lnxb_t = xb

    lds = p.stream("lnld", 2)
    ld2 = p.stream("lnld2", 2)
    xst = p.stream("lnxst", 2) if xln_dram is not None else None
    xst_k0 = xst.k if xst is not None else 0
    ev0 = p.ev.n
    ac0 = p.ac.n
    mm0 = p.mm.n
    ld_use = []
    ld2_use = []

    @p.block.sync
    def _(s):
        if entry_ev:
            s.wait_ge(p.ev.h, entry_ev)
        if entry_stream is not None:
            entry_stream[0].wait_all(s, entry_stream[1])
        for c in range(NCH):
            if c >= 2:
                s.wait_ge(p.ev.h, ev0 + c - 1)
                s.wait_ge(p.ac.h, ac0 + c - 1)
            ld_use.append(lds.issue(
                s.dma_start(xbuf[:, (c % 2) * G:(c % 2 + 1) * G],
                            x_dram.ap()[c * 128:(c + 1) * 128, t0:t0 + G])))

    @p.block.vector
    def _(v):
        for c in range(NCH):
            lds.wait(v, ld_use[c])
            if c >= 2:
                v.wait_ge(p.mm.h, mm0 + c - 1)
            elif xst is not None and xst_k0 >= 2:
                xst.wait_idx(v, xst_k0 - 2 + c)
            p.ev.inc(v.tensor_copy(xb[:, (c % 2) * G:(c % 2 + 1) * G],
                                   xbuf[:, (c % 2) * G:(c % 2 + 1) * G]))

    @p.block.scalar
    def _(sc):
        for c in range(NCH):
            lds.wait(sc, ld_use[c])
            if c >= 2:
                sc.wait_ge(p.mm.h, mm0 + c - 1)
            p.ac.inc(sc.activation(xsq[:, (c % 2) * G:(c % 2 + 1) * G],
                                   xbuf[:, (c % 2) * G:(c % 2 + 1) * G],
                                   AF.Square))

    @p.block.tensor
    def _(t):
        for c in range(NCH):
            t.wait_ge(p.ev.h, ev0 + c + 1)
            t.wait_ge(p.ac.h, ac0 + c + 1)
            last = None
            for ti in range(nt):
                sl = slice((c % 2) * G + ti * 512, (c % 2) * G + (ti + 1) * 512)
                t.matmul(ps[ti][0:1, :], ones_bf[:, 0:1], xb[:, sl],
                         start=(c == 0), stop=(c == NCH - 1))
                last = t.matmul(ps[4 + ti][0:1, :], ones_bf[:, 0:1], xsq[:, sl],
                                start=(c == 0), stop=(c == NCH - 1))
            p.mm.inc(last)
    stats_mm = p.mm.n

    @p.block.vector
    def _(v):
        v.wait_ge(p.mm.h, stats_mm)
        last = None
        for ti in range(nt):
            v.tensor_copy(srow[:, ti * 512:(ti + 1) * 512], ps[ti][0:1, :])
            last = v.tensor_copy(srow[:, G + ti * 512:G + (ti + 1) * 512],
                                 ps[4 + ti][0:1, :])
        p.chain(v, last)
        p.chain(v, v.tensor_scalar_mul(row[:, 0:G], srow[:, 0:G], 1.0 / H))
        p.chain(v, v.tensor_mul(srow[:, 0:G], row[:, 0:G], row[:, 0:G]))
        p.ev.inc(v.scalar_tensor_tensor(srow[:, G:2 * G], srow[:, G:2 * G],
                                        1.0 / H, srow[:, 0:G],
                                        ALU.mult, ALU.subtract))
    var_m = p.ev.n

    @p.block.scalar
    def _(sc):
        sc.wait_ge(p.ev.h, var_m)
        p.ac.inc(sc.activation(srow[:, 0:G], srow[:, G:2 * G], AF.Sqrt,
                               bias=p.cvals[0:1, CV_EPS_LN:CV_EPS_LN + 1]))
    sd_m = p.ac.n

    @p.block.vector
    def _(v):
        v.wait_ge(p.ac.h, sd_m)
        p.ev.inc(v.reciprocal(row[:, G:2 * G], srow[:, 0:G]))
    r_m = p.ev.n

    bc_m = []

    @p.block.tensor
    def _(t):
        t.wait_ge(p.ev.h, r_m)
        for ti in range(nt):
            if ti >= 1:
                t.wait_ge(p.ev.h, r_m + ti)
            t.matmul(ps[0][:, :], ones_f[0:1, :],
                     row[:, ti * 512:(ti + 1) * 512], start=True, stop=True)
            bc_m.append(p.mm.inc(
                t.matmul(ps[1][:, :], ones_f[0:1, :],
                         row[:, G + ti * 512:G + (ti + 1) * 512],
                         start=True, stop=True)))

    @p.block.vector
    def _(v):
        for ti in range(nt):
            v.wait_ge(p.mm.h, bc_m[ti])
            v.tensor_copy(mub[:, ti * 512:(ti + 1) * 512], ps[0][:, :])
            p.ev.inc(v.tensor_copy(rb[:, ti * 512:(ti + 1) * 512], ps[1][:, :]))
    bcast_done = p.ev.n
    ap0 = p.ev.n

    @p.block.gpsimd
    def _(g):
        for c in range(NCH):
            if c >= 2:
                g.wait_ge(p.ev.h, ap0 + c - 1)
            else:
                g.wait_ge(p.mm.h, stats_mm)
            ld2_use.append(ld2.issue(
                g.dma_start(xbuf2[:, (c % 2) * G:(c % 2 + 1) * G],
                            x_dram.ap()[c * 128:(c + 1) * 128, t0:t0 + G])))

    @p.block.vector
    def _(v):
        v.wait_ge(p.ev.h, bcast_done)
        for c in range(NCH):
            ld2.wait(v, ld2_use[c])
            sl2 = slice((c % 2) * G, (c % 2 + 1) * G)
            p.chain(v, v.tensor_sub(xbuf2[:, sl2], xbuf2[:, sl2], mub[:, 0:G]))
            if xln_dram is None:
                p.ev.inc(v.tensor_mul(
                    xln_sb[:, c * TOK + t0:c * TOK + t0 + G],
                    xbuf2[:, sl2], rb[:, 0:G]))
            else:
                if c >= 2:
                    xst.wait_idx(v, xst_k0 + c - 2)
                p.ev.inc(v.tensor_mul(
                    xb[:, (c % 2) * G:(c % 2) * G + G],
                    xbuf2[:, sl2], rb[:, 0:G]))

    if xln_dram is not None:
        ap_base2 = ap0

        @p.block.sync
        def _(s):
            for c in range(NCH):
                s.wait_ge(p.ev.h, ap_base2 + c + 1)
                xst.issue(s.dma_start(
                    xln_dram.ap()[c * 128:(c + 1) * 128, t0:t0 + G],
                    xb[:, (c % 2) * G:(c % 2) * G + G]))
    return p.ev.n



def slab_load_split(stream, slot, s, dst_sb, src_ap, KC, width, nmax=8):
    u = None
    a0 = 0
    while a0 < KC:
        a1 = min(KC, a0 + nmax)
        dst = dst_sb[:, a0 * width:a1 * width].rearrange("p (a c) -> p a c",
                                                         a=a1 - a0)
        srcp = rearr_rows(src_ap[a0 * 128:a1 * 128, :])
        u = stream.issue_slot(slot, s.dma_start(dst, srcp))
        a0 = a1
    return u


def emit_ffn_gu(p, xln_sb, TOK, gT, uT, h_dram, OUT, ps, tag,
                w_ready, x_mark, wsl, tmp, htile, g_off=0, u_off=0):
    nt = TOK // 512
    nog = OUT // 512
    mm0 = p.mm.n
    ac0 = p.ac.n
    ev0 = p.ev.n
    wstr = p.stream("gusl", 2)
    hst = p.stream("guhst", 2)
    hst_k0 = hst.k
    ld_use = []

    @p.block.sync
    def _(s):
        if w_ready is not None:
            wrs, wru = w_ready
            wrs.wait_all(s, wru)
        for og in range(nog):
            if og >= 1:
                s.wait_ge(p.mm.h, mm0 + og * nt * 4 * 2)
            elif mm0 > 0:
                s.wait_ge(p.mm.h, mm0)
            src_g = gT.ap()[:, g_off + og * 512:g_off + (og + 1) * 512]
            src_u = uT.ap()[:, u_off + og * 512:u_off + (og + 1) * 512]
            slab_load_split(wstr, 0, s, wsl[0], src_g, NCH, 512)
            uu = slab_load_split(wstr, 1, s, wsl[1], src_u, NCH, 512)
            ld_use.append(uu)

    @p.block.tensor
    def _(t):
        t.wait_ge(p.ev.h, x_mark)
        it = 0
        for og in range(nog):
            wstr.wait_all(t, ld_use[og])
            for ti in range(nt):
                for ob in range(4):
                    bg = ps[0 + (it % 2)]
                    bu = ps[2 + (it % 2)]
                    if it >= 2:
                        t.wait_ge(p.ac.h, ac0 + it - 1)
                        t.wait_ge(p.ev.h, ev0 + it - 1)
                    else:
                        if ac0 > 0:
                            t.wait_ge(p.ac.h, ac0)
                        if ev0 > 0:
                            t.wait_ge(p.ev.h, ev0)
                    last = None
                    for kc in range(NCH):
                        last = t.matmul(
                            bg[:, :],
                            wsl[0][:, kc * 512 + ob * 128:kc * 512 + ob * 128 + 128],
                            xln_sb[:, kc * TOK + ti * 512:kc * TOK + (ti + 1) * 512],
                            start=(kc == 0), stop=(kc == NCH - 1))
                    p.mm.inc(last)
                    for kc in range(NCH):
                        last = t.matmul(
                            bu[:, :],
                            wsl[1][:, kc * 512 + ob * 128:kc * 512 + ob * 128 + 128],
                            xln_sb[:, kc * TOK + ti * 512:kc * TOK + (ti + 1) * 512],
                            start=(kc == 0), stop=(kc == NCH - 1))
                    p.mm.inc(last)
                    it += 1

    n_it = nog * nt * 4

    @p.block.scalar
    def _(sc):
        for it in range(n_it):
            sc.wait_ge(p.mm.h, mm0 + 2 * it + 1)
            if it >= 2:
                sc.wait_ge(p.ev.h, ev0 + it - 1)
            elif ev0 > 0:
                sc.wait_ge(p.ev.h, ev0)
            bg = ps[0 + (it % 2)]
            p.ac.inc(sc.activation(tmp[:, (it % 2) * 512:(it % 2 + 1) * 512],
                                   bg[:, :], AF.Silu))

    @p.block.vector
    def _(v):
        for it in range(n_it):
            v.wait_ge(p.mm.h, mm0 + 2 * it + 2)
            v.wait_ge(p.ac.h, ac0 + it + 1)
            if it >= 2:
                hst.wait_idx(v, hst_k0 + it - 2)
            elif hst_k0 > 0:
                hst.wait_all(v, hst_k0 - 1)
            bu = ps[2 + (it % 2)]
            p.ev.inc(v.tensor_mul(htile[:, (it % 2) * 512:(it % 2 + 1) * 512],
                                  tmp[:, (it % 2) * 512:(it % 2 + 1) * 512],
                                  bu[:, :]))

    @p.block.gpsimd
    def _(g):
        it = 0
        for og in range(nog):
            for ti in range(nt):
                for ob in range(4):
                    g.wait_ge(p.ev.h, ev0 + it + 1)
                    row0 = og * 512 + ob * 128
                    hst.issue(g.dma_start(
                        h_dram.ap()[row0:row0 + 128, ti * 512:(ti + 1) * 512],
                        htile[:, (it % 2) * 512:(it % 2 + 1) * 512]))
                    it += 1
    return hst, hst.k - 1


def emit_ffn_down(p, h_dram, INTER, TOK, dT, ps, tag, h_ready, w_ready,
                  hslab, wcol, kind, scr, comb_b=None, yr_dram=None,
                  x_dram=None, xout_dram=None, ev_pre_wait=0, st_pre=None,
                  x_pre=None):
    KC = INTER // 128
    nt = TOK // 512
    mm0 = p.mm.n
    ev0 = p.ev.n
    hstr = p.stream("dnh", 1)
    wstr = p.stream("dnw", 2)
    xld = p.stream("dnx", 2)
    yld = p.stream("dny", 2)
    ost = p.stream("dnout", 2)
    ost_k0 = ost.k
    xld_k0 = xld.k
    yld_k0 = yld.k
    hld_use = []
    wld_use = []

    n_it = nt * 16
    dst_out = xout_dram if kind == "res" else yr_dram

    def emit_store(s, it_s):
        t_s, o2_s = divmod(it_s, 16)
        s.wait_ge(p.ev.h, ev0 + it_s + 1)
        ost.issue(s.dma_start(
            dst_out.ap()[o2_s * 128:(o2_s + 1) * 128, t_s * 512:(t_s + 1) * 512],
            scr[:, 2048:3072][:, (it_s % 2) * 512:(it_s % 2 + 1) * 512]))

    @p.block.sync
    def _(s):
        hrs, hru = h_ready
        hrs.wait_all(s, hru)
        wrs, wru = w_ready
        wrs.wait_all(s, wru)
        hstr.wait_all(s)
        wstr.wait_all(s)
        ost.wait_all(s)
        if ev_pre_wait:
            s.wait_ge(p.ev.h, ev_pre_wait)
        it = 0
        for t in range(nt):
            if t >= 1:
                s.wait_ge(p.mm.h, mm0 + t * 16)
            src = h_dram.ap()[:, t * 512:(t + 1) * 512]
            hld_use.append(slab_load_split(hstr, 0, s, hslab, src, KC, 512))
            for o2 in range(16):
                if it >= 2:
                    s.wait_ge(p.mm.h, mm0 + it - 1)
                srcw = dT.ap()[:, o2 * 128:(o2 + 1) * 128]
                wld_use.append(slab_load_split(wstr, it % 2, s, wcol[it % 2],
                                               srcw, KC, 128))
                if it >= 2:
                    emit_store(s, it - 2)
                it += 1
        emit_store(s, n_it - 2)
        emit_store(s, n_it - 1)

    if kind == "res":
        @p.block.gpsimd
        def _(g):
            if st_pre is not None:
                st_pre[0].wait_all(g, st_pre[1])
            if x_pre is not None:
                x_pre[0].wait_all(g, x_pre[1])
            it = 0
            for t in range(nt):
                for o2 in range(16):
                    if it >= 2:
                        g.wait_ge(p.ev.h, ev0 + it - 1)
                    sl = slice((it % 2) * 512, (it % 2 + 1) * 512)
                    xld.issue(g.dma_start(scr[:, 0:1024][:, sl],
                                          x_dram.ap()[o2 * 128:(o2 + 1) * 128,
                                                      t * 512:(t + 1) * 512]))
                    yld.issue(g.dma_start(scr[:, 1024:2048][:, sl],
                                          yr_dram.ap()[o2 * 128:(o2 + 1) * 128,
                                                       t * 512:(t + 1) * 512]))
                    it += 1

    @p.block.tensor
    def _(t_):
        it = 0
        for t in range(nt):
            hstr.wait(t_, hld_use[t])
            for o2 in range(16):
                wstr.wait(t_, wld_use[it])
                if it >= 2:
                    t_.wait_ge(p.ev.h, ev0 + it - 1)
                elif ev0 > 0:
                    t_.wait_ge(p.ev.h, ev0)
                bank = ps[4 + (it % 2)]
                last = None
                for kc in range(KC):
                    last = t_.matmul(bank[:, :],
                                     wcol[it % 2][:, kc * 128:(kc + 1) * 128],
                                     hslab[:, kc * 512:(kc + 1) * 512],
                                     start=(kc == 0), stop=(kc == KC - 1))
                p.mm.inc(last)
                it += 1

    @p.block.vector
    def _(v):
        it = 0
        for t in range(nt):
            for o2 in range(16):
                v.wait_ge(p.mm.h, mm0 + it + 1)
                bank = ps[4 + (it % 2)]
                sl = slice((it % 2) * 512, (it % 2 + 1) * 512)
                if it >= 2:
                    ost.wait(v, ost_k0 + it - 2)
                elif ost_k0 > 0:
                    ost.wait_all(v, ost_k0 - 1)
                if kind == "yr":
                    if ev_pre_wait:
                        v.wait_ge(p.ev.h, ev_pre_wait)
                    p.ev.inc(v.tensor_mul(scr[:, 2048:3072][:, sl], bank[:, :],
                                          comb_b[:, t * 512:(t + 1) * 512]))
                else:
                    xld.wait(v, xld_k0 + it)
                    yld.wait(v, yld_k0 + it)
                    p.chain(v, v.tensor_add(scr[:, 2048:3072][:, sl], bank[:, :],
                                            scr[:, 1024:2048][:, sl]))
                    p.ev.inc(v.scalar_tensor_tensor(
                        scr[:, 2048:3072][:, sl], scr[:, 2048:3072][:, sl], 0.5,
                        scr[:, 0:1024][:, sl], ALU.mult, ALU.add))
                it += 1

    return ost, ost.k - 1


def emit_router_comb(p, xln_sb, TOK, router_w_ap, ps, ones_f, comb_b, tag,
                     x_mark, ident_f, ident_ready=None):
    nb = TOK // 128
    rsb = p.sbuf(f"rt_w_{tag}", [8, H], F32)
    rT = p.sbuf(f"rt_wT_{tag}", [128, NCH * 8], BF16)
    lg = p.sbuf(f"rt_lg_{tag}", [128, 16], F32)
    col = p.sbuf(f"rt_col_{tag}", [128, 8], F32)
    combc = p.sbuf(f"rt_comb_{tag}", [128, nb], F32)
    Gln = min(2048, TOK)
    crow = p.lnrows[0:1, Gln:Gln + TOK]

    rld = p.stream("rtld", 1)

    @p.block.gpsimd
    def _(g):
        rld.issue(g.dma_start(rsb[:], router_w_ap))
    rld_use = rld.k - 1

    ev_rt0 = p.ev.n
    tr_m = []

    @p.block.tensor
    def _(t):
        rld.wait(t, rld_use)
        t.wait_ge(p.x1.h, 1)
        if ident_ready is not None:
            ident_ready[0].wait(t, ident_ready[1])
        for c in range(NCH):
            if c >= 2:
                t.wait_ge(p.ev.h, ev_rt0 + c - 1)
            bank = ps[7] if c % 2 == 0 else ps[5]
            tr_m.append(p.mm.inc(
                t.transpose(bank[0:128, 0:8],
                            rsb[:, c * 128:(c + 1) * 128], ident_f[0:8, 0:8])))

    @p.block.vector
    def _(v):
        for c in range(NCH):
            v.wait_ge(p.mm.h, tr_m[c])
            bank = ps[7] if c % 2 == 0 else ps[5]
            p.ev.inc(v.tensor_copy(rT[:, c * 8:(c + 1) * 8], bank[0:128, 0:8]))
    rT_done = p.ev.n

    evA0 = p.ev.n
    lg_m = []

    @p.block.tensor
    def _(t):
        t.wait_ge(p.ev.h, rT_done)
        t.wait_ge(p.ev.h, x_mark)
        for tb in range(nb):
            if tb >= 2:
                t.wait_ge(p.ev.h, evA0 + 2 * (tb - 2) + 1)
            last = None
            lbank = ps[6] if tb % 2 == 0 else ps[4]
            for c in range(NCH):
                last = t.matmul(lbank[:, 0:8],
                                xln_sb[:, c * TOK + tb * 128:c * TOK + (tb + 1) * 128],
                                rT[:, c * 8:(c + 1) * 8],
                                start=(c == 0), stop=(c == NCH - 1))
            lg_m.append(p.mm.inc(last))

    ac0 = p.ac.n

    @p.block.vector
    def _(v):
        for tb in range(nb):
            v.wait_ge(p.mm.h, lg_m[tb])
            lbank = ps[6] if tb % 2 == 0 else ps[4]
            p.chain(v, v.tensor_copy(lg[:, 0:8], lbank[:, 0:8]))
            p.chain(v, v.reduce_max(col[:, 0:1], lg[:, 0:8], axis=AX.X))
            p.chain(v, v.tensor_scalar(lg[:, 8:16], lg[:, 0:8], col[:, 0:1],
                                       None, ALU.is_equal))
            p.chain(v, v.scalar_tensor_tensor(lg[:, 8:16], lg[:, 8:16], -1e30,
                                              lg[:, 0:8], ALU.mult, ALU.add))
            p.chain(v, v.reduce_max(col[:, 1:2], lg[:, 8:16], axis=AX.X))
            p.ev.inc(v.tensor_scalar_mul(col[:, 4:5], col[:, 0:1], -1.0))
            v.wait_ge(p.ac.h, ac0 + tb + 1)
            p.chain(v, v.tensor_scalar_add(col[:, 3:4], col[:, 2:3], 1.0))
            p.chain(v, v.reciprocal(col[:, 3:4], col[:, 3:4]))
            p.chain(v, v.tensor_mul(col[:, 5:6], col[:, 2:3], col[:, 3:4]))
            p.ev.inc(v.tensor_add(combc[:, tb:tb + 1], col[:, 5:6],
                                  col[:, 3:4]))

    @p.block.scalar
    def _(sc):
        for tb in range(nb):
            sc.wait_ge(p.ev.h, evA0 + 2 * tb + 1)
            p.ac.inc(sc.activation(col[:, 2:3], col[:, 1:2], AF.Exp,
                                   bias=col[:, 4:5]))

    evB0 = evA0 + 2 * nb
    trc_m = []

    @p.block.tensor
    def _(t):
        for tb in range(nb):
            t.wait_ge(p.ev.h, evA0 + 2 * tb + 2)
            if tb >= 2:
                t.wait_ge(p.ev.h, evB0 + tb - 1)
            cbank = ps[7] if tb % 2 == 0 else ps[5]
            trc_m.append(p.mm.inc(
                t.transpose(cbank[0:1, 0:128],
                            combc[:, tb:tb + 1], ident_f[:, :])))

    @p.block.vector
    def _(v):
        for tb in range(nb):
            v.wait_ge(p.mm.h, trc_m[tb])
            cbank = ps[7] if tb % 2 == 0 else ps[5]
            p.ev.inc(v.tensor_copy(
                crow[:, tb * 128:(tb + 1) * 128], cbank[0:1, 0:128]))
    crow_done = p.ev.n
    evC0 = p.ev.n
    bc_m = []

    @p.block.tensor
    def _(t):
        t.wait_ge(p.ev.h, crow_done)
        for ti in range(TOK // 512):
            if ti >= 1:
                t.wait_ge(p.ev.h, evC0 + ti)
            bc_m.append(p.mm.inc(
                t.matmul(ps[6][:, :], ones_f[0:1, :],
                         crow[:, ti * 512:(ti + 1) * 512],
                         start=True, stop=True)))

    @p.block.vector
    def _(v):
        for ti in range(TOK // 512):
            v.wait_ge(p.mm.h, bc_m[ti])
            p.ev.inc(v.tensor_copy(comb_b[:, ti * 512:(ti + 1) * 512],
                                   ps[6][:, :]))
    return p.ev.n


def build_launch1(TOK=2048):
    p = Prog()
    ps = p.psum_pool()

    x_in = p.dram("x_in", [H, TOK], F32, kind="ExternalInput")
    ident_in = p.dram("ident_in", [128, 128], F32, kind="ExternalInput")
    w_names = ["r_gate", "r_up", "r_down", "sh_gate", "sh_up", "sh_down"]
    w_shapes = {"sh_gate": (8192, H), "sh_up": (8192, H), "sh_down": (H, 8192),
                "r_gate": (1024, H), "r_up": (1024, H), "r_down": (H, 1024)}
    w_in = {n: p.dram(f"w_{n}", list(w_shapes[n]), F32, kind="ExternalInput")
            for n in w_names}
    router_in = p.dram("w_router", [8, H], F32, kind="ExternalInput")
    x1_out = p.dram("x1_out", [H, TOK], F32, kind="ExternalOutput")

    wbf = {n: p.dram(f"wbf_{n}", list(w_shapes[n]), BF16) for n in w_names}
    wT = {n: p.dram(f"wT_{n}", [w_shapes[n][1], w_shapes[n][0]], BF16)
          for n in w_names}
    h_dram = p.dram("h_d", [8192, TOK], BF16)
    hr_dram = p.dram("hr_d", [1024, TOK], BF16)
    yr_dram = p.dram("yr_d", [H, TOK], F32)

    ones_bf, ones_f, _ = emit_consts(p)
    ident_f = p.sbuf("ident_f", [128, 128], F32)
    ild = p.stream("identld", 1)

    @p.block.gpsimd
    def _(g):
        ild.issue(g.dma_start(ident_f[:], ident_in.ap()))

    A = p.sbuf("A", [128, NCH * TOK], BF16)
    xln_mark = emit_ln_fm(p, x_in, A, TOK, ps, ones_bf, ones_f, "l1")

    stage = [p.sbuf(f"stage{j}", [128, 8192], BF16) for j in range(2)]
    bounce = p.sbuf("wpbounce", [128, 2 * 4096], BF16)
    bslots = [bounce[:, 0:4096], bounce[:, 4096:8192]]
    specs = [(w_in[n].ap(), wbf[n], wT[n], *w_shapes[n]) for n in w_names]
    cast_m, tr_m, wbs = emit_weight_prep(p, specs, "l1", bslots, piece=4096)
    prep_all = (wbs, wbs.k - 1)
    prep_thru = {n: (wbs, tr_m[j]) for j, n in enumerate(w_names)}

    comb_b = p.lnmu
    comb_mark = emit_router_comb(p, A, TOK, router_in.ap(), ps, ones_f, comb_b,
                                 "l1", xln_mark, ident_f, ident_ready=(ild, 0))

    tmp = p.sbuf("gu_tmp", [128, 2 * 512], BF16)
    htile = p.sbuf("gu_ht", [128, 2 * 512], BF16)
    hr_ready = emit_ffn_gu(p, A, TOK, wT["r_gate"], wT["r_up"], hr_dram, 1024,
                           ps, "rgu", prep_thru["r_up"], xln_mark, stage, tmp,
                           htile)

    hrslab = (p.lnsq if 2 * min(2048, TOK) >= 8 * 512
              else p.sbuf("hrslab", [128, 8 * 512], BF16))
    rwcol = [p.sbuf(f"rwcol{j}", [128, 8 * 128], BF16) for j in range(2)]
    scr = (p.lnx if 2 * min(2048, TOK) >= 3072
           else p.sbuf("scr", [128, 3072], F32))
    yr_ready = emit_ffn_down(p, hr_dram, 1024, TOK, wT["r_down"], ps, "rdn",
                             hr_ready, prep_thru["r_down"], hrslab, rwcol,
                             "yr", scr, comb_b=comb_b, yr_dram=yr_dram,
                             ev_pre_wait=comb_mark)

    h_ready = emit_ffn_gu(p, A, TOK, wT["sh_gate"], wT["sh_up"], h_dram, 8192,
                          ps, "sgu", prep_thru["sh_up"], xln_mark, stage, tmp,
                          htile)

    hs_big = (A if NCH * TOK >= 64 * 512
              else p.sbuf("hslab_big", [128, 64 * 512], BF16))
    ost, ost_u = emit_ffn_down(p, h_dram, 8192, TOK, wT["sh_down"], ps, "sdn",
                               h_ready, prep_thru["sh_down"], hs_big, stage,
                               "res", scr,
                               yr_dram=yr_dram, x_dram=x_in, xout_dram=x1_out,
                               ev_pre_wait=comb_mark, st_pre=yr_ready)

    @p.block.gpsimd
    def _(g):
        ost.wait_all(g)
    return p


BASE_LR_INV = math.log(math.expm1(0.001))
CV_EPS_L2 = 5


def hview(ap2d, d, j0, j1):
    return ap2d.rearrange("p (h d) -> p h d", d=d)[:, :, j0:j1]


def build_launch2(TOK=2048):
    p = Prog()
    ps = p.psum_pool()
    p.evq = Sem(p, "evq")

    x_in = p.dram("x_in", [H, TOK], F32, kind="ExternalInput")
    qkv_w = p.dram("qkv_w", [3 * H, H], F32, kind="ExternalInput")
    bnames = ["qnb", "knb", "s0b", "s1b", "o0b", "o1b"]
    bcast_in = {n: p.dram(f"{n}_in", [128, H], F32, kind="ExternalInput")
                for n in bnames}
    lrb_in = p.dram("lrb_in", [128, 12], F32, kind="ExternalInput")
    lr_w = p.dram("lr_w", [16, H], F32, kind="ExternalInput")
    ctab = p.dram("ctab", [TOK, 4 * 1024], BF16, kind="ExternalInput")

    onames = ["qa", "ka", "va", "fq", "fk", "fv"]
    outs = {n: p.dram(f"{n}_out", [TOK, H], BF16, kind="ExternalOutput")
            for n in onames}
    lr_out = p.dram("lr_out", [TOK, 12], F32, kind="ExternalOutput")

    qkvbf = p.dram("qkvbf", [3 * H, H], BF16)
    qkvT = p.dram("qkvT", [H, 3 * H], BF16)

    ones_bf, ones_f, _ = emit_consts(p)

    @p.block.vector
    def _(v):
        v.memset(p.cvals[:, CV_EPS_L2:CV_EPS_L2 + 1], 1e-12)

    lrbb = p.sbuf("lrbb", [128, 12], F32)
    lrT16 = p.sbuf("lrT16", [128, NCH * 16], BF16)
    lrwbf = p.dram("lrwbf", [16, H], BF16)

    ild = p.stream("identld", 1)

    @p.block.gpsimd
    def _(g):
        ild.issue(g.dma_start(lrwbf.ap(), lr_w.ap()))
        ild.issue(g.dma_start(lrbb[:], lrb_in.ap()))
    rows_ld = ild.k - 1

    ltr = p.stream("lrtr", 1)

    @p.block.sync
    def _(s):
        ild.wait_all(s, rows_ld)
        for c in range(NCH):
            ltr.issue(s.dma_start(lrT16[:, c * 16:(c + 1) * 16],
                                  lrwbf.ap()[:, c * 128:(c + 1) * 128],
                                  transpose=True))
    lrt_ld = ltr.k - 1

    @p.block.vector
    def _(v):
        ild.wait_all(v, rows_ld)
        p.x1.inc(v.tensor_scalar_add(lrbb[:], lrbb[:], BASE_LR_INV))
    lrbb_mark = p.x1.n

    xln_d = p.dram("xln_d", [H, TOK], BF16)
    Gln = min(1024, TOK)
    xm = 0
    for t0 in range(0, TOK, Gln):
        xm = emit_ln_fm(p, x_in, None, TOK, ps, ones_bf, ones_f, "l2", t0=t0,
                        G=Gln, entry_ev=xm, xln_dram=xln_d)
    xln_mark = xm
    xst_all = p.streams["lnxst"]
    A_grp = [p.sbuf(f"Agrp{j}", [128, NCH * 512], BF16) for j in range(2)]
    astr = p.stream("agld", 2)

    bt = {n: p.sbuf(f"bt_{n}", [128, H], BF16) for n in bnames}
    bstr = p.stream("bcld", 1)

    @p.block.gpsimd
    def _(g):
        for n in bnames:
            bstr.issue(g.dma_start(bt[n][:], bcast_in[n].ap()))
    bts_ld = bstr.k - 1

    stage = [p.sbuf(f"stage{j}", [128, 8192], BF16) for j in range(2)]
    cast_m, tr_m, wbs = emit_weight_prep(
        p, [(qkv_w.ap(), qkvbf, qkvT, 3 * H, H)], "l2", stage)
    prep_all = (wbs, wbs.k - 1)

    qg = p.sbuf("qg", [128, 4 * 6144], BF16)
    ctile = p.sbuf("ctile", [128, 4 * 1024], BF16)
    dvq = p.sbuf("dvq", [128, H], BF16)
    dvk = p.sbuf("dvk", [128, H], BF16)
    dv3 = p.sbuf("dv3", [128, H], BF16)
    dv4 = p.sbuf("dv4", [128, H], BF16)
    qa_st = p.sbuf("qa_st", [128, H], BF16)
    dcol = p.sbuf("dcol", [128, 16], F32)
    lrs = p.sbuf("lrs", [128, 48], F32)

    wstr = p.stream("gusl", 2)
    cstr = p.stream("ctabld", 1)
    st_s = {n: p.stream(f"st_{n}", 1) for n in onames + ["lr"]}

    ngrp = TOK // 512
    mm0 = p.mm.n
    evq0 = p.evq.n
    wld_use = {}
    lrmm = {}
    M = {}

    ag_use = {}

    @p.block.sync
    def _(s):
        wrs, wru = prep_all
        wrs.wait_all(s, wru)
        xst_all.wait_all(s)
        k = 0
        for g in range(ngrp):
            if g >= 2:
                s.wait_ge(p.mm.h, mm0 + (g - 1) * 52)
            ag_use[g] = slab_load_split(
                astr, g % 2, s, A_grp[g % 2],
                xln_d.ap()[:, g * 512:(g + 1) * 512], NCH, 512)
            for os_ in range(12):
                if k >= 2:
                    kp = k - 2
                    s.wait_ge(p.mm.h,
                              mm0 + (kp // 12) * 52 + (kp % 12 + 1) * 4)
                wld_use[(g, os_)] = slab_load_split(
                    wstr, k % 2, s, stage[k % 2],
                    qkvT.ap()[:, os_ * 512:(os_ + 1) * 512], NCH, 512)
                k += 1

    @p.block.tensor
    def _(t):
        ltr.wait_all(t, lrt_ld)
        k = 0
        for g in range(ngrp):
            astr.wait(t, ag_use[g])
            for os_ in range(12):
                wstr.wait(t, wld_use[(g, os_)])
                for tb in range(4):
                    idx = os_ * 4 + tb
                    if g * 52 + idx >= 2:
                        t.wait_ge(p.evq.h, evq0 + g * 52 + idx - 1)
                    bank = ps[idx % 2]
                    last = None
                    for kc in range(NCH):
                        last = t.matmul(
                            bank[:, :],
                            A_grp[g % 2][:, kc * 512 + tb * 128:
                                         kc * 512 + (tb + 1) * 128],
                            stage[k % 2][:, kc * 512:(kc + 1) * 512],
                            start=(kc == 0), stop=(kc == NCH - 1))
                    p.mm.inc(last)
                k += 1
            for tb in range(4):
                t.wait_ge(p.evq.h, evq0 + g * 52 + 48 + tb)
                last = None
                for kc in range(NCH):
                    last = t.matmul(
                        ps[4][:, 0:12],
                        A_grp[g % 2][:, kc * 512 + tb * 128:
                                     kc * 512 + (tb + 1) * 128],
                        lrT16[:, kc * 16:kc * 16 + 12],
                        start=(kc == 0), stop=(kc == NCH - 1))
                lrmm[(g, tb)] = p.mm.inc(last)

    @p.block.gpsimd
    def _(g_):
        cstr.issue(g_.dma_start(ctile[:], ctab.ap()[0:128, :]))

    for g in range(ngrp):
        @p.block.vector
        def _(v, g=g):
            for os_ in range(12):
                for tb in range(4):
                    idx = os_ * 4 + tb
                    v.wait_ge(p.mm.h, mm0 + g * 52 + idx + 1)
                    if g > 0 and idx == 0:
                        v.wait_ge(p.ac.h, M[("fv", g - 1, 3)])
                        st_s["va"].wait(v, st_s["va"].k - 1)
                        st_s["fv"].wait(v, st_s["fv"].k - 1)
                        st_s["ka"].wait(v, st_s["ka"].k - 1)
                    bank = ps[idx % 2]
                    p.evq.inc(v.tensor_copy(
                        qg[:, tb * 6144 + os_ * 512:tb * 6144 + (os_ + 1) * 512],
                        bank[:, :]))

        for tb in range(4):
            q_sl = qg[:, tb * 6144:tb * 6144 + 2048]
            k_sl = qg[:, tb * 6144 + 2048:tb * 6144 + 4096]
            v_sl = qg[:, tb * 6144 + 4096:tb * 6144 + 6144]

            @p.block.scalar
            def _(sc, g=g, tb=tb, q_sl=q_sl, k_sl=k_sl):
                sc.wait_ge(p.evq.h, evq0 + g * 52 + 48)
                p.chain(sc, sc.activation(dv3[:], q_sl, AF.Square,
                                          accum_out=dcol[:, 0:1]), "s")
                p.chain(sc, sc.activation(dv3[:], k_sl, AF.Square,
                                          accum_out=dcol[:, 1:2]), "s")
                p.chain(sc, sc.activation(
                    dcol[:, 2:4], dcol[:, 0:2], AF.Ln, scale=1.0 / H,
                    bias=p.cvals[:, CV_EPS_RMS:CV_EPS_RMS + 1]), "s")
                p.ac.inc(sc.activation(dcol[:, 4:6], dcol[:, 2:4], AF.Exp,
                                       scale=-0.5))
            M[("rstd", g, tb)] = p.ac.n

            @p.block.vector
            def _(v, g=g, tb=tb, q_sl=q_sl, k_sl=k_sl, k_sl_v=k_sl):
                v.wait_ge(p.ac.h, M[("rstd", g, tb)])
                bstr.wait_all(v, bts_ld)
                if g + tb > 0:
                    st_s["fq"].wait(v, st_s["fq"].k - 1)
                    st_s["fk"].wait(v, st_s["fk"].k - 1)
                p.chain(v, v.scalar_tensor_tensor(dvq[:], q_sl, dcol[:, 4:5],
                                                  bt["qnb"][:], ALU.mult,
                                                  ALU.mult))
                p.chain(v, v.scalar_tensor_tensor(dvk[:], k_sl, dcol[:, 5:6],
                                                  bt["knb"][:], ALU.mult,
                                                  ALU.mult))
                cstr.wait(v, 4 * g + tb)
                if g + tb > 0:
                    st_s["qa"].wait(v, st_s["qa"].k - 1)
                    st_s["ka"].wait(v, st_s["ka"].k - 1)
                T = hview(dv3[:, 0:1024], 64, 0, 64)
                for (src, cof, dst) in ((dvq, 0, qa_st), (dvk, 2048, k_sl_v)):
                    x1 = hview(src[:], 128, 0, 64)
                    x2 = hview(src[:], 128, 64, 128)
                    o1 = hview(dst[:], 128, 0, 64)
                    o2 = hview(dst[:], 128, 64, 128)
                    cc = hview(ctile[:, cof:cof + 1024], 64, 0, 64)
                    ss = hview(ctile[:, cof + 1024:cof + 2048], 64, 0, 64)
                    p.chain(v, v.tensor_mul(o1, x1, cc))
                    p.chain(v, v.tensor_mul(T, x2, ss))
                    p.chain(v, v.tensor_sub(o1, o1, T))
                    p.chain(v, v.tensor_mul(o2, x2, cc))
                    p.chain(v, v.tensor_mul(T, x1, ss))
                    p.chain(v, v.tensor_add(o2, o2, T))
                p.ev.inc(v.engine_nop())
                M[("rope", g, tb)] = p.chv.n
                p.chain(v, v.tensor_mul(dv3[:], dvq[:], bt["s0b"][:]))
                p.ev.inc(v.tensor_add(dv4[:], dv3[:], bt["o0b"][:]))
            M[("ropeev", g, tb)] = p.ev.n - 1
            M[("fqpre", g, tb)] = p.ev.n

            @p.block.scalar
            def _(sc, g=g, tb=tb):
                sc.wait_ge(p.ev.h, M[("fqpre", g, tb)])
                p.chain(sc, sc.activation(dv3[:], dv4[:], AF.Silu), "s")
                for j in range(4):
                    p.chain(sc, sc.activation(
                        dv4[:, j * 512:(j + 1) * 512],
                        dv3[:, j * 512:(j + 1) * 512], AF.Square,
                        accum_out=dcol[:, 8 + j:9 + j]), "s")
                p.chain(sc, sc.activation(
                    dcol[:, 12:16], dcol[:, 8:12], AF.Ln,
                    bias=p.cvals[:, CV_EPS_L2:CV_EPS_L2 + 1]), "s")
                p.ac.inc(sc.activation(dcol[:, 8:12], dcol[:, 12:16], AF.Exp,
                                       scale=-0.5))
            M[("fqr", g, tb)] = p.ac.n

            @p.block.vector
            def _(v, g=g, tb=tb):
                v.wait_ge(p.ac.h, M[("fqr", g, tb)])
                for j in range(4):
                    p.chain(v, v.tensor_scalar_mul(
                        dvq[:, j * 512:(j + 1) * 512],
                        dv3[:, j * 512:(j + 1) * 512], dcol[:, 8 + j:9 + j]))
                p.ev.inc(v.engine_nop())
                M[("fq", g, tb)] = p.ev.n
                p.chain(v, v.tensor_mul(dv4[:], dvk[:], bt["s1b"][:]))
                p.ev.inc(v.tensor_add(dv3[:], dv4[:], bt["o1b"][:]))
            M[("fkpre", g, tb)] = p.ev.n

            @p.block.scalar
            def _(sc, g=g, tb=tb, v_sl=v_sl, q_sl_=q_sl):
                sc.wait_ge(p.ev.h, M[("fkpre", g, tb)])
                p.chain(sc, sc.activation(dv4[:], dv3[:], AF.Silu), "s")
                for j in range(4):
                    p.chain(sc, sc.activation(
                        dv3[:, j * 512:(j + 1) * 512],
                        dv4[:, j * 512:(j + 1) * 512], AF.Square,
                        accum_out=dcol[:, 8 + j:9 + j]), "s")
                p.chain(sc, sc.activation(
                    dcol[:, 12:16], dcol[:, 8:12], AF.Ln,
                    bias=p.cvals[:, CV_EPS_L2:CV_EPS_L2 + 1]), "s")
                p.chain(sc, sc.activation(dcol[:, 8:12], dcol[:, 12:16],
                                          AF.Exp, scale=-0.5), "s")
                if g + tb > 0:
                    st_s["fv"].wait(sc, st_s["fv"].k - 1)
                p.ac.inc(sc.activation(q_sl_, v_sl, AF.Silu))
            M[("fkr", g, tb)] = p.ac.n
            M[("fv", g, tb)] = p.ac.n

            @p.block.vector
            def _(v, g=g, tb=tb):
                v.wait_ge(p.ac.h, M[("fkr", g, tb)])
                for j in range(4):
                    p.chain(v, v.tensor_scalar_mul(
                        dvk[:, j * 512:(j + 1) * 512],
                        dv4[:, j * 512:(j + 1) * 512], dcol[:, 8 + j:9 + j]))
                p.ev.inc(v.engine_nop())
                M[("fk", g, tb)] = p.ev.n
                v.wait_ge(p.mm.h, lrmm[(g, tb)])
                if g + tb > 0:
                    st_s["lr"].wait(v, st_s["lr"].k - 1)
                p.evq.inc(v.tensor_add(lrs[:, 0:12], ps[4][:, 0:12],
                                       lrbb[:]))
                v.wait_ge(p.evq.h, p.evq.n)
                p.chain(v, v.tensor_scalar_max(lrs[:, 16:28], lrs[:, 0:12],
                                               0.0))
                p.ev.inc(v.engine_nop())
            M[("lrm", g, tb)] = p.ev.n

            @p.block.scalar
            def _(sc, g=g, tb=tb):
                sc.wait_ge(p.ev.h, M[("lrm", g, tb)])
                p.chain(sc, sc.activation(lrs[:, 32:44], lrs[:, 0:12], AF.Abs),
                        "s")
                p.chain(sc, sc.activation(lrs[:, 0:12], lrs[:, 32:44], AF.Exp,
                                          scale=-1.0), "s")
                p.ac.inc(sc.activation(lrs[:, 32:44], lrs[:, 0:12], AF.Ln,
                                       bias=p.cvals[:, CV_ONE:CV_ONE + 1]))
            M[("lrl", g, tb)] = p.ac.n

            @p.block.vector
            def _(v, g=g, tb=tb):
                v.wait_ge(p.ac.h, M[("lrl", g, tb)])
                p.ev.inc(v.tensor_add(lrs[:, 0:12], lrs[:, 16:28],
                                      lrs[:, 32:44]))
            M[("lr", g, tb)] = p.ev.n

            @p.block.gpsimd
            def _(g_, g=g, tb=tb, v_sl=v_sl):
                r0 = g * 512 + tb * 128
                g_.wait_ge(p.ev.h, M[("ropeev", g, tb)])
                st_s["qa"].issue(g_.dma_start(outs["qa"].ap()[r0:r0 + 128, :],
                                              qa_st[:]))
                st_s["ka"].issue(g_.dma_start(outs["ka"].ap()[r0:r0 + 128, :],
                                              qg[:, tb * 6144 + 2048:
                                                 tb * 6144 + 4096]))
                st_s["va"].issue(g_.dma_start(outs["va"].ap()[r0:r0 + 128, :],
                                              v_sl))
                g_.wait_ge(p.ev.h, M[("fq", g, tb)])
                st_s["fq"].issue(g_.dma_start(outs["fq"].ap()[r0:r0 + 128, :],
                                              dvq[:]))
                g_.wait_ge(p.ev.h, M[("fk", g, tb)])
                g_.wait_ge(p.ac.h, M[("fv", g, tb)])
                st_s["fk"].issue(g_.dma_start(outs["fk"].ap()[r0:r0 + 128, :],
                                              dvk[:]))
                st_s["fv"].issue(g_.dma_start(outs["fv"].ap()[r0:r0 + 128, :],
                                              qg[:, tb * 6144:tb * 6144 + 2048]))
                g_.wait_ge(p.ev.h, M[("lr", g, tb)])
                st_s["lr"].issue(g_.dma_start(lr_out.ap()[r0:r0 + 128, :],
                                              lrs[:, 0:12]))
                if (g, tb) != (ngrp - 1, 3):
                    nt_ = (g, tb + 1) if tb < 3 else (g + 1, 0)
                    g_.wait_ge(p.chv.h, M[("rope", g, tb)])
                    cstr.issue(g_.dma_start(
                        ctile[:], ctab.ap()[nt_[0] * 512 + nt_[1] * 128:
                                            nt_[0] * 512 + (nt_[1] + 1) * 128, :]))

    @p.block.gpsimd
    def _(g_):
        for n in onames + ["lr"]:
            st_s[n].wait_all(g_)
    return p


def build_launch3(S=8192, QTOK=2048, CHUNK=512):
    FD = 512
    p = Prog()
    ps = [p.ctx.enter_context(p.nc.psum_tensor(f"pp{i}", [128, 512], F32))
          for i in range(6)]
    pb = [p.ctx.enter_context(p.nc.psum_tensor(f"pbt{i}", [128, 128], BF16))
          for i in range(2)]

    fq_u = p.dram("fq_u", [S, FD], BF16, kind="ExternalInput")
    fk_u = p.dram("fk_u", [S, FD], BF16, kind="ExternalInput")
    fv_u = p.dram("fv_u", [S, FD], BF16, kind="ExternalInput")
    lr_u = p.dram("lr_u", [S, 4], F32, kind="ExternalInput")
    w0_in = p.dram("w0_in", [FD, FD], F32, kind="ExternalInput")
    w1_in = p.dram("w1_in", [FD, FD], F32, kind="ExternalInput")
    w2_in = p.dram("w2_in", [FD, FD], F32, kind="ExternalInput")
    tttnb_in = p.dram("tttnb_in", [128, FD], F32, kind="ExternalInput")
    ident_in = p.dram("ident_in", [128, 128], F32, kind="ExternalInput")
    qa_own = p.dram("qa_own", [QTOK, H], BF16, kind="ExternalInput")
    ka_ctx = p.dram("ka_ctx", [2 * QTOK, H], BF16, kind="ExternalInput")
    va_ctx = p.dram("va_ctx", [2 * QTOK, H], BF16, kind="ExternalInput")
    lo_in = p.dram("lo_in", [128, 1], F32, kind="ExternalInput")

    ottt_out = p.dram("ottt_out", [S, FD], BF16, kind="ExternalOutput")
    attn_out = p.dram("attn_out", [QTOK, H], BF16, kind="ExternalOutput")

    ones_bf, ones_f, _ = emit_consts(p)
    ident_f = p.sbuf("ident_f", [128, 128], F32)
    ident_b = p.sbuf("ident_b", [128, 128], BF16)
    tttnb = p.sbuf("tttnb", [128, FD], F32)
    lo_sb = p.sbuf("lo_sb", [128, 4], F32)

    ild = p.stream("identld", 1)

    @p.block.gpsimd
    def _(g):
        ild.issue(g.dma_start(ident_f[:], ident_in.ap()))
        ild.issue(g.dma_start(ident_b[:], ident_in.ap()))
        ild.issue(g.dma_start(tttnb[:], tttnb_in.ap()))
        ild.issue(g.dma_start(lo_sb[:, 0:1], lo_in.ap()))
    consts_ld = ild.k - 1

    w0f = p.sbuf("w0f", [128, 4 * FD], F32)
    w2f = p.sbuf("w2f", [128, 4 * FD], F32)
    w1f_ed = p.sbuf("w1f_ed", [128, 4 * FD], F32)
    w1f_de = p.sbuf("w1f_de", [128, 4 * FD], F32)
    w0b = p.sbuf("w0b", [128, 4 * FD], BF16)
    w2b = p.sbuf("w2b", [128, 4 * FD], BF16)
    w1b_ed = p.sbuf("w1b_ed", [128, 4 * FD], BF16)
    w1b_de = p.sbuf("w1b_de", [128, 4 * FD], BF16)
    cbuf = [p.sbuf(f"cb{j}", [128, 5 * 4 * FD], BF16) for j in range(2)]
    lrc = [p.sbuf(f"lrc{j}", [128, 4 * 4], F32) for j in range(2)]
    hqT = p.sbuf("hqT", [128, 4 * FD], BF16)
    gate = p.sbuf("gate", [128, 4 * FD], BF16)
    sig = p.sbuf("sig", [128, 4 * FD], BF16)
    g1sb = p.sbuf("g1sb", [128, 4 * FD], BF16)
    g2sb = p.sbuf("g2sb", [128, 4 * FD], BF16)
    dhsb = p.sbuf("dhsb", [128, 4 * FD], BF16)
    hid = p.sbuf("hid", [128, 4 * FD], BF16)
    fbuf = p.sbuf("fbuf", [128, 4 * FD], BF16)
    d1l = p.sbuf("d1l", [128, 4 * FD], BF16)
    d2l = p.sbuf("d2l", [128, 4 * FD], BF16)
    l1v = p.sbuf("l1v", [128, 4 * FD], BF16)
    tsb = p.sbuf("tsb", [128, FD], BF16)
    osb = p.sbuf("osb", [128, 2 * FD], BF16)
    dcol = p.sbuf("dcol", [128, 8], F32)

    NC_ = S // CHUNK

    wstage = p.sbuf("wstage", [128, 4 * FD], F32)
    NC_ = S // CHUNK

    wld = p.stream("wld", 1)
    ost = p.stream("ottt_st", 2)
    cld = p.stream("chld", 2)

    w1ld = p.stream("w1ld", 1)

    @p.block.sync
    def _(s):
        w1ld.issue(s.dma_start(w1f_de[:].rearrange("p (a c) -> p a c", a=4),
                               rearr_rows(w1_in.ap())))
    w1de_ld = w1ld.k - 1

    for mi, (win, carry, bcopy) in enumerate(
            [(w0_in, w0f, w0b), (w2_in, w2f, w2b), (w1_in, w1f_ed, w1b_ed)]):
        ev_tr0 = p.ev.n

        @p.block.sync
        def _(s, win=win, mi=mi):
            if mi == 0:
                ild.wait_all(s, consts_ld)
            else:
                s.wait_ge(p.ev.h, ev_tr0)
            wld.issue(s.dma_start(wstage[:].rearrange("p (a c) -> p a c", a=4),
                                  rearr_rows(win.ap())))
        ldm = wld.k - 1
        trm = []

        @p.block.tensor
        def _(t, ldm=ldm):
            wld.wait(t, ldm)
            ild.wait_all(t, consts_ld)
            for blk in range(16):
                i_, j_ = blk // 4, blk % 4
                if blk >= 2:
                    t.wait_ge(p.ev.h, ev_tr0 + blk - 1)
                trm.append(p.mm.inc(t.transpose(
                    ps[blk % 2][:, 0:128],
                    wstage[:, i_ * 512 + j_ * 128:i_ * 512 + (j_ + 1) * 128],
                    ident_f[:])))

        @p.block.vector
        def _(v, carry=carry, bcopy=bcopy, trm=trm):
            for blk in range(16):
                i_, j_ = blk // 4, blk % 4
                v.wait_ge(p.mm.h, trm[blk])
                dst = carry[:, j_ * 512 + i_ * 128:j_ * 512 + (i_ + 1) * 128]
                p.chain(v, v.tensor_copy(dst, ps[blk % 2][:, 0:128]))
                p.ev.inc(v.tensor_copy(
                    bcopy[:, j_ * 512 + i_ * 128:j_ * 512 + (i_ + 1) * 128],
                    dst))

    @p.block.vector
    def _(v):
        w1ld.wait(v, w1de_ld)
        p.ev.inc(v.tensor_copy(w1b_de[:], w1f_de[:]))
    init_done = p.ev.n

    ch_done_mm = {}
    ch_ld = {}

    def emit_chunk_loads(c):
        @p.block.sync
        def _(s):
            sl = c % 2
            if c >= 2:
                s.wait_ge(p.mm.h, ch_done_mm[c - 2])
            base = c * CHUNK
            for j, src_t in enumerate((fq_u, fk_u, fv_u)):
                for dch in range(4):
                    cld.issue_slot(sl, s.dma_start(
                        cbuf[sl][:, j * 2048 + dch * 512:
                                 j * 2048 + (dch + 1) * 512],
                        src_t.ap()[base:base + CHUNK,
                                   dch * 128:(dch + 1) * 128],
                        transpose=True))
            cld.issue_slot(sl, s.dma_start(
                cbuf[sl][:, 3 * 2048:4 * 2048].rearrange("p (a c) -> p a c", a=4),
                rearr_rows(fk_u.ap()[base:base + CHUNK, :])))
            cld.issue_slot(sl, s.dma_start(
                cbuf[sl][:, 4 * 2048:5 * 2048].rearrange("p (a c) -> p a c", a=4),
                rearr_rows(fv_u.ap()[base:base + CHUNK, :])))
            cld.issue_slot(sl, s.dma_start(
                lrc[sl][:].rearrange("p (a c) -> p a c", a=4),
                rearr_rows(lr_u.ap()[base:base + CHUNK, :])))
        return cld.k - 1

    ch_ld[0] = emit_chunk_loads(0)
    if NC_ > 1:
        ch_ld[1] = emit_chunk_loads(1)

    bfm = {-1: init_done}
    for c in range(NC_):
        sl = c % 2
        qcT = cbuf[sl][:, 0:2048]
        kcT = cbuf[sl][:, 2048:4096]
        vcT = cbuf[sl][:, 4096:6144]
        kc = cbuf[sl][:, 6144:8192]
        vc = cbuf[sl][:, 8192:10240]
        lr3 = lrc[sl]

        mmA0, acA0, evA0 = p.mm.n, p.ac.n, p.ev.n

        @p.block.tensor
        def _(t, c=c, qcT=qcT, mmA0=mmA0, acA0=acA0, evA0=evA0):
            cld.wait_all(t, ch_ld[c])
            t.wait_ge(p.ev.h, bfm[c - 1])
            for eb in range(4):
                if eb >= 1:
                    t.wait_ge(p.ac.h, acA0 + eb)
                    t.wait_ge(p.ev.h, evA0 + eb)
                for wmat, bank in ((w0b, ps[0]), (w2b, ps[1])):
                    last = None
                    for j in range(4):
                        last = t.matmul(
                            bank[:, :],
                            wmat[:, j * 512 + eb * 128:j * 512 + (eb + 1) * 128],
                            qcT[:, j * 512:(j + 1) * 512],
                            start=(j == 0), stop=(j == 3))
                    p.mm.inc(last)

        @p.block.scalar
        def _(sc, mmA0=mmA0, evA0=evA0):
            for eb in range(4):
                sc.wait_ge(p.mm.h, mmA0 + 2 * eb + 1)
                if eb >= 1:
                    sc.wait_ge(p.ev.h, evA0 + eb)
                p.ac.inc(sc.activation(tsb[:], ps[0][:, :], AF.Silu))

        @p.block.vector
        def _(v, mmA0=mmA0, acA0=acA0):
            for eb in range(4):
                v.wait_ge(p.mm.h, mmA0 + 2 * eb + 2)
                v.wait_ge(p.ac.h, acA0 + eb + 1)
                p.ev.inc(v.tensor_mul(hqT[:, eb * 512:(eb + 1) * 512],
                                      tsb[:], ps[1][:, :]))

        mmG0, acG0, evG0 = p.mm.n, p.ac.n, p.ev.n

        @p.block.tensor
        def _(t, kcT=kcT, vcT=vcT, acG0=acG0, evG0=evG0):
            for cb in range(4):
                if cb >= 1:
                    t.wait_ge(p.ac.h, acG0 + 2 * cb)
                    t.wait_ge(p.ev.h, evG0 + 3 * cb)
                for lmat, rmat, bank in ((kcT, w0b, ps[2]), (kcT, w2b, ps[3]),
                                         (vcT, w1b_de, ps[4])):
                    last = None
                    for j in range(4):
                        last = t.matmul(
                            bank[:, :],
                            lmat[:, j * 512 + cb * 128:j * 512 + (cb + 1) * 128],
                            rmat[:, j * 512:(j + 1) * 512],
                            start=(j == 0), stop=(j == 3))
                    p.mm.inc(last)

        @p.block.scalar
        def _(sc, mmG0=mmG0):
            for cb in range(4):
                sc.wait_ge(p.mm.h, mmG0 + 3 * cb + 1)
                p.ac.inc(sc.activation(gate[:, cb * 512:(cb + 1) * 512],
                                       ps[2][:, :], AF.Silu))
                p.ac.inc(sc.activation(sig[:, cb * 512:(cb + 1) * 512],
                                       ps[2][:, :], AF.Sigmoid))

        @p.block.vector
        def _(v, mmG0=mmG0, acG0=acG0):
            for cb in range(4):
                v.wait_ge(p.mm.h, mmG0 + 3 * cb + 3)
                v.wait_ge(p.ac.h, acG0 + 2 * cb + 2)
                p.ev.inc(v.tensor_copy(g1sb[:, cb * 512:(cb + 1) * 512],
                                       ps[2][:, :]))
                p.ev.inc(v.tensor_copy(g2sb[:, cb * 512:(cb + 1) * 512],
                                       ps[3][:, :]))
                p.ev.inc(v.tensor_copy(dhsb[:, cb * 512:(cb + 1) * 512],
                                       ps[4][:, :]))

        @p.block.vector
        def _(v, vc=vc, lr3=lr3, acG0=acG0, evG0=evG0):
            v.wait_ge(p.ac.h, acG0 + 8)
            v.wait_ge(p.ev.h, evG0 + 12)
            p.chain(v, v.tensor_mul(hid[:], gate[:], g2sb[:]))
            p.chain(v, v.tensor_scalar(fbuf[:], sig[:], -1.0, 1.0,
                                       ALU.mult, ALU.add))
            p.chain(v, v.tensor_mul(fbuf[:], g1sb[:], fbuf[:]))
            p.chain(v, v.tensor_scalar_add(fbuf[:], fbuf[:], 1.0))
            p.chain(v, v.tensor_mul(fbuf[:], sig[:], fbuf[:]))
            p.chain(v, v.tensor_mul(d1l[:], dhsb[:], g2sb[:]))
            for cb in range(4):
                slc = slice(cb * 512, (cb + 1) * 512)
                p.chain(v, v.scalar_tensor_tensor(
                    d1l[:, slc], d1l[:, slc], lr3[:, cb * 4:cb * 4 + 1],
                    fbuf[:, slc], ALU.mult, ALU.mult))
                p.chain(v, v.scalar_tensor_tensor(
                    d2l[:, slc], dhsb[:, slc], lr3[:, cb * 4 + 2:cb * 4 + 3],
                    gate[:, slc], ALU.mult, ALU.mult))
                p.chain(v, v.tensor_scalar_mul(
                    l1v[:, slc], vc[:, slc], lr3[:, cb * 4 + 1:cb * 4 + 2]))
            p.ev.inc(v.engine_nop())
        e_done = p.ev.n

        mmO0, acO0, evO0 = p.mm.n, p.ac.n, p.ev.n
        ostk0 = ost.k

        @p.block.tensor
        def _(t, acO0=acO0, evO0=evO0):
            for cb in range(4):
                if cb >= 1:
                    t.wait_ge(p.ac.h, acO0 + cb)
                    t.wait_ge(p.ev.h, evO0 + cb)
                last = None
                for j in range(4):
                    last = t.matmul(
                        ps[5][:, :],
                        hqT[:, j * 512 + cb * 128:j * 512 + (cb + 1) * 128],
                        w1b_ed[:, j * 512:(j + 1) * 512],
                        start=(j == 0), stop=(j == 3))
                p.mm.inc(last)

        @p.block.scalar
        def _(sc, mmO0=mmO0):
            for cb in range(4):
                sc.wait_ge(p.mm.h, mmO0 + cb + 1)
                p.chain(sc, sc.activation(tsb[:], ps[5][:, :], AF.Square,
                                          accum_out=dcol[:, 0:1]), "s")
                p.chain(sc, sc.activation(
                    dcol[:, 1:2], dcol[:, 0:1], AF.Ln, scale=1.0 / FD,
                    bias=p.cvals[:, CV_EPS_RMS:CV_EPS_RMS + 1]), "s")
                p.ac.inc(sc.activation(dcol[:, 2:3], dcol[:, 1:2], AF.Exp,
                                       scale=-0.5))

        @p.block.vector
        def _(v, acO0=acO0, ostk0=ostk0):
            for cb in range(4):
                v.wait_ge(p.ac.h, acO0 + cb + 1)
                if ostk0 + cb >= 2:
                    ost.wait_idx(v, ostk0 + cb - 2)
                p.ev.inc(v.scalar_tensor_tensor(
                    osb[:, (cb % 2) * FD:(cb % 2 + 1) * FD], ps[5][:, :],
                    dcol[:, 2:3], tttnb[:], ALU.mult, ALU.mult))

        @p.block.gpsimd
        def _(g_, c=c, evO0=evO0):
            for cb in range(4):
                g_.wait_ge(p.ev.h, evO0 + cb + 1)
                r0 = c * CHUNK + cb * 128
                ost.issue(g_.dma_start(
                    ottt_out.ap()[r0:r0 + 128, :],
                    osb[:, (cb % 2) * FD:(cb % 2 + 1) * FD]))

        mmU0, evU0 = p.mm.n, p.ev.n
        upd = [
            (hid, l1v, w1f_ed, w1b_ed),
            (l1v, hid, w1f_de, w1b_de),
            (kc, d1l, w0f, w0b),
            (kc, d2l, w2f, w2b),
        ]

        @p.block.tensor
        def _(t, upd=upd, e_done=e_done, evO0=evO0, evU0=evU0):
            t.wait_ge(p.ev.h, e_done)
            t.wait_ge(p.ev.h, evO0 + 4)
            u = 0
            for (lmat, rmat, cf, cbw) in upd:
                for blk in range(4):
                    if u >= 1:
                        t.wait_ge(p.ev.h, evU0 + 2 * u)
                    last = None
                    for j in range(4):
                        last = t.matmul(
                            ps[5][:, :],
                            lmat[:, j * 512 + blk * 128:j * 512 + (blk + 1) * 128],
                            rmat[:, j * 512:(j + 1) * 512],
                            start=(j == 0), stop=(j == 3))
                    p.mm.inc(last)
                    u += 1

        @p.block.vector
        def _(v, upd=upd, mmU0=mmU0):
            u = 0
            for (lmat, rmat, cf, cbw) in upd:
                for blk in range(4):
                    v.wait_ge(p.mm.h, mmU0 + u + 1)
                    slc = slice(blk * 512, (blk + 1) * 512)
                    p.chain(v, v.tensor_add(cf[:, slc], cf[:, slc],
                                            ps[5][:, :]))
                    p.ev.inc(v.engine_nop())
                    p.ev.inc(v.tensor_copy(cbw[:, slc], cf[:, slc]))
                    u += 1
        bfm[c] = p.ev.n
        ch_done_mm[c] = p.mm.n
        if c + 2 < NC_:
            ch_ld[c + 2] = emit_chunk_loads(c + 2)

    ttt_done_ev = p.ev.n
    ttt_done_mm = p.mm.n

    NQB = QTOK // 128
    WTILES = {}
    for qb in range(NQB):
        lo_t = (qb * 128 + 1) // 512
        hi_t = (qb * 128 + 2048 + 127) // 512
        WTILES[qb] = (lo_t, min(hi_t, 2 * QTOK // 512 - 1))

    s_sb2 = p.sbuf("s_sb", [128, 2 * 5 * 512], F32)
    psb2 = p.sbuf("psb", [128, 2 * 5 * 512], BF16)
    pT_sb = p.sbuf("pT_sb", [128, 2 * 128], BF16)
    msk = [p.sbuf(f"msk{j}", [128, 5 * 512], BF16) for j in range(2)]
    attn_col = p.sbuf("attn_col", [128, NQB * 128], BF16)
    mcol2 = p.sbuf("mcol", [128, 32], F32)
    mcol = mcol2[:, 0:16]
    Jt = p.sbuf("Jt", [128, 512], F32)
    icol = p.sbuf("icol", [128, 4], F32)
    mbld = p.sbuf("mbld", [128, 2 * 512], F32)

    mask_d = p.dram("mask_d", [QTOK, 5 * 512], BF16)

    kld = p.stream("attnkld", 2)
    mld = p.stream("attnmld", 2)
    ast = p.stream("attnst", 2)
    mst = p.stream("maskst", 2)

    @p.block.gpsimd
    def _(g):
        g.iota(Jt[:], [[1, 512]], channel_multiplier=0,
               allow_small_or_imprecise_dtypes=True)
        p.chg.inc(g.iota(icol[:, 0:1], [[1, 1]], channel_multiplier=1,
                         allow_small_or_imprecise_dtypes=True))
        g.wait_ge(p.chg.h, p.chg.n)
    iota_m = p.chg.n

    mq_marks = {}

    @p.block.vector
    def _(v):
        v.wait_ge(p.chg.h, iota_m)
        ild.wait_all(v, consts_ld)
        for qb in range(NQB):
            lo_t, hi_t = WTILES[qb]
            for kt in range(lo_t, hi_t + 1):
                kti = kt - lo_t
                if kti == 0 and qb >= 1:
                    mst.wait_idx(v, qb - 1)
                slc = slice(kti * 512, (kti + 1) * 512)
                p.chain(v, v.tensor_scalar_add(mcol[:, 0:1], icol[:, 0:1],
                                               float(qb * 128 - kt * 512)))
                p.chain(v, v.tensor_scalar_add(mcol[:, 2:3], mcol[:, 0:1],
                                               float(2048.0)))
                p.chain(v, v.tensor_scalar_add(mcol[:, 1:2], lo_sb[:, 0:1],
                                               float(-kt * 512)))
                p.chain(v, v.tensor_max(mcol[:, 0:1], mcol[:, 0:1],
                                        mcol[:, 1:2]))
                p.chain(v, v.tensor_scalar(mbld[:, 0:512], Jt[:],
                                           mcol[:, 0:1], None, ALU.is_gt))
                p.chain(v, v.tensor_scalar(mbld[:, 512:1024], Jt[:],
                                           mcol[:, 2:3], None, ALU.is_le))
                p.chain(v, v.tensor_mul(mbld[:, 0:512], mbld[:, 0:512],
                                        mbld[:, 512:1024]))
                p.ev.inc(v.tensor_scalar(msk[0][:, slc], mbld[:, 0:512],
                                         1e9, -1e9, ALU.mult, ALU.add))
            mq_marks[qb] = p.ev.n

    @p.block.gpsimd
    def _(g):
        for qb in range(NQB):
            g.wait_ge(p.ev.h, mq_marks[qb])
            lo_t, hi_t = WTILES[qb]
            nkt = hi_t - lo_t + 1
            mst.issue(g.dma_start(
                mask_d.ap()[qb * 128:(qb + 1) * 128, 0:nkt * 512],
                msk[0][:, 0:nkt * 512]))


    kslab = cbuf
    kh_ld = {}

    def emit_head_loads(h):
        @p.block.sync
        def _(s):
            sl = h % 2
            if h < 2:
                s.wait_ge(p.mm.h, ttt_done_mm)
                s.wait_ge(p.ev.h, ttt_done_ev)
            else:
                s.wait_ge(p.mm.h, h_done_mm[h - 2])
            kld.issue_slot(sl, s.dma_start(
                kslab[sl][:, 0:2 * QTOK],
                ka_ctx.ap()[:, h * 128:(h + 1) * 128], transpose=True))
            kld.issue_slot(sl, s.dma_start(
                kslab[sl][:, 2 * QTOK:4 * QTOK].rearrange(
                    "p (a c) -> p a c", a=2 * QTOK // 128),
                rearr_rows(va_ctx.ap()[:, h * 128:(h + 1) * 128])))
            kld.issue_slot(sl, s.dma_start(
                kslab[sl][:, 4 * QTOK:5 * QTOK],
                qa_own.ap()[:, h * 128:(h + 1) * 128], transpose=True))
        return kld.k - 1

    h_done_mm = {}
    kh_ld[0] = emit_head_loads(0)
    if H // 128 > 1:
        kh_ld[1] = emit_head_loads(1)

    mt_ld = {}
    ac_attn0 = p.ac.n

    def emit_mask_loads(h):
        @p.block.gpsimd
        def _(g):
            if h == 0:
                mst.wait_all(g)
            for qb in range(NQB):
                it = h * NQB + qb
                lo_t, hi_t = WTILES[qb]
                nkt = hi_t - lo_t + 1
                if it >= 2:
                    g.wait_ge(p.ac.h, ac_attn0 + it - 1)
                mt_ld[(h, qb)] = mld.issue_slot(it % 2, g.dma_start(
                    msk[it % 2][:, 0:nkt * 512],
                    mask_d.ap()[qb * 128:(qb + 1) * 128, 0:nkt * 512]))

    it_g = 0
    prev_evac_last = [0]
    trp_hist = [0, 0]
    prev_scale = [0]
    prev_pv_last = [0]
    for h in range(16):
        emit_mask_loads(h)
        sl = h % 2
        kaT = kslab[sl][:, 0:2 * QTOK]
        va_s = kslab[sl][:, 2 * QTOK:4 * QTOK]
        qaT = kslab[sl][:, 4 * QTOK:5 * QTOK]
        for qb in range(NQB):
            lo_t, hi_t = WTILES[qb]
            nkt = hi_t - lo_t + 1
            mmq0 = p.mm.n
            evq0 = p.ev.n
            u2 = it_g % 2
            s_sb = s_sb2[:, u2 * 2560:u2 * 2560 + 2560]
            psb = psb2[:, u2 * 2560:u2 * 2560 + 2560]
            mcol = mcol2[:, u2 * 16:u2 * 16 + 16]

            @p.block.tensor
            def _(t, h=h, qb=qb, lo_t=lo_t, nkt=nkt, kaT=kaT, qaT=qaT,
                  it_g=it_g, mmq0=mmq0, evq0=evq0):
                if qb == 0:
                    kld.wait_all(t, kh_ld[h])
                for kti in range(nkt):
                    kt = lo_t + kti
                    if kti >= 2:
                        t.wait_ge(p.ev.h, evq0 + kti - 1)
                    elif it_g > 0:
                        t.wait_ge(p.ev.h, prev_evac_last[0])
                    p.mm.inc(t.matmul(
                        ps[kti % 2][:, :], qaT[:, qb * 128:(qb + 1) * 128],
                        kaT[:, kt * 512:(kt + 1) * 512],
                        start=True, stop=True))

            @p.block.vector
            def _(v, h=h, qb=qb, nkt=nkt, it_g=it_g, mmq0=mmq0, evq0=evq0,
                  s_sb=s_sb, mcol=mcol):
                mld.wait(v, mt_ld[(h, qb)])
                if it_g > 1:
                    v.wait_ge(p.ac.h, ac_attn0 + it_g - 1)
                for kti in range(nkt):
                    v.wait_ge(p.mm.h, mmq0 + kti + 1)
                    p.chain(v, v.tensor_tensor(
                        s_sb[:, kti * 512:(kti + 1) * 512],
                        ps[kti % 2][:, :],
                        msk[it_g % 2][:, kti * 512:(kti + 1) * 512],
                        ALU.add))
                    p.ev.inc(v.reduce_max(mcol[:, 4 + kti:5 + kti],
                                          s_sb[:, kti * 512:(kti + 1) * 512],
                                          axis=AX.X))
                v.wait_ge(p.ev.h, evq0 + nkt)
                p.chain(v, v.reduce_max(mcol[:, 0:1], mcol[:, 4:4 + nkt],
                                        axis=AX.X))
                p.ev.inc(v.tensor_scalar_mul(mcol[:, 1:2], mcol[:, 0:1], -1.0))
            evac_last = evq0 + nkt
            negm_m = evq0 + nkt + 1
            prev_evac_last[0] = evac_last

            @p.block.scalar
            def _(sc, nkt=nkt, negm_m=negm_m, it_g=it_g, psb=psb, s_sb=s_sb,
                  mcol=mcol):
                sc.wait_ge(p.ev.h, negm_m)
                if it_g > 1:
                    sc.wait_ge(p.mm.h, trp_hist[0])
                p.ac.inc(sc.activation(psb[:, 0:nkt * 512], s_sb[:, 0:nkt * 512],
                                       AF.Exp, bias=mcol[:, 1:2],
                                       accum_out=mcol[:, 2:3]))
            exp_m = p.ac.n
            assert exp_m == ac_attn0 + it_g + 1, (exp_m, it_g)

            @p.block.vector
            def _(v, exp_m=exp_m, mcol=mcol):
                v.wait_ge(p.ac.h, exp_m)
                p.ev.inc(v.reciprocal(mcol[:, 3:4], mcol[:, 2:3]))
            rcol_m = p.ev.n

            trp = []
            pvm = []
            pv_ev0 = p.ev.n
            NKB = nkt * 4

            @p.block.tensor
            def _(t, nkt=nkt, lo_t=lo_t, va_s=va_s, exp_m=exp_m,
                  pv_ev0=pv_ev0, it_g=it_g, NKB=NKB, psb=psb):
                t.wait_ge(p.ac.h, exp_m)
                if it_g > 0:
                    t.wait_ge(p.ev.h, prev_scale[0])
                for kb in range(NKB):
                    if kb >= 2:
                        t.wait_ge(p.ev.h, pv_ev0 + kb - 1)
                    trp.append(p.mm.inc(t.transpose(
                        pb[kb % 2][:, :], psb[:, kb * 128:(kb + 1) * 128],
                        ident_b[:])))
                    if kb >= 2:
                        kbv = kb - 2
                        pvm.append(p.mm.inc(t.matmul(
                            ps[2][:, 0:128],
                            pT_sb[:, (kbv % 2) * 128:(kbv % 2 + 1) * 128],
                            va_s[:, (lo_t * 4 + kbv) * 128:(lo_t * 4 + kbv + 1) * 128],
                            start=(kbv == 0), stop=False)))
                for kbv in (NKB - 2, NKB - 1):
                    t.wait_ge(p.ev.h, pv_ev0 + kbv + 1)
                    pvm.append(p.mm.inc(t.matmul(
                        ps[2][:, 0:128],
                        pT_sb[:, (kbv % 2) * 128:(kbv % 2 + 1) * 128],
                        va_s[:, (lo_t * 4 + kbv) * 128:(lo_t * 4 + kbv + 1) * 128],
                        start=(kbv == 0), stop=(kbv == NKB - 1))))

            @p.block.vector
            def _(v, trp=trp, pvm=pvm, it_g=it_g, NKB=NKB):
                for kb in range(NKB):
                    v.wait_ge(p.mm.h, trp[kb])
                    if kb >= 2:
                        v.wait_ge(p.mm.h, pvm[kb - 2])
                    elif it_g > 0:
                        v.wait_ge(p.mm.h, prev_pv_last[0])
                    p.ev.inc(v.tensor_copy(
                        pT_sb[:, (kb % 2) * 128:(kb % 2 + 1) * 128],
                        pb[kb % 2][:, :]))
            pv_mm = pvm[-1]

            @p.block.vector
            def _(v, qb=qb, pv_mm=pv_mm, rcol_m=rcol_m, it_g=it_g, mcol=mcol):
                v.wait_ge(p.mm.h, pv_mm)
                v.wait_ge(p.ev.h, rcol_m)
                if it_g >= NQB:
                    ast.wait_idx(v, (it_g // NQB) - 1)
                p.ev.inc(v.tensor_scalar_mul(attn_col[:, qb * 128:(qb + 1) * 128],
                                             ps[2][:, 0:128], mcol[:, 3:4]))
            prev_scale[0] = p.ev.n
            trp_hist.append(trp[-1])
            trp_hist.pop(0)
            prev_pv_last[0] = pvm[-1]
            it_g += 1

        h_done_mm[h] = p.mm.n
        col_done = p.ev.n

        @p.block.sync
        def _(s_, h=h, col_done=col_done):
            s_.wait_ge(p.ev.h, col_done)
            ast.issue(s_.dma_start(
                rearr_rows(attn_out.ap()[:, h * 128:(h + 1) * 128]),
                attn_col[:].rearrange("p (a c) -> p a c", a=NQB)))
        if h + 2 < 16:
            kh_ld[h + 2] = emit_head_loads(h + 2)

    @p.block.gpsimd
    def _(g_):
        ost.wait_all(g_)
        ast.wait_all(g_)
    return p


def emit_down_resp(p, src_dram, KC, TOK, dT, ps, tag, src_ready, w_ready,
                   hslab, wcol, scr, x_dram, xout_dram, src_is_tm=False):
    nt = TOK // 512
    no2 = H // 128
    mm0 = p.mm.n
    ev0 = p.ev.n
    hstr = p.stream(f"dr_h{tag}", 1)
    wstr = p.stream(f"dr_w{tag}", 2)
    xld = p.stream(f"dr_x{tag}", 2)
    ost = p.stream(f"dr_o{tag}", 2)
    hld_use = []
    wld_use = []

    @p.block.sync
    def _(s):
        src_ready[0].wait_all(s, src_ready[1])
        if w_ready is not None:
            w_ready[0].wait_all(s, w_ready[1])
        it = 0
        for t in range(nt):
            if t >= 1:
                s.wait_ge(p.mm.h, mm0 + t * no2)
            if src_is_tm:
                u = None
                for kc in range(KC):
                    u = hstr.issue_slot(0, s.dma_start(
                        hslab[:, kc * 512:(kc + 1) * 512],
                        src_dram.ap()[t * 512:(t + 1) * 512,
                                      kc * 128:(kc + 1) * 128],
                        transpose=True))
                hld_use.append(u)
            else:
                hld_use.append(slab_load_split(
                    hstr, 0, s, hslab,
                    src_dram.ap()[:, t * 512:(t + 1) * 512], KC, 512))
            for o2 in range(no2):
                if it >= 2:
                    s.wait_ge(p.mm.h, mm0 + it - 1)
                wld_use.append(slab_load_split(
                    wstr, it % 2, s, wcol[it % 2],
                    dT.ap()[:, o2 * 128:(o2 + 1) * 128], KC, 128))
                if it >= 2:
                    it_s = it - 2
                    t_s, o2_s = divmod(it_s, no2)
                    s.wait_ge(p.ev.h, ev0 + it_s + 1)
                    ost.issue(s.dma_start(
                        xout_dram.ap()[o2_s * 128:(o2_s + 1) * 128,
                                       t_s * 512:(t_s + 1) * 512],
                        scr[:, 1024:2048][:, (it_s % 2) * 512:
                                          (it_s % 2 + 1) * 512]))
                it += 1
        for it_s in (nt * no2 - 2, nt * no2 - 1):
            t_s, o2_s = divmod(it_s, no2)
            s.wait_ge(p.ev.h, ev0 + it_s + 1)
            ost.issue(s.dma_start(
                xout_dram.ap()[o2_s * 128:(o2_s + 1) * 128,
                               t_s * 512:(t_s + 1) * 512],
                scr[:, 1024:2048][:, (it_s % 2) * 512:(it_s % 2 + 1) * 512]))

    @p.block.gpsimd
    def _(g):
        it = 0
        for t in range(nt):
            for o2 in range(no2):
                if it >= 2:
                    g.wait_ge(p.ev.h, ev0 + it - 1)
                xld.issue(g.dma_start(
                    scr[:, 0:1024][:, (it % 2) * 512:(it % 2 + 1) * 512],
                    x_dram.ap()[o2 * 128:(o2 + 1) * 128,
                                t * 512:(t + 1) * 512]))
                it += 1

    @p.block.tensor
    def _(t_):
        it = 0
        for t in range(nt):
            hstr.wait_all(t_, hld_use[t])
            for o2 in range(no2):
                wstr.wait(t_, wld_use[it])
                if it >= 2:
                    t_.wait_ge(p.ev.h, ev0 + it - 1)
                elif ev0 > 0:
                    t_.wait_ge(p.ev.h, ev0)
                bank = ps[4 + (it % 2)]
                last = None
                for kc in range(KC):
                    last = t_.matmul(bank[:, :],
                                     wcol[it % 2][:, kc * 128:(kc + 1) * 128],
                                     hslab[:, kc * 512:(kc + 1) * 512],
                                     start=(kc == 0), stop=(kc == KC - 1))
                p.mm.inc(last)
                it += 1

    @p.block.vector
    def _(v):
        it = 0
        for t in range(nt):
            for o2 in range(no2):
                v.wait_ge(p.mm.h, mm0 + it + 1)
                xld.wait_idx(v, it)
                if it >= 2:
                    ost.wait_idx(v, it - 2)
                bank = ps[4 + (it % 2)]
                sl = slice((it % 2) * 512, (it % 2 + 1) * 512)
                p.ev.inc(v.tensor_add(scr[:, 1024:2048][:, sl], bank[:, :],
                                      scr[:, 0:1024][:, sl]))
                it += 1
    return ost, ost.k - 1


def emit_down_rms(p, src_dram, KC, TOK, dT, ps, tag, src_ready, w_ready,
                  hslab, wcol, scr, y_sb, x_dram, xout_dram, ones_bf, rowbuf,
                  x_pre=None):
    nt = TOK // 512
    no2 = 16
    mm0 = p.mm.n
    evq0 = p.evq.n
    hstr = p.stream(f"dm_h{tag}", 1)
    wstr = p.stream(f"dm_w{tag}", 2)
    xld = p.stream(f"dm_x{tag}", 4)
    ost = p.stream(f"dm_o{tag}", 2)
    hld_use = []
    wld_use = []
    ac_sq = {}
    ev_apply = {}
    rb_prev = [0]

    def mm_down(t, o2):
        return mm0 + 18 * t + o2 + 1

    def mm_statd(t):
        return mm0 + 18 * t + 17

    def mm_bc(t):
        return mm0 + 18 * t + 18

    @p.block.sync
    def _(s):
        src_ready[0].wait_all(s, src_ready[1])
        if w_ready is not None:
            w_ready[0].wait_all(s, w_ready[1])
        it = 0
        for t in range(nt):
            if t >= 1:
                s.wait_ge(p.mm.h, mm_statd(t - 1))
            hld_use.append(slab_load_split(
                hstr, 0, s, hslab, src_dram.ap()[:, t * 512:(t + 1) * 512],
                KC, 512))
            for o2 in range(no2):
                if it >= 2:
                    tp, op = divmod(it - 2, no2)
                    s.wait_ge(p.mm.h, mm_down(tp, op))
                wld_use.append(slab_load_split(
                    wstr, it % 2, s, wcol[it % 2],
                    dT.ap()[:, o2 * 128:(o2 + 1) * 128], KC, 128))
                it += 1

    @p.block.gpsimd
    def _(g):
        if x_pre is not None:
            x_pre[0].wait_all(g, x_pre[1])
        for o2 in range(4):
            xld.issue(g.dma_start(
                scr[:, 0:2048][:, o2 * 512:(o2 + 1) * 512],
                x_dram.ap()[o2 * 128:(o2 + 1) * 128, 0:512]))

    for t in range(nt):
        @p.block.tensor
        def _(t_, t=t):
            hstr.wait_all(t_, hld_use[t])
            for o2 in range(no2):
                it = t * no2 + o2
                wstr.wait(t_, wld_use[it])
                if it >= 2:
                    t_.wait_ge(p.evq.h, evq0 + it - 1)
                bank = ps[4 + (it % 2)]
                last = None
                for kc in range(KC):
                    last = t_.matmul(bank[:, :],
                                     wcol[it % 2][:, kc * 128:(kc + 1) * 128],
                                     hslab[:, kc * 512:(kc + 1) * 512],
                                     start=(kc == 0), stop=(kc == KC - 1))
                p.mm.inc(last)

        @p.block.vector
        def _(v, t=t):
            for o2 in range(no2):
                it = t * no2 + o2
                v.wait_ge(p.mm.h, mm_down(t, o2))
                if t > 0 and o2 == 0:
                    v.wait_ge(p.ac.h, ac_sq[t - 1])
                p.evq.inc(v.tensor_copy(y_sb[:, o2 * 512:(o2 + 1) * 512],
                                        ps[4 + (it % 2)][:, :]))

        @p.block.scalar
        def _(sc, t=t):
            sc.wait_ge(p.evq.h, evq0 + (t + 1) * no2)
            if t > 0:
                sc.wait_ge(p.mm.h, mm_statd(t - 1))
            p.ac.inc(sc.activation(hslab[:, 0:8192], y_sb[:], AF.Square))
        ac_sq[t] = p.ac.n

        @p.block.tensor
        def _(t_, t=t):
            t_.wait_ge(p.ac.h, ac_sq[t])
            if t > 0:
                t_.wait_ge(p.ev.h, rb_prev[0])
            last = None
            for c in range(16):
                last = t_.matmul(ps[2][0:1, :], ones_bf[:, 0:1],
                                 hslab[:, c * 512:(c + 1) * 512],
                                 start=(c == 0), stop=(c == 15))
            p.mm.inc(last)

        @p.block.vector
        def _(v, t=t):
            v.wait_ge(p.mm.h, mm_statd(t))
            p.ev.inc(v.tensor_copy(rowbuf[0:1, 0:512], ps[2][0:1, :]))
        srow_m = p.ev.n

        @p.block.scalar
        def _(sc, srow_m=srow_m):
            sc.wait_ge(p.ev.h, srow_m)
            p.chain(sc, sc.activation(
                rowbuf[0:1, 512:1024], rowbuf[0:1, 0:512], AF.Ln,
                scale=1.0 / H,
                bias=p.cvals[0:1, CV_EPS_RMS:CV_EPS_RMS + 1]), "s")
            p.ac.inc(sc.activation(rowbuf[0:1, 0:512], rowbuf[0:1, 512:1024],
                                   AF.Exp, scale=-0.5))
        rstd_m = p.ac.n

        @p.block.tensor
        def _(t_, rstd_m=rstd_m):
            t_.wait_ge(p.ac.h, rstd_m)
            p.mm.inc(t_.matmul(ps[2][:, :], p.ones_f_g[0:1, :],
                               rowbuf[0:1, 0:512], start=True, stop=True))

        @p.block.vector
        def _(v, t=t):
            v.wait_ge(p.mm.h, mm_bc(t))
            p.ev.inc(v.tensor_copy(scr[:, 3072:3584], ps[2][:, :]))
            rb_prev.append(p.ev.n)
            rb_prev.pop(0) if len(rb_prev) > 1 else None
            v.wait_ge(p.ev.h, rb_prev[0])
            for o2 in range(no2):
                it = t * no2 + o2
                xld.wait_idx(v, it)
                if it >= 2:
                    ost.wait_idx(v, it - 2)
                osl = scr[:, 2048:3072][:, (it % 2) * 512:(it % 2 + 1) * 512]
                p.chain(v, v.tensor_mul(osl, y_sb[:, o2 * 512:(o2 + 1) * 512],
                                        scr[:, 3072:3584]))
                p.ev.inc(v.scalar_tensor_tensor(
                    osl, osl, 0.5,
                    scr[:, 0:2048][:, (it % 4) * 512:(it % 4 + 1) * 512],
                    ALU.mult, ALU.add))
                ev_apply[(t, o2)] = p.ev.n

        @p.block.gpsimd
        def _(g, t=t):
            for o2 in range(no2):
                it = t * no2 + o2
                g.wait_ge(p.ev.h, ev_apply[(t, o2)])
                ost.issue(g.dma_start(
                    xout_dram.ap()[o2 * 128:(o2 + 1) * 128,
                                   t * 512:(t + 1) * 512],
                    scr[:, 2048:3072][:, (it % 2) * 512:(it % 2 + 1) * 512]))
                itn = it + 4
                if itn < nt * no2:
                    tn, on = divmod(itn, no2)
                    xld.issue(g.dma_start(
                        scr[:, 0:2048][:, (itn % 4) * 512:(itn % 4 + 1) * 512],
                        x_dram.ap()[on * 128:(on + 1) * 128,
                                    tn * 512:(tn + 1) * 512]))
    return ost, ost.k - 1


def emit_add_tm(p, a_dram, b_dram, out_dram, TOK, arena, tag):
    nr = TOK // 128
    lda = p.stream(f"ad_a{tag}", 2)
    ldb = p.stream(f"ad_b{tag}", 2)
    ost = p.stream(f"ad_o{tag}", 2)
    abuf = arena[:, 0:2 * H]
    bbuf = arena[:, 2 * H:4 * H]
    obuf = arena[:, 4 * H:6 * H]
    ev0 = p.ev.n

    @p.block.gpsimd
    def _(g):
        for r in range(nr):
            if r >= 2:
                g.wait_ge(p.ev.h, ev0 + r - 1)
            sl = slice((r % 2) * H, (r % 2 + 1) * H)
            lda.issue(g.dma_start(abuf[:, sl],
                                  a_dram.ap()[r * 128:(r + 1) * 128, :]))
            ldb.issue(g.dma_start(bbuf[:, sl],
                                  b_dram.ap()[r * 128:(r + 1) * 128, :]))

    @p.block.vector
    def _(v):
        for r in range(nr):
            lda.wait(v, r)
            ldb.wait(v, r)
            if r >= 2:
                ost.wait_idx(v, r - 2)
            sl = slice((r % 2) * H, (r % 2 + 1) * H)
            p.ev.inc(v.tensor_add(obuf[:, sl], abuf[:, sl], bbuf[:, sl]))

    @p.block.sync
    def _(s):
        for r in range(nr):
            s.wait_ge(p.ev.h, ev0 + r + 1)
            sl = slice((r % 2) * H, (r % 2 + 1) * H)
            ost.issue(s.dma_start(out_dram.ap()[r * 128:(r + 1) * 128, :],
                                  obuf[:, sl]))
    return ost, ost.k - 1


class DqCtx:
    def __init__(self, p, tag="dq", vf=None, ob=None):
        self.p = p
        self.pk = p.sbuf(f"{tag}_pk", [128, 2 * 704], I32)
        self.vf = vf if vf is not None else p.sbuf(f"{tag}_vf", [128, 2048], F32)
        self.ti = p.sbuf(f"{tag}_ti", [128, 256], I32)
        self.t1 = p.sbuf(f"{tag}_t1", [128, 256], F32)
        self.t2 = p.sbuf(f"{tag}_t2", [128, 256], F32)
        self.t3 = p.sbuf(f"{tag}_t3", [128, 256], F32)
        self.ob = ob if ob is not None else p.sbuf(f"{tag}_ob", [128, 2 * 2048], BF16)
        self.sch = p.sbuf(f"{tag}_sch", [128, 2 * 44], F16)
        self.scf = p.sbuf(f"{tag}_scf", [128, 2 * 44], F32)
        self.pld = p.stream(f"{tag}_pld", 2)
        self.sld = p.stream(f"{tag}_sld", 2)
        self.ost = p.stream(f"{tag}_ost", 2)
        self.tile_i = 0
        self.ck_i = 0
        self.tile_done = {}


def emit_dequant(c, packed_in, scl_in, R, IN, wbf_dram):
    p = c.p
    W = IN // 8
    NG = IN // 128
    CW = W
    while CW > 256:
        assert CW % 2 == 0
        CW //= 2
    nck = W // CW
    GPC = CW * 8 // 128
    ntile = R // 128
    ev_scale = {}

    for tt in range(ntile):
        ti = c.tile_i
        c.tile_i += 1
        ck0 = c.ck_i
        pslot = slice((ti % 2) * 704, (ti % 2) * 704 + W)
        sslot = slice((ti % 2) * 44, (ti % 2) * 44 + NG)
        uses = []

        @p.block.gpsimd
        def _(g, tt=tt, ti=ti):
            if ti >= 2:
                g.wait_ge(p.ev.h, c.tile_done[ti - 2])
            uses.append(c.pld.issue_slot(ti % 2, g.dma_start(
                c.pk[:, pslot], packed_in.ap()[tt * 128:(tt + 1) * 128, :])))
            uses.append(c.sld.issue_slot(ti % 2, g.dma_start(
                c.sch[:, sslot], scl_in.ap()[tt * 128:(tt + 1) * 128, :])))

        @p.block.vector
        def _(v, tt=tt, ti=ti, ck0=ck0):
            c.pld.wait(v, uses[0])
            c.sld.wait(v, uses[1])
            p.chain(v, v.tensor_copy(c.scf[:, sslot], c.sch[:, sslot]))
            for ck in range(nck):
                cki = ck0 + ck
                pk_sl = c.pk[:, (ti % 2) * 704 + ck * CW:
                             (ti % 2) * 704 + (ck + 1) * CW]
                vf = c.vf[:, 0:CW * 8]
                tiv = c.ti[:, 0:CW]
                a = c.t1[:, 0:CW]
                m = c.t2[:, 0:CW]
                b = c.t3[:, 0:CW]
                for j in range(8):
                    out_sl = vf[:, j:CW * 8:8]
                    p.chain(v, v.tensor_scalar(tiv, pk_sl, 4 * j, 15,
                                               ALU.logical_shift_right,
                                               ALU.bitwise_and))
                    p.chain(v, v.tensor_copy(a, tiv))
                    p.chain(v, v.tensor_scalar(m, a, 8.0, None, ALU.is_ge))
                    p.chain(v, v.scalar_tensor_tensor(b, m, -8.0, a,
                                                      ALU.mult, ALU.add))
                    p.chain(v, v.tensor_scalar(a, b, -4.0, 0.0,
                                               ALU.add, ALU.max))
                    p.chain(v, v.tensor_tensor(a, a, b, ALU.add))
                    p.chain(v, v.tensor_scalar(b, b, 7.0, None, ALU.is_ge))
                    v.scalar_tensor_tensor(a, b, 2.0, a, ALU.mult, ALU.add)
                    p.chain(v, v.tensor_scalar(m, m, -1.0, 0.5,
                                               ALU.mult, ALU.add))
                    p.chain(v, v.tensor_tensor(out_sl, a, m, ALU.mult))
                if cki >= 2:
                    c.ost.wait_idx(v, cki - 2)
                last = None
                for g2 in range(GPC):
                    gabs = ck * GPC + g2
                    last = v.tensor_scalar(
                        c.ob[:, (cki % 2) * 2048 + g2 * 128:
                             (cki % 2) * 2048 + (g2 + 1) * 128],
                        vf[:, g2 * 128:(g2 + 1) * 128],
                        c.scf[:, (ti % 2) * 44 + gabs:
                              (ti % 2) * 44 + gabs + 1],
                        None, ALU.mult)
                p.ev.inc(last)
                ev_scale[cki] = p.ev.n
                c.ck_i += 1
        c.tile_done[ti] = ev_scale[ck0 + nck - 1]

        @p.block.sync
        def _(s, tt=tt, ck0=ck0):
            for ck in range(nck):
                cki = ck0 + ck
                s.wait_ge(p.ev.h, ev_scale[cki])
                c.ost.issue(s.dma_start(
                    wbf_dram.ap()[tt * 128:(tt + 1) * 128,
                                  ck * CW * 8:(ck + 1) * CW * 8],
                    c.ob[:, (cki % 2) * 2048:(cki % 2) * 2048 + CW * 8]))
    return c.ost, c.ost.k - 1


def build_launch4(TOK=2048, RGU=1408, RDN=256):
    p = Prog()
    ps = p.psum_pool()

    x1_in = p.dram("x1_in", [H, TOK], F32, kind="ExternalInput")
    attn_in = p.dram("attn_in", [TOK, H], BF16, kind="ExternalInput")
    ottt_in = p.dram("ottt_in", [TOK, H], BF16, kind="ExternalInput")
    oproj_in = p.dram("w_oproj", [H, H], F32, kind="ExternalInput")
    ident_in = p.dram("ident_in", [128, 128], F32, kind="ExternalInput")
    w_names = ["r_gate", "r_up", "r_down", "sh_gate", "sh_up", "sh_down"]
    w_shapes = {"sh_gate": (8192, H), "sh_up": (8192, H), "sh_down": (H, 8192),
                "r_gate": (1024, H), "r_up": (1024, H), "r_down": (H, 1024)}
    w_in = {n: p.dram(f"w_{n}", list(w_shapes[n]), F32, kind="ExternalInput")
            for n in w_names}
    router_in = p.dram("w_router", [8, H], F32, kind="ExternalInput")
    gu_pk_in = p.dram("gu_pk", [RGU, H // 8], I32, kind="ExternalInput")
    gu_sc_in = p.dram("gu_sc", [RGU, H // 128], F16, kind="ExternalInput")
    dn_pk_in = p.dram("dn_pk", [RDN, 5632 // 8], I32, kind="ExternalInput")
    dn_sc_in = p.dram("dn_sc", [RDN, 5632 // 128], F16, kind="ExternalInput")

    x3_out = p.dram("x3_out", [H, TOK], F32, kind="ExternalOutput")
    wguT_out = p.dram("wguT_out", [H, RGU], BF16, kind="ExternalOutput")
    wdnT_out = p.dram("wdnT_out", [5632, RDN], BF16, kind="ExternalOutput")

    wbf = {n: p.dram(f"wbf_{n}", list(w_shapes[n]), BF16) for n in w_names}
    wbf["oproj"] = p.dram("wbf_oproj", [H, H], BF16)
    wT = {n: p.dram(f"wT_{n}", [w_shapes[n][1], w_shapes[n][0]], BF16)
          for n in w_names}
    wT["oproj"] = p.dram("wT_oproj", [H, H], BF16)
    comb_d = p.dram("comb_d", [TOK, H], BF16)
    x2_d = p.dram("x2_d", [H, TOK], F32)
    h_dram = p.dram("h_d", [8192, TOK], BF16)
    hr_dram = p.dram("hr_d", [1024, TOK], BF16)
    yr_dram = p.dram("yr_d", [H, TOK], F32)
    wbf_gu_d = p.dram("wbf_gu_d", [RGU, H], BF16)
    wbf_dn_d = p.dram("wbf_dn_d", [RDN, 5632], BF16)

    ones_bf, ones_f, _ = emit_consts(p)
    ident_f = p.sbuf("ident_f", [128, 128], F32)
    ild = p.stream("identld", 1)

    @p.block.gpsimd
    def _(g):
        ild.issue(g.dma_start(ident_f[:], ident_in.ap()))

    A = p.sbuf("A", [128, max(NCH * TOK, 64 * 512)], BF16)

    add_done = emit_add_tm(p, attn_in, ottt_in, comb_d, TOK, A, "cmb")

    stage = [p.sbuf(f"stage{j}", [128, 8192], BF16) for j in range(2)]
    bounce = p.sbuf("wpbounce", [128, 2 * 2048], BF16)
    bslots = [bounce[:, 0:2048], bounce[:, 2048:4096]]
    prep_names = ["oproj"] + w_names
    shp = dict(w_shapes, oproj=(H, H))
    w_all_in = dict(w_in, oproj=oproj_in)
    specs = [(w_all_in[n].ap(), wbf[n], wT[n], *shp[n]) for n in prep_names]
    cast_m, tr_m, wbs = emit_weight_prep(p, specs, "l4", bslots, piece=2048)
    prep_oproj = (wbs, tr_m[0])
    prep_all = (wbs, wbs.k - 1)
    prep_thru = {n: (wbs, tr_m[j]) for j, n in enumerate(prep_names)}

    owcol = [p.sbuf(f"owcol{j}", [128, NCH * 128], BF16) for j in range(2)]
    op_scr = (p.named_buf("lnx", [128, 2 * min(2048, TOK)], F32)
              if 2 * min(2048, TOK) >= 2048
              else p.sbuf("op_scr", [128, 2048], F32))
    op_ost, op_u = emit_down_resp(p, comb_d, NCH, TOK, wT["oproj"], ps, "opj",
                                  add_done, prep_oproj,
                                  A[:, 6 * H:6 * H + 64 * 128],
                                  owcol, op_scr, x1_in, x2_d, src_is_tm=True)

    xln_mark = emit_ln_fm(p, x2_d, A, TOK, ps, ones_bf, ones_f, "l4",
                          entry_stream=(op_ost, op_u))

    comb_b = p.lnmu
    comb_mark = emit_router_comb(p, A, TOK, router_in.ap(), ps, ones_f, comb_b,
                                 "l4", xln_mark, ident_f, ident_ready=(ild, 0))

    tmp = p.sbuf("gu_tmp", [128, 2 * 512], BF16)
    htile = p.sbuf("gu_ht", [128, 2 * 512], BF16)
    hr_ready = emit_ffn_gu(p, A, TOK, wT["r_gate"], wT["r_up"], hr_dram, 1024,
                           ps, "rgu", prep_thru["r_up"], xln_mark, stage, tmp,
                           htile)

    hrslab = (p.lnsq if 2 * min(2048, TOK) >= 8 * 512
              else p.sbuf("hrslab", [128, 8 * 512], BF16))
    rwcol = [p.sbuf(f"rwcol{j}", [128, 8 * 128], BF16) for j in range(2)]
    scr = (p.lnx if 2 * min(2048, TOK) >= 3072
           else p.sbuf("scr", [128, 3072], F32))
    yr_ready = emit_ffn_down(p, hr_dram, 1024, TOK, wT["r_down"], ps, "rdn",
                             hr_ready, prep_thru["r_down"], hrslab, rwcol,
                             "yr", scr, comb_b=comb_b, yr_dram=yr_dram,
                             ev_pre_wait=comb_mark)

    h_ready = emit_ffn_gu(p, A, TOK, wT["sh_gate"], wT["sh_up"], h_dram, 8192,
                          ps, "sgu", prep_thru["sh_up"], xln_mark, stage, tmp,
                          htile)

    ost, ost_u = emit_ffn_down(p, h_dram, 8192, TOK, wT["sh_down"], ps, "sdn",
                               h_ready, prep_thru["sh_down"], A, stage, "res",
                               scr,
                               yr_dram=yr_dram, x_dram=x2_d, xout_dram=x3_out,
                               ev_pre_wait=comb_mark, st_pre=yr_ready,
                               x_pre=(op_ost, op_u))

    dq = DqCtx(p, vf=p.lnrows[:, 0:2048], ob=A[:, 0:4096])
    dq_ost1 = emit_dequant(dq, gu_pk_in, gu_sc_in, RGU, H, wbf_gu_d)
    dq_ost2 = emit_dequant(dq, dn_pk_in, dn_sc_in, RDN, 5632, wbf_dn_d)
    dq_specs = [(None, wbf_gu_d, wguT_out, RGU, H),
                (None, wbf_dn_d, wdnT_out, RDN, 5632)]
    _, dq_tr, dq_wbs = emit_weight_prep(
        p, dq_specs, "dq", bslots, piece=2048,
        ready_map={0: [dq_ost1, prep_all], 1: [dq_ost2]})

    @p.block.gpsimd
    def _(g):
        ost.wait_all(g)
        dq_wbs.wait_all(g)
    return p


def emit_proj_fm(p, xln_sb, TOK, wT, ps, tag, w_ready, x_mark, wcol,
                 vmin, ebuf, obuf, out_dram, fmap=True, pre_ev=0):
    nt = TOK // 512
    no2 = NCH
    n_it = no2 * nt
    mm0 = p.mm.n
    ev0 = p.ev.n
    ac0 = p.ac.n
    EPI = 2 if fmap else 1
    wstr = p.stream(f"pf_w{tag}", 2)
    ost = p.stream(f"pf_o{tag}", 2)
    wld_use = []

    def mmk(it):
        return mm0 + it + 1

    def ev_min(it):
        return ev0 + EPI * it + 1

    def ev_out(it):
        return ev0 + EPI * it + EPI

    @p.block.sync
    def _(s):
        if w_ready is not None:
            w_ready[0].wait_all(s, w_ready[1])
        for o2 in range(no2):
            if o2 >= 2:
                s.wait_ge(p.mm.h, mm0 + (o2 - 1) * nt)
            wld_use.append(slab_load_split(
                wstr, o2 % 2, s, wcol[o2 % 2],
                wT.ap()[:, o2 * 128:(o2 + 1) * 128], NCH, 128))

    @p.block.tensor
    def _(t_):
        t_.wait_ge(p.ev.h, x_mark)
        if pre_ev:
            t_.wait_ge(p.ev.h, pre_ev)
        it = 0
        for o2 in range(no2):
            wstr.wait(t_, wld_use[o2])
            for t in range(nt):
                if it >= 2:
                    t_.wait_ge(p.ev.h, ev_out(it - 2))
                bank = ps[4 + (it % 2)]
                last = None
                for kc in range(NCH):
                    last = t_.matmul(
                        bank[:, :], wcol[o2 % 2][:, kc * 128:(kc + 1) * 128],
                        xln_sb[:, kc * TOK + t * 512:kc * TOK + (t + 1) * 512],
                        start=(kc == 0), stop=(kc == NCH - 1))
                p.mm.inc(last)
                it += 1

    if fmap:
        @p.block.scalar
        def _(sc):
            for it in range(n_it):
                sc.wait_ge(p.ev.h, ev_min(it))
                if it >= 2:
                    sc.wait_ge(p.ev.h, ev_out(it - 2))
                sl = slice((it % 2) * 512, (it % 2 + 1) * 512)
                p.ac.inc(sc.activation(ebuf[:, sl], vmin[:, sl], AF.Exp))

    @p.block.vector
    def _(v):
        for it in range(n_it):
            sl = slice((it % 2) * 512, (it % 2 + 1) * 512)
            bank = ps[4 + (it % 2)]
            v.wait_ge(p.mm.h, mmk(it))
            if fmap:
                if it >= 2:
                    v.wait_ge(p.ac.h, ac0 + it - 1)
                p.ev.inc(v.tensor_scalar(vmin[:, sl], bank[:, :], 0.0, None,
                                         ALU.min))
                v.wait_ge(p.ac.h, ac0 + it + 1)
                if it >= 2:
                    ost.wait_idx(v, it - 2)
                p.ev.inc(v.scalar_tensor_tensor(obuf[:, sl], bank[:, :], 0.0,
                                                ebuf[:, sl], ALU.max, ALU.add))
            else:
                if it >= 2:
                    ost.wait_idx(v, it - 2)
                p.ev.inc(v.tensor_copy(obuf[:, sl], bank[:, :]))

    @p.block.gpsimd
    def _(g):
        it = 0
        for o2 in range(no2):
            for t in range(nt):
                g.wait_ge(p.ev.h, ev_out(it))
                ost.issue(g.dma_start(
                    out_dram.ap()[o2 * 128:(o2 + 1) * 128,
                                  t * 512:(t + 1) * 512],
                    obuf[:, (it % 2) * 512:(it % 2 + 1) * 512]))
                it += 1
    return ost, ost.k - 1


def emit_proj_tm(p, xln_sb, TOK, wT, ps, tag, w_ready, x_mark, wsl,
                 vmin, ebuf, obuf, out_dram, fmap=True, pre_mm=0, pre_ev=0):
    ntb = TOK // 128
    nog = H // 512
    n_it = nog * ntb
    mm0 = p.mm.n
    ev0 = p.ev.n
    ac0 = p.ac.n
    EPI = 2 if fmap else 1
    wstr = p.stream(f"pt_w{tag}", 2)
    ost = p.stream(f"pt_o{tag}", 2)
    wld_use = []

    def ev_min(it):
        return ev0 + EPI * it + 1

    def ev_out(it):
        return ev0 + EPI * it + EPI

    @p.block.sync
    def _(s):
        if w_ready is not None:
            w_ready[0].wait_all(s, w_ready[1])
        if pre_mm:
            s.wait_ge(p.mm.h, pre_mm)
        for og in range(nog):
            if og >= 2:
                s.wait_ge(p.mm.h, mm0 + (og - 1) * ntb)
            wld_use.append(slab_load_split(
                wstr, og % 2, s, wsl[og % 2],
                wT.ap()[:, og * 512:(og + 1) * 512], NCH, 512))

    @p.block.tensor
    def _(t_):
        t_.wait_ge(p.ev.h, x_mark)
        if pre_ev:
            t_.wait_ge(p.ev.h, pre_ev)
        it = 0
        for og in range(nog):
            wstr.wait(t_, wld_use[og])
            for tb in range(ntb):
                if it >= 2:
                    t_.wait_ge(p.ev.h, ev_out(it - 2))
                bank = ps[4 + (it % 2)]
                last = None
                for kc in range(NCH):
                    last = t_.matmul(
                        bank[:, :],
                        xln_sb[:, kc * TOK + tb * 128:kc * TOK + (tb + 1) * 128],
                        wsl[og % 2][:, kc * 512:(kc + 1) * 512],
                        start=(kc == 0), stop=(kc == NCH - 1))
                p.mm.inc(last)
                it += 1

    if fmap:
        @p.block.scalar
        def _(sc):
            for it in range(n_it):
                sc.wait_ge(p.ev.h, ev_min(it))
                if it >= 2:
                    sc.wait_ge(p.ev.h, ev_out(it - 2))
                sl = slice((it % 2) * 512, (it % 2 + 1) * 512)
                p.ac.inc(sc.activation(ebuf[:, sl], vmin[:, sl], AF.Exp))

    @p.block.vector
    def _(v):
        for it in range(n_it):
            sl = slice((it % 2) * 512, (it % 2 + 1) * 512)
            bank = ps[4 + (it % 2)]
            v.wait_ge(p.mm.h, mm0 + it + 1)
            if fmap:
                if it >= 2:
                    v.wait_ge(p.ac.h, ac0 + it - 1)
                p.ev.inc(v.tensor_scalar(vmin[:, sl], bank[:, :], 0.0, None,
                                         ALU.min))
                v.wait_ge(p.ac.h, ac0 + it + 1)
                if it >= 2:
                    ost.wait_idx(v, it - 2)
                p.ev.inc(v.scalar_tensor_tensor(obuf[:, sl], bank[:, :], 0.0,
                                                ebuf[:, sl], ALU.max, ALU.add))
            else:
                if it >= 2:
                    ost.wait_idx(v, it - 2)
                p.ev.inc(v.tensor_copy(obuf[:, sl], bank[:, :]))

    @p.block.gpsimd
    def _(g):
        it = 0
        for og in range(nog):
            for tb in range(ntb):
                g.wait_ge(p.ev.h, ev_out(it))
                ost.issue(g.dma_start(
                    out_dram.ap()[tb * 128:(tb + 1) * 128,
                                  og * 512:(og + 1) * 512],
                    obuf[:, (it % 2) * 512:(it % 2 + 1) * 512]))
                it += 1
    return ost, ost.k - 1


def emit_kvks(p, k_d, v_d, TOK, ps, tag, kv_ready, ones_bf, kvks_sb,
              kcol, vcol, out_dram):
    ntb = TOK // 128
    NHD = 16
    mm0 = p.mm.n
    ev0 = p.ev.n
    kst = p.stream(f"kv_k{tag}", 2)
    vst = p.stream(f"kv_v{tag}", 2)
    ost = p.stream(f"kv_o{tag}", 1)
    kuse, vuse = [], []

    @p.block.gpsimd
    def _(g):
        for pr in kv_ready:
            pr[0].wait_all(g, pr[1])
        for h in range(NHD):
            if h >= 2:
                g.wait_ge(p.ev.h, ev0 + 2 * (h - 2) + 2)
            kuse.append(slab_load_split(kst, h % 2, g, kcol[h % 2],
                                        k_d.ap()[:, h * 128:(h + 1) * 128],
                                        ntb, 128))
            vuse.append(slab_load_split(vst, h % 2, g, vcol[h % 2],
                                        v_d.ap()[:, h * 128:(h + 1) * 128],
                                        ntb, 128))

    @p.block.tensor
    def _(t_):
        for h in range(NHD):
            kst.wait(t_, kuse[h])
            vst.wait(t_, vuse[h])
            if h >= 2:
                t_.wait_ge(p.ev.h, ev0 + 2 * (h - 2) + 2)
            bkv = ps[0 + (h % 2)]
            bks = ps[2 + (h % 2)]
            last = None
            for tb in range(ntb):
                last = t_.matmul(bkv[:, 0:128],
                                 kcol[h % 2][:, tb * 128:(tb + 1) * 128],
                                 vcol[h % 2][:, tb * 128:(tb + 1) * 128],
                                 start=(tb == 0), stop=(tb == ntb - 1))
            p.mm.inc(last)
            for tb in range(ntb):
                last = t_.matmul(bks[:, 0:1],
                                 kcol[h % 2][:, tb * 128:(tb + 1) * 128],
                                 ones_bf[:, 0:1],
                                 start=(tb == 0), stop=(tb == ntb - 1))
            p.mm.inc(last)

    @p.block.vector
    def _(v):
        for h in range(NHD):
            v.wait_ge(p.mm.h, mm0 + 2 * h + 2)
            bkv = ps[0 + (h % 2)]
            bks = ps[2 + (h % 2)]
            p.ev.inc(v.tensor_copy(kvks_sb[:, h * 129:h * 129 + 128],
                                   bkv[:, 0:128]))
            p.ev.inc(v.tensor_copy(kvks_sb[:, h * 129 + 128:h * 129 + 129],
                                   bks[:, 0:1]))

    @p.block.sync
    def _(s):
        s.wait_ge(p.ev.h, ev0 + 2 * NHD)
        ost.issue(s.dma_start(out_dram.ap(), kvks_sb[:, 0:NHD * 129]))
    return ost, ost.k - 1


def emit_linattn(p, q_in, kvp_in, TOK, ps, tag, kv_f, ptmp, kv_b, qrow, zb,
                 obuf, o_d):
    NHD = 16
    ntb = TOK // 128
    n_it = NHD * ntb
    mm0 = p.mm.n
    pld = p.stream(f"la_p{tag}", 2)
    qld = p.stream(f"la_q{tag}", 2)
    ost = p.stream(f"la_o{tag}", 2)

    ev_add0 = p.ev.n

    @p.block.gpsimd
    def _(g):
        pld.issue_slot(0, g.dma_start(kv_f[:, 0:2064],
                                      kvp_in.ap()[0:128, :]))
        for j in range(1, 4):
            if j >= 2:
                g.wait_ge(p.ev.h, ev_add0 + j - 1)
            pld.issue_slot(j % 2, g.dma_start(
                ptmp[:, 0:2064], kvp_in.ap()[j * 128:(j + 1) * 128, :]))

    @p.block.vector
    def _(v):
        pld.wait(v, 0)
        for j in range(1, 4):
            pld.wait(v, j)
            p.ev.inc(v.tensor_add(kv_f[:, 0:2064], kv_f[:, 0:2064],
                                  ptmp[:, 0:2064]))
            v.wait_ge(p.ev.h, ev_add0 + j)
        p.ev.inc(v.tensor_copy(kv_b[:, 0:2064], kv_f[:, 0:2064]))
    kvb_m = p.ev.n
    ev0 = p.ev.n

    def ev_out(it):
        return ev0 + it + 1

    @p.block.sync
    def _(s):
        for h in range(NHD):
            if h >= 2:
                s.wait_ge(p.mm.h, mm0 + (h - 1) * ntb)
            qld.issue(s.dma_start(qrow[:, (h % 2) * TOK:(h % 2 + 1) * TOK],
                                  q_in.ap()[h * 128:(h + 1) * 128, :]))

    @p.block.tensor
    def _(t_):
        t_.wait_ge(p.ev.h, kvb_m)
        it = 0
        for h in range(NHD):
            qld.wait(t_, h)
            for tb in range(ntb):
                if it >= 2:
                    t_.wait_ge(p.ev.h, ev_out(it - 2))
                bank = ps[4 + (it % 2)]
                p.mm.inc(t_.matmul(
                    bank[:, 0:129],
                    qrow[:, (h % 2) * TOK + tb * 128:
                         (h % 2) * TOK + (tb + 1) * 128],
                    kv_b[:, h * 129:(h + 1) * 129],
                    start=True, stop=True))
                it += 1

    @p.block.vector
    def _(v):
        it = 0
        for h in range(NHD):
            for tb in range(ntb):
                v.wait_ge(p.mm.h, mm0 + it + 1)
                if it >= 2:
                    ost.wait_idx(v, it - 2)
                bank = ps[4 + (it % 2)]
                zs = slice((it % 2) * 2, (it % 2) * 2 + 1)
                zs2 = slice((it % 2) * 2 + 1, (it % 2) * 2 + 2)
                p.chain(v, v.tensor_scalar(zb[:, zs], bank[:, 128:129],
                                           1e-6, None, ALU.add))
                p.chain(v, v.reciprocal(zb[:, zs2], zb[:, zs]))
                p.ev.inc(v.tensor_scalar(
                    obuf[:, (it % 2) * 128:(it % 2 + 1) * 128],
                    bank[:, 0:128], zb[:, zs2], None, ALU.mult))
                it += 1

    @p.block.gpsimd
    def _(g):
        it = 0
        for h in range(NHD):
            for tb in range(ntb):
                g.wait_ge(p.ev.h, ev_out(it))
                ost.issue(g.dma_start(
                    o_d.ap()[tb * 128:(tb + 1) * 128, h * 128:(h + 1) * 128],
                    obuf[:, (it % 2) * 128:(it % 2 + 1) * 128]))
                it += 1
    return ost, ost.k - 1


def build_launch5(TOK=2048):
    assert TOK == 2048
    p = Prog()
    ps = p.psum_pool()
    p.evq = Sem(p, "evq")

    x3_in = p.dram("x3_in", [H, TOK], F32, kind="ExternalInput")
    wguT_in = p.dram("wguT_in", [H, 11264], BF16, kind="ExternalInput")
    wdnT_in = p.dram("wdnT_in", [5632, H], BF16, kind="ExternalInput")
    wq_in = p.dram("w_q", [H, H], F32, kind="ExternalInput")
    wk_in = p.dram("w_k", [H, H], F32, kind="ExternalInput")
    wv_in = p.dram("w_v", [H, H], F32, kind="ExternalInput")

    x4_out = p.dram("x4_out", [H, TOK], F32, kind="ExternalOutput")
    q_out = p.dram("q_out", [H, TOK], BF16, kind="ExternalOutput")
    kvks_out = p.dram("kvks_out", [128, 16 * 129], F32, kind="ExternalOutput")

    wbf = {n: p.dram(f"wbf_{n}", [H, H], BF16) for n in ["q", "k", "v"]}
    wT = {n: p.dram(f"wT_{n}", [H, H], BF16) for n in ["q", "k", "v"]}
    h1_d = p.dram("h1_d", [5632, TOK], BF16)
    k_d = p.dram("k_d", [TOK, H], BF16)
    v_d = p.dram("v_d", [TOK, H], BF16)

    ones_bf, ones_f, _ = emit_consts(p)
    p.ones_f_g = ones_f

    A = p.sbuf("A", [128, NCH * TOK], BF16)
    stage = [p.sbuf(f"stage{j}", [128, 8192], BF16) for j in range(2)]
    qstage = [p.sbuf(f"qstage{j}", [128, H], BF16) for j in range(2)]

    specs = [(w.ap(), wbf[n], wT[n], H, H)
             for n, w in [("q", wq_in), ("k", wk_in), ("v", wv_in)]]
    cast_m, tr_m, wbs = emit_weight_prep(p, specs, "l5", qstage)
    prep_all = (wbs, wbs.k - 1)

    xln_mark = emit_ln_fm(p, x3_in, A, TOK, ps, ones_bf, ones_f, "l5a")

    tmp = p.sbuf("gu_tmp", [128, 2 * 512], BF16)
    htile = p.sbuf("gu_ht", [128, 2 * 512], BF16)
    h_ready = emit_ffn_gu(p, A, TOK, wguT_in, wguT_in, h1_d, 5632, ps, "f1a",
                          None, xln_mark, stage, tmp, htile,
                          g_off=0, u_off=5632)

    rms_ost, rms_u = emit_down_rms(p, h1_d, 44, TOK, wdnT_in, ps, "r1a",
                                   h_ready, None, A[:, 0:44 * 512], stage,
                                   p.lnx, A[:, 44 * 512:44 * 512 + 8192],
                                   x3_in, x4_out, ones_bf, p.lnrows)

    xln2_mark = emit_ln_fm(p, x4_out, A, TOK, ps, ones_bf, ones_f, "l5b",
                           entry_stream=(rms_ost, rms_u))

    wcol_q = [p.sbuf(f"qwcol{j}", [128, NCH * 128], BF16) for j in range(2)]
    vmin = p.lnx[:, 0:1024]
    ebuf = p.lnx[:, 1024:2048]
    obq = p.lnxb_t[:, 0:1024]
    obk = p.lnxb_t[:, 1024:2048]
    obv = p.lnxb_t[:, 2048:3072]
    q_ost, q_u = emit_proj_fm(p, A, TOK, wT["q"], ps, "q", prep_all,
                              xln2_mark, wcol_q, vmin, ebuf, obq, q_out,
                              fmap=True)
    k_ost, k_u = emit_proj_tm(p, A, TOK, wT["k"], ps, "k", prep_all,
                              xln2_mark, stage, vmin, ebuf, obk, k_d,
                              fmap=True, pre_mm=p.mm.n, pre_ev=p.ev.n)
    v_ost, v_u = emit_proj_tm(p, A, TOK, wT["v"], ps, "v", prep_all,
                              xln2_mark, stage, vmin, ebuf, obv, v_d,
                              fmap=False, pre_mm=p.mm.n, pre_ev=p.ev.n)

    kcol = [A[:, j * 2048:(j + 1) * 2048] for j in range(2)]
    vcol = [A[:, (2 + j) * 2048:(3 + j) * 2048] for j in range(2)]
    kvks_sb = p.lnrows[:, 0:2064]
    kv_ost, kv_u = emit_kvks(p, k_d, v_d, TOK, ps, "kv",
                             [(k_ost, k_u), (v_ost, v_u)], ones_bf,
                             kvks_sb, kcol, vcol, kvks_out)

    @p.block.gpsimd
    def _(g):
        rms_ost.wait_all(g)
        q_ost.wait_all(g)
        kv_ost.wait_all(g)
    return p


def build_launch6(TOK=2048):
    assert TOK == 2048
    p = Prog()
    ps = p.psum_pool()
    p.evq = Sem(p, "evq")

    x4_in = p.dram("x4_in", [H, TOK], F32, kind="ExternalInput")
    q_in = p.dram("q_in", [H, TOK], BF16, kind="ExternalInput")
    kvp_in = p.dram("kvp_in", [512, 2064], F32, kind="ExternalInput")
    wo_in = p.dram("w_o", [H, H], F32, kind="ExternalInput")
    wguT_in = p.dram("wguT_in", [H, 11264], BF16, kind="ExternalInput")
    wdnT_in = p.dram("wdnT_in", [5632, H], BF16, kind="ExternalInput")

    x6_out = p.dram("x6_out", [H, TOK], F32, kind="ExternalOutput")

    wbf_o = p.dram("wbf_o", [H, H], BF16)
    wT_o = p.dram("wT_o", [H, H], BF16)
    o_d = p.dram("o_d", [TOK, H], BF16)
    x5_d = p.dram("x5_d", [H, TOK], F32)
    h2_d = p.dram("h2_d", [5632, TOK], BF16)

    ones_bf, ones_f, _ = emit_consts(p)
    p.ones_f_g = ones_f

    A = p.sbuf("A", [128, NCH * TOK], BF16)
    stage = [p.sbuf(f"stage{j}", [128, 8192], BF16) for j in range(2)]
    qstage = [p.sbuf(f"qstage{j}", [128, H], BF16) for j in range(2)]

    specs = [(wo_in.ap(), wbf_o, wT_o, H, H)]
    cast_m, tr_m, wbs = emit_weight_prep(p, specs, "l6", qstage)
    prep_o = (wbs, wbs.k - 1)

    lnx = p.named_buf("lnx", [128, 2 * 2048], F32)
    lnrows = p.named_buf("lnrows", [128, 2 * 2048], F32)
    lnxb = p.named_buf("lnxb", [128, 2 * 2048], BF16)
    kv_f = lnrows[:, 0:2064]
    ptmp = lnx[:, 0:2064]
    kv_b = lnxb[:, 0:2064]
    qrow = p.sbuf("qrow", [128, 2 * TOK], BF16)
    zb = p.sbuf("zb", [128, 4], F32)
    obuf = lnxb[:, 2064:2064 + 256]
    la_ost, la_u = emit_linattn(p, q_in, kvp_in, TOK, ps, "la", kv_f, ptmp,
                                kv_b, qrow, zb, obuf, o_d)

    owcol = [p.sbuf(f"owcol{j}", [128, NCH * 128], BF16) for j in range(2)]
    op_scr = p.sbuf("op_scr", [128, 2048], F32)
    wo_ost, wo_u = emit_down_resp(p, o_d, NCH, TOK, wT_o, ps, "wo",
                                  (la_ost, la_u), prep_o, A[:, 0:8192],
                                  owcol, op_scr, x4_in, x5_d, src_is_tm=True)

    xln_mark = emit_ln_fm(p, x5_d, A, TOK, ps, ones_bf, ones_f, "l6",
                          entry_stream=(wo_ost, wo_u))

    tmp = p.sbuf("gu_tmp", [128, 2 * 512], BF16)
    htile = p.sbuf("gu_ht", [128, 2 * 512], BF16)
    h_ready = emit_ffn_gu(p, A, TOK, wguT_in, wguT_in, h2_d, 5632, ps, "f1b",
                          None, xln_mark, stage, tmp, htile,
                          g_off=0, u_off=5632)

    rms_ost, rms_u = emit_down_rms(p, h2_d, 44, TOK, wdnT_in, ps, "r1b",
                                   h_ready, None, A[:, 0:44 * 512], stage,
                                   p.lnx, A[:, 44 * 512:44 * 512 + 8192],
                                   x5_d, x6_out, ones_bf, p.lnrows,
                                   x_pre=(wo_ost, wo_u))

    @p.block.gpsimd
    def _(g):
        rms_ost.wait_all(g)
    return p


import ml_dtypes
from concourse.bass_utils import run_bass_kernel_spmd

BFNP = ml_dtypes.bfloat16
F32NP = np.float32
SEQ = 8192
TOKC = 2048


def _ctab(qt):
    pos = np.arange(TOKC, dtype=np.float64) + qt * TOKC
    inv = 1.0 / (500000.0 ** (np.arange(64, dtype=np.float64) / 64))
    f = (pos[:, None] * inv[None, :]).astype(F32NP)
    cq = np.tile(np.cos(f) / np.sqrt(128.0), (1, 16))
    sq = np.tile(np.sin(f) / np.sqrt(128.0), (1, 16))
    ck = np.tile(np.cos(f), (1, 16))
    sk = np.tile(np.sin(f), (1, 16))
    return np.concatenate([cq, sq, ck, sk], 1).astype(BFNP)


def _tile(r):
    return np.ascontiguousarray(
        np.tile(np.asarray(r, F32NP).reshape(1, -1), (128, 1)))


def kernel(**inp):
    x = np.asarray(inp["L0_attn_hidden_states"], F32NP)
    cores = list(range(8))
    ident = np.eye(128, dtype=F32NP)

    ffn0_w = {
        "w_router": np.asarray(inp["L0_ffn_router_weight"], F32NP),
        "w_r_gate": np.asarray(inp["L0_ffn_r_gate"], F32NP),
        "w_r_up": np.asarray(inp["L0_ffn_r_up"], F32NP),
        "w_r_down": np.asarray(inp["L0_ffn_r_down"], F32NP),
        "w_sh_gate": np.asarray(inp["L0_ffn_sh_gate"], F32NP),
        "w_sh_up": np.asarray(inp["L0_ffn_sh_up"], F32NP),
        "w_sh_down": np.asarray(inp["L0_ffn_sh_down"], F32NP),
    }

    p = build_launch1(TOKC)
    nc = p.finish()
    ins = []
    for c in cores:
        b, qt = c // 4, c % 4
        d = dict(ffn0_w)
        d["ident_in"] = ident
        d["x_in"] = np.ascontiguousarray(
            x[b, qt * TOKC:(qt + 1) * TOKC, :].T)
        ins.append(d)
    r1 = run_bass_kernel_spmd(nc, ins, core_ids=cores).results
    x1 = [r1[c]["x1_out"] for c in cores]

    qks = np.asarray(inp["L0_attn_qk_scale"], F32NP)
    qko = np.asarray(inp["L0_attn_qk_offset"], F32NP)
    p = build_launch2(TOKC)
    nc = p.finish()
    ins = []
    for c in cores:
        b, qt = c // 4, c % 4
        ins.append(dict(
            x_in=x1[c],
            qkv_w=np.asarray(inp["L0_attn_qkv_w"], F32NP),
            qnb_in=_tile(inp["L0_attn_q_norm_w"]),
            knb_in=_tile(inp["L0_attn_k_norm_w"]),
            s0b_in=_tile(qks[:, 0]), s1b_in=_tile(qks[:, 1]),
            o0b_in=_tile(qko[:, 0]), o1b_in=_tile(qko[:, 1]),
            lrb_in=_tile(inp["L0_attn_lr_b"]),
            lr_w=np.pad(np.asarray(inp["L0_attn_lr_w"], F32NP),
                        ((0, 4), (0, 0))),
            ctab=_ctab(qt)))
    r2 = run_bass_kernel_spmd(nc, ins, core_ids=cores).results

    w0 = np.asarray(inp["L0_attn_w0"], F32NP)
    w1 = np.asarray(inp["L0_attn_w1"], F32NP)
    w2 = np.asarray(inp["L0_attn_w2"], F32NP)
    tttnb = _tile(inp["L0_attn_ttt_norm_w"])
    p = build_launch3(S=SEQ, QTOK=TOKC, CHUNK=512)
    nc = p.finish()
    ins = []
    for c in cores:
        b, fw = c // 4, c % 4
        qt = c % 4
        fq_u = np.concatenate(
            [r2[4 * b + q]["fq_out"][:, fw * 512:(fw + 1) * 512]
             for q in range(4)], 0)
        fk_u = np.concatenate(
            [r2[4 * b + q]["fk_out"][:, fw * 512:(fw + 1) * 512]
             for q in range(4)], 0)
        fv_u = np.concatenate(
            [r2[4 * b + q]["fv_out"][:, fw * 512:(fw + 1) * 512]
             for q in range(4)], 0)
        lr_u = np.concatenate(
            [np.ascontiguousarray(
                np.pad(r2[4 * b + q]["lr_out"][:, [fw, 4 + fw, 8 + fw]],
                       ((0, 0), (0, 1))))
             for q in range(4)], 0).astype(F32NP)
        ka_own = r2[c]["ka_out"]
        va_own = r2[c]["va_out"]
        if qt == 0:
            ka_prev = np.zeros_like(ka_own)
            va_prev = np.zeros_like(va_own)
            lo = np.full((128, 1), 2047.0, F32NP)
        else:
            ka_prev = r2[c - 1]["ka_out"]
            va_prev = r2[c - 1]["va_out"]
            lo = np.full((128, 1), -1e30, F32NP)
        ins.append(dict(
            fq_u=np.ascontiguousarray(fq_u),
            fk_u=np.ascontiguousarray(fk_u),
            fv_u=np.ascontiguousarray(fv_u), lr_u=lr_u,
            w0_in=np.ascontiguousarray(w0[fw]),
            w1_in=np.ascontiguousarray(w1[fw]),
            w2_in=np.ascontiguousarray(w2[fw]),
            tttnb_in=tttnb, ident_in=ident,
            qa_own=r2[c]["qa_out"],
            ka_ctx=np.concatenate([ka_prev, ka_own], 0),
            va_ctx=np.concatenate([va_prev, va_own], 0),
            lo_in=lo))
    r3 = run_bass_kernel_spmd(nc, ins, core_ids=cores).results

    gu_pk = np.asarray(inp["L1_ffn_gate_up_packed"])
    gu_sc = np.asarray(inp["L1_ffn_gate_up_scales"])
    dn_pk = np.asarray(inp["L1_ffn_down_packed"])
    dn_sc = np.asarray(inp["L1_ffn_down_scales"])
    p = build_launch4(TOKC)
    nc = p.finish()
    ins = []
    for c in cores:
        b, qt = c // 4, c % 4
        ottt = np.concatenate(
            [r3[4 * b + fw]["ottt_out"][qt * TOKC:(qt + 1) * TOKC, :]
             for fw in range(4)], 1)
        d = dict(ffn0_w)
        d.update(
            x1_in=x1[c], attn_in=r3[c]["attn_out"],
            ottt_in=np.ascontiguousarray(ottt),
            w_oproj=np.asarray(inp["L0_attn_o_proj_w"], F32NP),
            ident_in=ident,
            gu_pk=np.ascontiguousarray(gu_pk[c * 1408:(c + 1) * 1408, :]),
            gu_sc=np.ascontiguousarray(gu_sc[:, c * 1408:(c + 1) * 1408].T),
            dn_pk=np.ascontiguousarray(dn_pk[c * 256:(c + 1) * 256, :]),
            dn_sc=np.ascontiguousarray(dn_sc[:, c * 256:(c + 1) * 256].T))
        ins.append(d)
    r4 = run_bass_kernel_spmd(nc, ins, core_ids=cores).results
    wguT = np.concatenate([r4[c]["wguT_out"] for c in cores], 1)
    wdnT = np.concatenate([r4[c]["wdnT_out"] for c in cores], 1)
    wguT = np.ascontiguousarray(wguT)
    wdnT = np.ascontiguousarray(wdnT)

    p = build_launch5(TOKC)
    nc = p.finish()
    ins = []
    for c in cores:
        ins.append(dict(
            x3_in=r4[c]["x3_out"], wguT_in=wguT, wdnT_in=wdnT,
            w_q=np.asarray(inp["L1_attn_Wq"], F32NP),
            w_k=np.asarray(inp["L1_attn_Wk"], F32NP),
            w_v=np.asarray(inp["L1_attn_Wv"], F32NP)))
    r5 = run_bass_kernel_spmd(nc, ins, core_ids=cores).results

    p = build_launch6(TOKC)
    nc = p.finish()
    ins = []
    for c in cores:
        b = c // 4
        kvp = np.concatenate([r5[4 * b + j]["kvks_out"] for j in range(4)], 0)
        ins.append(dict(
            x4_in=r5[c]["x4_out"], q_in=r5[c]["q_out"],
            kvp_in=np.ascontiguousarray(kvp),
            w_o=np.asarray(inp["L1_attn_Wo"], F32NP),
            wguT_in=wguT, wdnT_in=wdnT))
    r6 = run_bass_kernel_spmd(nc, ins, core_ids=cores).results

    out = np.empty((2, SEQ, H), F32NP)
    for c in cores:
        b, qt = c // 4, c % 4
        out[b, qt * TOKC:(qt + 1) * TOKC, :] = r6[c]["x6_out"].T
    return out


# version 09
# speedup vs baseline: 1.0170x; 1.0170x over previous
import math
from contextlib import ExitStack

import numpy as np
import concourse.bass as bass
import concourse.mybir as mybir

F32 = mybir.dt.float32
BF16 = mybir.dt.bfloat16
I32 = mybir.dt.int32
F16 = mybir.dt.float16
AF = mybir.ActivationFunctionType
ALU = mybir.AluOpType
AX = mybir.AxisListType

H = 2048
NCH = H // 128

CV_EPS_LN = 0
CV_ONE = 1
CV_EPS_RMS = 2
CV_HALF = 3
CV_LR = 4


class Sem:
    def __init__(self, prog, name):
        self.h = prog.ctx.enter_context(prog.nc.semaphore(name))
        self.n = 0

    def inc(self, inst, k=1):
        inst.then_inc(self.h, k)
        self.n += k
        return self.n

    def wait(self, eng, val=None):
        v = self.n if val is None else val
        if v > 0:
            eng.wait_ge(self.h, v)


class DStream:

    def __init__(self, p, name, nslots=2):
        self.sems = [Sem(p, f"{name}{j}") for j in range(nslots)]
        self.counts = [0] * nslots
        self.n = nslots
        self.k = 0
        self.use_map = []

    def issue_slot(self, slot, inst):
        self.sems[slot].inc(inst, 16)
        self.counts[slot] += 1
        self.use_map.append((slot, self.counts[slot]))
        self.k += 1
        return self.k - 1

    def issue(self, inst):
        return self.issue_slot(self.k % self.n, inst)

    def wait(self, eng, use):
        slot, cnt = self.use_map[use]
        eng.wait_ge(self.sems[slot].h, 16 * cnt)

    def wait_idx(self, eng, use):
        eng.wait_ge(self.sems[use % self.n].h, 16 * (use // self.n + 1))

    def wait_all(self, eng, thru=None):
        t = self.k if thru is None else thru + 1
        seen = {}
        for u in range(t):
            slot, cnt = self.use_map[u]
            seen[slot] = max(seen.get(slot, 0), cnt)
        for slot, cnt in seen.items():
            eng.wait_ge(self.sems[slot].h, 16 * cnt)


class Prog:
    def __init__(self):
        self.nc = bass.Bass("TRN2", target_bir_lowering=False)
        self.ctx = ExitStack()
        self.block = self.ctx.enter_context(self.nc.Block())
        self.sb_bytes = 0
        for r in ["mm", "ev", "ac", "x1", "chv", "chs", "chg"]:
            setattr(self, r, Sem(self, r))
        self.streams = {}

    def chain(self, eng, inst, which="v"):
        s = {"v": self.chv, "s": self.chs, "g": self.chg}[which]
        s.inc(inst)
        eng.wait_ge(s.h, s.n)

    def stream(self, name, nslots=2):
        if name not in self.streams:
            self.streams[name] = DStream(self, name, nslots)
        assert self.streams[name].n == nslots
        return self.streams[name]

    def named_buf(self, name, shape, dtype):
        if not hasattr(self, "_nbufs"):
            self._nbufs = {}
        if name in self._nbufs:
            t, sh, dt_ = self._nbufs[name]
            assert sh == tuple(shape) and dt_ == dtype, (name, sh, shape)
            return t
        t = self.sbuf(name, shape, dtype)
        self._nbufs[name] = (t, tuple(shape), dtype)
        return t

    def sbuf(self, name, shape, dtype):
        esz = {F32: 4, BF16: 2, I32: 4, F16: 2}[dtype]
        self.sb_bytes += shape[1] * esz
        assert self.sb_bytes < 206 * 1024, f"SBUF over budget at {name}: {self.sb_bytes}"
        return self.ctx.enter_context(self.nc.sbuf_tensor(name, shape, dtype))

    def psum_pool(self):
        return [self.ctx.enter_context(self.nc.psum_tensor(f"pp{i}", [128, 512], F32))
                for i in range(8)]

    def dram(self, name, shape, dtype, **kw):
        return self.nc.dram_tensor(name, shape, dtype, **kw)

    def finish(self):
        self.ctx.close()
        return self.nc


def rearr_rows(ap_2d, p=128):
    return ap_2d.rearrange("(a p) c -> p a c", p=p)


def emit_consts(p):
    ones_bf = p.sbuf("ones_bf", [128, 128], BF16)
    ones_f = p.sbuf("ones_f", [128, 128], F32)
    cvals = p.sbuf("cvals", [128, 8], F32)

    @p.block.vector
    def _(v):
        v.memset(ones_bf[:], 1.0)
        v.memset(cvals[:, CV_EPS_LN:CV_EPS_LN + 1], 1e-5)
        v.memset(cvals[:, CV_ONE:CV_ONE + 1], 1.0)
        v.memset(cvals[:, CV_EPS_RMS:CV_EPS_RMS + 1], 1e-6)
        v.memset(cvals[:, CV_HALF:CV_HALF + 1], 0.5)
        p.x1.inc(v.memset(ones_f[:], 1.0))
    p.cvals = cvals
    return ones_bf, ones_f, p.x1.n


def emit_weight_prep(p, specs, tag, stage, ready_map=None, piece=None):
    cst = p.stream(f"wpcast{tag}", 2)
    trs = p.stream(f"wptr{tag}", 2)
    wbs = p.stream(f"wpwb{tag}", 2)
    cast_done = {}
    tr_done = {}
    bwid = stage[0].shape[1] if piece is None else piece

    @p.block.gpsimd
    def _(g):
        for j, (w_ap, wbf, wT, o, i) in enumerate(specs):
            if w_ap is None:
                continue
            rows = max(128, min(o, (1 << 21) // max(i, 1)))
            r0 = 0
            while r0 < o:
                r1 = min(o, r0 + rows)
                cst.issue(g.dma_start(wbf.ap()[r0:r1, :],
                                      w_ap[r0:r1, :]))
                r0 = r1
            cast_done[j] = cst.k - 1

    @p.block.sync
    def _(s):
        for j, (w_ap, wbf, wT, o, i) in enumerate(specs):
            if wT is None:
                tr_done[j] = wbs.k - 1
                continue
            if w_ap is None:
                rdy = (ready_map or {}).get(j)
                if rdy is not None:
                    if isinstance(rdy, list):
                        for rs, ru in rdy:
                            rs.wait_all(s, ru)
                    else:
                        rdy[0].wait_all(s, rdy[1])
            else:
                cst.wait_all(s, cast_done[j])
            for ic in range(i // 128):
                for c0 in range(0, o, bwid):
                    c1 = min(o, c0 + bwid)
                    b = stage[trs.k % 2]
                    if trs.k >= 2:
                        wbs.wait(s, wbs.k - 2)
                    u = trs.issue(s.dma_start(
                        b[:, 0:c1 - c0],
                        wbf.ap()[c0:c1, ic * 128:(ic + 1) * 128],
                        transpose=True))
                    trs.wait(s, u)
                    wbs.issue(s.dma_start(
                        wT.ap()[ic * 128:(ic + 1) * 128, c0:c1],
                        b[:, 0:c1 - c0]))
            tr_done[j] = wbs.k - 1
    return cast_done, tr_done, wbs


def emit_ln_fm(p, x_dram, xln_sb, TOK, ps, ones_bf, ones_f, tag, t0=0,
               G=None, entry_ev=0, xln_dram=None, entry_stream=None):
    G = G or min(2048, TOK)
    nt = G // 512
    xbuf = p.named_buf("lnx", [128, 2 * G], F32)
    xbuf2 = xbuf
    p.lnx = xbuf
    xb = p.named_buf("lnxb", [128, 2 * G], BF16)
    xsq = p.named_buf("lnsq", [128, 2 * G], BF16)
    rowsT = p.named_buf("lnrows", [128, 2 * G], F32)
    row = rowsT[0:1, :]
    srow = rowsT[32:33, :]
    p.lnrows = rowsT
    mub = p.named_buf("lnmu", [128, G], F32)
    rb = p.named_buf("lnrb", [128, G], F32)
    p.lnmu = mub
    p.lnsq = xsq
    p.lnxb_t = xb

    lds = p.stream("lnld", 2)
    ld2 = p.stream("lnld2", 2)
    xst = p.stream("lnxst", 2) if xln_dram is not None else None
    xst_k0 = xst.k if xst is not None else 0
    ev0 = p.ev.n
    ac0 = p.ac.n
    mm0 = p.mm.n
    ld_use = []
    ld2_use = []

    @p.block.sync
    def _(s):
        if entry_ev:
            s.wait_ge(p.ev.h, entry_ev)
        if entry_stream is not None:
            entry_stream[0].wait_all(s, entry_stream[1])
        for c in range(NCH):
            if c >= 2:
                s.wait_ge(p.ev.h, ev0 + c - 1)
                s.wait_ge(p.ac.h, ac0 + c - 1)
            ld_use.append(lds.issue(
                s.dma_start(xbuf[:, (c % 2) * G:(c % 2 + 1) * G],
                            x_dram.ap()[c * 128:(c + 1) * 128, t0:t0 + G])))

    @p.block.vector
    def _(v):
        for c in range(NCH):
            lds.wait(v, ld_use[c])
            if c >= 2:
                v.wait_ge(p.mm.h, mm0 + c - 1)
            elif xst is not None and xst_k0 >= 2:
                xst.wait_idx(v, xst_k0 - 2 + c)
            p.ev.inc(v.tensor_copy(xb[:, (c % 2) * G:(c % 2 + 1) * G],
                                   xbuf[:, (c % 2) * G:(c % 2 + 1) * G]))

    @p.block.scalar
    def _(sc):
        for c in range(NCH):
            lds.wait(sc, ld_use[c])
            if c >= 2:
                sc.wait_ge(p.mm.h, mm0 + c - 1)
            p.ac.inc(sc.activation(xsq[:, (c % 2) * G:(c % 2 + 1) * G],
                                   xbuf[:, (c % 2) * G:(c % 2 + 1) * G],
                                   AF.Square))

    @p.block.tensor
    def _(t):
        for c in range(NCH):
            t.wait_ge(p.ev.h, ev0 + c + 1)
            t.wait_ge(p.ac.h, ac0 + c + 1)
            last = None
            for ti in range(nt):
                sl = slice((c % 2) * G + ti * 512, (c % 2) * G + (ti + 1) * 512)
                t.matmul(ps[ti][0:1, :], ones_bf[:, 0:1], xb[:, sl],
                         start=(c == 0), stop=(c == NCH - 1))
                last = t.matmul(ps[4 + ti][0:1, :], ones_bf[:, 0:1], xsq[:, sl],
                                start=(c == 0), stop=(c == NCH - 1))
            p.mm.inc(last)
    stats_mm = p.mm.n

    @p.block.vector
    def _(v):
        v.wait_ge(p.mm.h, stats_mm)
        last = None
        for ti in range(nt):
            v.tensor_copy(srow[:, ti * 512:(ti + 1) * 512], ps[ti][0:1, :])
            last = v.tensor_copy(srow[:, G + ti * 512:G + (ti + 1) * 512],
                                 ps[4 + ti][0:1, :])
        p.chain(v, last)
        p.chain(v, v.tensor_scalar_mul(row[:, 0:G], srow[:, 0:G], 1.0 / H))
        p.chain(v, v.tensor_mul(srow[:, 0:G], row[:, 0:G], row[:, 0:G]))
        p.ev.inc(v.scalar_tensor_tensor(srow[:, G:2 * G], srow[:, G:2 * G],
                                        1.0 / H, srow[:, 0:G],
                                        ALU.mult, ALU.subtract))
    var_m = p.ev.n

    @p.block.scalar
    def _(sc):
        sc.wait_ge(p.ev.h, var_m)
        p.ac.inc(sc.activation(srow[:, 0:G], srow[:, G:2 * G], AF.Sqrt,
                               bias=p.cvals[0:1, CV_EPS_LN:CV_EPS_LN + 1]))
    sd_m = p.ac.n

    @p.block.vector
    def _(v):
        v.wait_ge(p.ac.h, sd_m)
        p.ev.inc(v.reciprocal(row[:, G:2 * G], srow[:, 0:G]))
    r_m = p.ev.n

    bc_m = []

    @p.block.tensor
    def _(t):
        t.wait_ge(p.ev.h, r_m)
        for ti in range(nt):
            if ti >= 1:
                t.wait_ge(p.ev.h, r_m + ti)
            t.matmul(ps[0][:, :], ones_f[0:1, :],
                     row[:, ti * 512:(ti + 1) * 512], start=True, stop=True)
            bc_m.append(p.mm.inc(
                t.matmul(ps[1][:, :], ones_f[0:1, :],
                         row[:, G + ti * 512:G + (ti + 1) * 512],
                         start=True, stop=True)))

    @p.block.vector
    def _(v):
        for ti in range(nt):
            v.wait_ge(p.mm.h, bc_m[ti])
            v.tensor_copy(mub[:, ti * 512:(ti + 1) * 512], ps[0][:, :])
            p.ev.inc(v.tensor_copy(rb[:, ti * 512:(ti + 1) * 512], ps[1][:, :]))
    bcast_done = p.ev.n
    ap0 = p.ev.n

    @p.block.gpsimd
    def _(g):
        for c in range(NCH):
            if c >= 2:
                g.wait_ge(p.ev.h, ap0 + c - 1)
            else:
                g.wait_ge(p.mm.h, stats_mm)
            ld2_use.append(ld2.issue(
                g.dma_start(xbuf2[:, (c % 2) * G:(c % 2 + 1) * G],
                            x_dram.ap()[c * 128:(c + 1) * 128, t0:t0 + G])))

    @p.block.vector
    def _(v):
        v.wait_ge(p.ev.h, bcast_done)
        for c in range(NCH):
            ld2.wait(v, ld2_use[c])
            sl2 = slice((c % 2) * G, (c % 2 + 1) * G)
            p.chain(v, v.tensor_sub(xbuf2[:, sl2], xbuf2[:, sl2], mub[:, 0:G]))
            if xln_dram is None:
                p.ev.inc(v.tensor_mul(
                    xln_sb[:, c * TOK + t0:c * TOK + t0 + G],
                    xbuf2[:, sl2], rb[:, 0:G]))
            else:
                if c >= 2:
                    xst.wait_idx(v, xst_k0 + c - 2)
                p.ev.inc(v.tensor_mul(
                    xb[:, (c % 2) * G:(c % 2) * G + G],
                    xbuf2[:, sl2], rb[:, 0:G]))

    if xln_dram is not None:
        ap_base2 = ap0

        @p.block.sync
        def _(s):
            for c in range(NCH):
                s.wait_ge(p.ev.h, ap_base2 + c + 1)
                xst.issue(s.dma_start(
                    xln_dram.ap()[c * 128:(c + 1) * 128, t0:t0 + G],
                    xb[:, (c % 2) * G:(c % 2) * G + G]))
    return p.ev.n



def slab_load_split(stream, slot, s, dst_sb, src_ap, KC, width, nmax=8):
    u = None
    a0 = 0
    while a0 < KC:
        a1 = min(KC, a0 + nmax)
        dst = dst_sb[:, a0 * width:a1 * width].rearrange("p (a c) -> p a c",
                                                         a=a1 - a0)
        srcp = rearr_rows(src_ap[a0 * 128:a1 * 128, :])
        u = stream.issue_slot(slot, s.dma_start(dst, srcp))
        a0 = a1
    return u


def emit_ffn_gu(p, xln_sb, TOK, gT, uT, h_dram, OUT, ps, tag,
                w_ready, x_mark, wsl, tmp, htile, g_off=0, u_off=0):
    nt = TOK // 512
    nog = OUT // 512
    mm0 = p.mm.n
    ac0 = p.ac.n
    ev0 = p.ev.n
    wstr = p.stream("gusl", 2)
    hst = p.stream("guhst", 2)
    hst_k0 = hst.k
    ld_use = []

    @p.block.sync
    def _(s):
        if w_ready is not None:
            wrs, wru = w_ready
            wrs.wait_all(s, wru)
        for og in range(nog):
            if og >= 1:
                s.wait_ge(p.mm.h, mm0 + og * nt * 4 * 2)
            elif mm0 > 0:
                s.wait_ge(p.mm.h, mm0)
            src_g = gT.ap()[:, g_off + og * 512:g_off + (og + 1) * 512]
            src_u = uT.ap()[:, u_off + og * 512:u_off + (og + 1) * 512]
            slab_load_split(wstr, 0, s, wsl[0], src_g, NCH, 512)
            uu = slab_load_split(wstr, 1, s, wsl[1], src_u, NCH, 512)
            ld_use.append(uu)

    @p.block.tensor
    def _(t):
        t.wait_ge(p.ev.h, x_mark)
        it = 0
        for og in range(nog):
            wstr.wait_all(t, ld_use[og])
            for ti in range(nt):
                for ob in range(4):
                    bg = ps[0 + (it % 2)]
                    bu = ps[2 + (it % 2)]
                    if it >= 2:
                        t.wait_ge(p.ac.h, ac0 + it - 1)
                        t.wait_ge(p.ev.h, ev0 + it - 1)
                    else:
                        if ac0 > 0:
                            t.wait_ge(p.ac.h, ac0)
                        if ev0 > 0:
                            t.wait_ge(p.ev.h, ev0)
                    last = None
                    for kc in range(NCH):
                        last = t.matmul(
                            bg[:, :],
                            wsl[0][:, kc * 512 + ob * 128:kc * 512 + ob * 128 + 128],
                            xln_sb[:, kc * TOK + ti * 512:kc * TOK + (ti + 1) * 512],
                            start=(kc == 0), stop=(kc == NCH - 1))
                    p.mm.inc(last)
                    for kc in range(NCH):
                        last = t.matmul(
                            bu[:, :],
                            wsl[1][:, kc * 512 + ob * 128:kc * 512 + ob * 128 + 128],
                            xln_sb[:, kc * TOK + ti * 512:kc * TOK + (ti + 1) * 512],
                            start=(kc == 0), stop=(kc == NCH - 1))
                    p.mm.inc(last)
                    it += 1

    n_it = nog * nt * 4

    @p.block.scalar
    def _(sc):
        for it in range(n_it):
            sc.wait_ge(p.mm.h, mm0 + 2 * it + 1)
            if it >= 2:
                sc.wait_ge(p.ev.h, ev0 + it - 1)
            elif ev0 > 0:
                sc.wait_ge(p.ev.h, ev0)
            bg = ps[0 + (it % 2)]
            p.ac.inc(sc.activation(tmp[:, (it % 2) * 512:(it % 2 + 1) * 512],
                                   bg[:, :], AF.Silu))

    @p.block.vector
    def _(v):
        for it in range(n_it):
            v.wait_ge(p.mm.h, mm0 + 2 * it + 2)
            v.wait_ge(p.ac.h, ac0 + it + 1)
            if it >= 2:
                hst.wait_idx(v, hst_k0 + it - 2)
            elif hst_k0 > 0:
                hst.wait_all(v, hst_k0 - 1)
            bu = ps[2 + (it % 2)]
            p.ev.inc(v.tensor_mul(htile[:, (it % 2) * 512:(it % 2 + 1) * 512],
                                  tmp[:, (it % 2) * 512:(it % 2 + 1) * 512],
                                  bu[:, :]))

    @p.block.gpsimd
    def _(g):
        it = 0
        for og in range(nog):
            for ti in range(nt):
                for ob in range(4):
                    g.wait_ge(p.ev.h, ev0 + it + 1)
                    row0 = og * 512 + ob * 128
                    hst.issue(g.dma_start(
                        h_dram.ap()[row0:row0 + 128, ti * 512:(ti + 1) * 512],
                        htile[:, (it % 2) * 512:(it % 2 + 1) * 512]))
                    it += 1
    return hst, hst.k - 1


def emit_ffn_down(p, h_dram, INTER, TOK, dT, ps, tag, h_ready, w_ready,
                  hslab, wcol, kind, scr, comb_b=None, yr_dram=None,
                  x_dram=None, xout_dram=None, ev_pre_wait=0, st_pre=None,
                  x_pre=None):
    KC = INTER // 128
    nt = TOK // 512
    mm0 = p.mm.n
    ev0 = p.ev.n
    hstr = p.stream("dnh", 1)
    wstr = p.stream("dnw", 2)
    xld = p.stream("dnx", 2)
    yld = p.stream("dny", 2)
    ost = p.stream("dnout", 2)
    ost_k0 = ost.k
    xld_k0 = xld.k
    yld_k0 = yld.k
    hld_use = []
    wld_use = []

    n_it = nt * 16
    dst_out = xout_dram if kind == "res" else yr_dram

    def emit_store(s, it_s):
        t_s, o2_s = divmod(it_s, 16)
        s.wait_ge(p.ev.h, ev0 + it_s + 1)
        ost.issue(s.dma_start(
            dst_out.ap()[o2_s * 128:(o2_s + 1) * 128, t_s * 512:(t_s + 1) * 512],
            scr[:, 2048:3072][:, (it_s % 2) * 512:(it_s % 2 + 1) * 512]))

    @p.block.sync
    def _(s):
        hrs, hru = h_ready
        hrs.wait_all(s, hru)
        wrs, wru = w_ready
        wrs.wait_all(s, wru)
        hstr.wait_all(s)
        wstr.wait_all(s)
        ost.wait_all(s)
        if ev_pre_wait:
            s.wait_ge(p.ev.h, ev_pre_wait)
        it = 0
        for t in range(nt):
            if t >= 1:
                s.wait_ge(p.mm.h, mm0 + t * 16)
            src = h_dram.ap()[:, t * 512:(t + 1) * 512]
            hld_use.append(slab_load_split(hstr, 0, s, hslab, src, KC, 512))
            for o2 in range(16):
                if it >= 2:
                    s.wait_ge(p.mm.h, mm0 + it - 1)
                srcw = dT.ap()[:, o2 * 128:(o2 + 1) * 128]
                wld_use.append(slab_load_split(wstr, it % 2, s, wcol[it % 2],
                                               srcw, KC, 128))
                if it >= 2:
                    emit_store(s, it - 2)
                it += 1
        emit_store(s, n_it - 2)
        emit_store(s, n_it - 1)

    if kind == "res":
        @p.block.gpsimd
        def _(g):
            if st_pre is not None:
                st_pre[0].wait_all(g, st_pre[1])
            if x_pre is not None:
                x_pre[0].wait_all(g, x_pre[1])
            it = 0
            for t in range(nt):
                for o2 in range(16):
                    if it >= 2:
                        g.wait_ge(p.ev.h, ev0 + it - 1)
                    sl = slice((it % 2) * 512, (it % 2 + 1) * 512)
                    xld.issue(g.dma_start(scr[:, 0:1024][:, sl],
                                          x_dram.ap()[o2 * 128:(o2 + 1) * 128,
                                                      t * 512:(t + 1) * 512]))
                    yld.issue(g.dma_start(scr[:, 1024:2048][:, sl],
                                          yr_dram.ap()[o2 * 128:(o2 + 1) * 128,
                                                       t * 512:(t + 1) * 512]))
                    it += 1

    @p.block.tensor
    def _(t_):
        it = 0
        for t in range(nt):
            hstr.wait(t_, hld_use[t])
            for o2 in range(16):
                wstr.wait(t_, wld_use[it])
                if it >= 2:
                    t_.wait_ge(p.ev.h, ev0 + it - 1)
                elif ev0 > 0:
                    t_.wait_ge(p.ev.h, ev0)
                bank = ps[4 + (it % 2)]
                last = None
                for kc in range(KC):
                    last = t_.matmul(bank[:, :],
                                     wcol[it % 2][:, kc * 128:(kc + 1) * 128],
                                     hslab[:, kc * 512:(kc + 1) * 512],
                                     start=(kc == 0), stop=(kc == KC - 1))
                p.mm.inc(last)
                it += 1

    @p.block.vector
    def _(v):
        it = 0
        for t in range(nt):
            for o2 in range(16):
                v.wait_ge(p.mm.h, mm0 + it + 1)
                bank = ps[4 + (it % 2)]
                sl = slice((it % 2) * 512, (it % 2 + 1) * 512)
                if it >= 2:
                    ost.wait(v, ost_k0 + it - 2)
                elif ost_k0 > 0:
                    ost.wait_all(v, ost_k0 - 1)
                if kind == "yr":
                    if ev_pre_wait:
                        v.wait_ge(p.ev.h, ev_pre_wait)
                    p.ev.inc(v.tensor_mul(scr[:, 2048:3072][:, sl], bank[:, :],
                                          comb_b[:, t * 512:(t + 1) * 512]))
                else:
                    xld.wait(v, xld_k0 + it)
                    yld.wait(v, yld_k0 + it)
                    p.chain(v, v.tensor_add(scr[:, 2048:3072][:, sl], bank[:, :],
                                            scr[:, 1024:2048][:, sl]))
                    p.ev.inc(v.scalar_tensor_tensor(
                        scr[:, 2048:3072][:, sl], scr[:, 2048:3072][:, sl], 0.5,
                        scr[:, 0:1024][:, sl], ALU.mult, ALU.add))
                it += 1

    return ost, ost.k - 1


def emit_router_comb(p, xln_sb, TOK, router_w_ap, ps, ones_f, comb_b, tag,
                     x_mark, ident_f, ident_ready=None):
    nb = TOK // 128
    rsb = p.sbuf(f"rt_w_{tag}", [8, H], F32)
    rT = p.sbuf(f"rt_wT_{tag}", [128, NCH * 8], BF16)
    lg = p.sbuf(f"rt_lg_{tag}", [128, 16], F32)
    col = p.sbuf(f"rt_col_{tag}", [128, 8], F32)
    combc = p.sbuf(f"rt_comb_{tag}", [128, nb], F32)
    Gln = min(2048, TOK)
    crow = p.lnrows[0:1, Gln:Gln + TOK]

    rld = p.stream("rtld", 1)

    @p.block.gpsimd
    def _(g):
        rld.issue(g.dma_start(rsb[:], router_w_ap))
    rld_use = rld.k - 1

    ev_rt0 = p.ev.n
    tr_m = []

    @p.block.tensor
    def _(t):
        rld.wait(t, rld_use)
        t.wait_ge(p.x1.h, 1)
        if ident_ready is not None:
            ident_ready[0].wait(t, ident_ready[1])
        for c in range(NCH):
            if c >= 2:
                t.wait_ge(p.ev.h, ev_rt0 + c - 1)
            bank = ps[7] if c % 2 == 0 else ps[5]
            tr_m.append(p.mm.inc(
                t.transpose(bank[0:128, 0:8],
                            rsb[:, c * 128:(c + 1) * 128], ident_f[0:8, 0:8])))

    @p.block.vector
    def _(v):
        for c in range(NCH):
            v.wait_ge(p.mm.h, tr_m[c])
            bank = ps[7] if c % 2 == 0 else ps[5]
            p.ev.inc(v.tensor_copy(rT[:, c * 8:(c + 1) * 8], bank[0:128, 0:8]))
    rT_done = p.ev.n

    evA0 = p.ev.n
    lg_m = []

    @p.block.tensor
    def _(t):
        t.wait_ge(p.ev.h, rT_done)
        t.wait_ge(p.ev.h, x_mark)
        for tb in range(nb):
            if tb >= 2:
                t.wait_ge(p.ev.h, evA0 + 2 * (tb - 2) + 1)
            last = None
            lbank = ps[6] if tb % 2 == 0 else ps[4]
            for c in range(NCH):
                last = t.matmul(lbank[:, 0:8],
                                xln_sb[:, c * TOK + tb * 128:c * TOK + (tb + 1) * 128],
                                rT[:, c * 8:(c + 1) * 8],
                                start=(c == 0), stop=(c == NCH - 1))
            lg_m.append(p.mm.inc(last))

    ac0 = p.ac.n

    @p.block.vector
    def _(v):
        for tb in range(nb):
            v.wait_ge(p.mm.h, lg_m[tb])
            lbank = ps[6] if tb % 2 == 0 else ps[4]
            p.chain(v, v.tensor_copy(lg[:, 0:8], lbank[:, 0:8]))
            p.chain(v, v.reduce_max(col[:, 0:1], lg[:, 0:8], axis=AX.X))
            p.chain(v, v.tensor_scalar(lg[:, 8:16], lg[:, 0:8], col[:, 0:1],
                                       None, ALU.is_equal))
            p.chain(v, v.scalar_tensor_tensor(lg[:, 8:16], lg[:, 8:16], -1e30,
                                              lg[:, 0:8], ALU.mult, ALU.add))
            p.chain(v, v.reduce_max(col[:, 1:2], lg[:, 8:16], axis=AX.X))
            p.ev.inc(v.tensor_scalar_mul(col[:, 4:5], col[:, 0:1], -1.0))
            v.wait_ge(p.ac.h, ac0 + tb + 1)
            p.chain(v, v.tensor_scalar_add(col[:, 3:4], col[:, 2:3], 1.0))
            p.chain(v, v.reciprocal(col[:, 3:4], col[:, 3:4]))
            p.chain(v, v.tensor_mul(col[:, 5:6], col[:, 2:3], col[:, 3:4]))
            p.ev.inc(v.tensor_add(combc[:, tb:tb + 1], col[:, 5:6],
                                  col[:, 3:4]))

    @p.block.scalar
    def _(sc):
        for tb in range(nb):
            sc.wait_ge(p.ev.h, evA0 + 2 * tb + 1)
            p.ac.inc(sc.activation(col[:, 2:3], col[:, 1:2], AF.Exp,
                                   bias=col[:, 4:5]))

    evB0 = evA0 + 2 * nb
    trc_m = []

    @p.block.tensor
    def _(t):
        for tb in range(nb):
            t.wait_ge(p.ev.h, evA0 + 2 * tb + 2)
            if tb >= 2:
                t.wait_ge(p.ev.h, evB0 + tb - 1)
            cbank = ps[7] if tb % 2 == 0 else ps[5]
            trc_m.append(p.mm.inc(
                t.transpose(cbank[0:1, 0:128],
                            combc[:, tb:tb + 1], ident_f[:, :])))

    @p.block.vector
    def _(v):
        for tb in range(nb):
            v.wait_ge(p.mm.h, trc_m[tb])
            cbank = ps[7] if tb % 2 == 0 else ps[5]
            p.ev.inc(v.tensor_copy(
                crow[:, tb * 128:(tb + 1) * 128], cbank[0:1, 0:128]))
    crow_done = p.ev.n
    evC0 = p.ev.n
    bc_m = []

    @p.block.tensor
    def _(t):
        t.wait_ge(p.ev.h, crow_done)
        for ti in range(TOK // 512):
            if ti >= 1:
                t.wait_ge(p.ev.h, evC0 + ti)
            bc_m.append(p.mm.inc(
                t.matmul(ps[6][:, :], ones_f[0:1, :],
                         crow[:, ti * 512:(ti + 1) * 512],
                         start=True, stop=True)))

    @p.block.vector
    def _(v):
        for ti in range(TOK // 512):
            v.wait_ge(p.mm.h, bc_m[ti])
            p.ev.inc(v.tensor_copy(comb_b[:, ti * 512:(ti + 1) * 512],
                                   ps[6][:, :]))
    return p.ev.n


def build_launch1(TOK=2048):
    p = Prog()
    ps = p.psum_pool()

    x_in = p.dram("x_in", [H, TOK], F32, kind="ExternalInput")
    ident_in = p.dram("ident_in", [128, 128], F32, kind="ExternalInput")
    w_names = ["r_gate", "r_up", "r_down", "sh_gate", "sh_up", "sh_down"]
    w_shapes = {"sh_gate": (8192, H), "sh_up": (8192, H), "sh_down": (H, 8192),
                "r_gate": (1024, H), "r_up": (1024, H), "r_down": (H, 1024)}
    w_in = {n: p.dram(f"w_{n}", list(w_shapes[n]), F32, kind="ExternalInput")
            for n in w_names}
    router_in = p.dram("w_router", [8, H], F32, kind="ExternalInput")
    x1_out = p.dram("x1_out", [H, TOK], F32, kind="ExternalOutput")

    wbf = {n: p.dram(f"wbf_{n}", list(w_shapes[n]), BF16) for n in w_names}
    wT = {n: p.dram(f"wT_{n}", [w_shapes[n][1], w_shapes[n][0]], BF16)
          for n in w_names}
    h_dram = p.dram("h_d", [8192, TOK], BF16)
    hr_dram = p.dram("hr_d", [1024, TOK], BF16)
    yr_dram = p.dram("yr_d", [H, TOK], F32)

    ones_bf, ones_f, _ = emit_consts(p)
    ident_f = p.sbuf("ident_f", [128, 128], F32)
    ild = p.stream("identld", 1)

    @p.block.gpsimd
    def _(g):
        ild.issue(g.dma_start(ident_f[:], ident_in.ap()))

    A = p.sbuf("A", [128, NCH * TOK], BF16)
    xln_mark = emit_ln_fm(p, x_in, A, TOK, ps, ones_bf, ones_f, "l1")

    stage = [p.sbuf(f"stage{j}", [128, 8192], BF16) for j in range(2)]
    bounce = p.sbuf("wpbounce", [128, 2 * 4096], BF16)
    bslots = [bounce[:, 0:4096], bounce[:, 4096:8192]]
    specs = [(w_in[n].ap(), wbf[n], wT[n], *w_shapes[n]) for n in w_names]
    cast_m, tr_m, wbs = emit_weight_prep(p, specs, "l1", bslots, piece=4096)
    prep_all = (wbs, wbs.k - 1)
    prep_thru = {n: (wbs, tr_m[j]) for j, n in enumerate(w_names)}

    comb_b = p.lnmu
    comb_mark = emit_router_comb(p, A, TOK, router_in.ap(), ps, ones_f, comb_b,
                                 "l1", xln_mark, ident_f, ident_ready=(ild, 0))

    tmp = p.sbuf("gu_tmp", [128, 2 * 512], BF16)
    htile = p.sbuf("gu_ht", [128, 2 * 512], BF16)
    hr_ready = emit_ffn_gu(p, A, TOK, wT["r_gate"], wT["r_up"], hr_dram, 1024,
                           ps, "rgu", prep_thru["r_up"], xln_mark, stage, tmp,
                           htile)

    hrslab = (p.lnsq if 2 * min(2048, TOK) >= 8 * 512
              else p.sbuf("hrslab", [128, 8 * 512], BF16))
    rwcol = [p.sbuf(f"rwcol{j}", [128, 8 * 128], BF16) for j in range(2)]
    scr = (p.lnx if 2 * min(2048, TOK) >= 3072
           else p.sbuf("scr", [128, 3072], F32))
    yr_ready = emit_ffn_down(p, hr_dram, 1024, TOK, wT["r_down"], ps, "rdn",
                             hr_ready, prep_thru["r_down"], hrslab, rwcol,
                             "yr", scr, comb_b=comb_b, yr_dram=yr_dram,
                             ev_pre_wait=comb_mark)

    h_ready = emit_ffn_gu(p, A, TOK, wT["sh_gate"], wT["sh_up"], h_dram, 8192,
                          ps, "sgu", prep_thru["sh_up"], xln_mark, stage, tmp,
                          htile)

    hs_big = (A if NCH * TOK >= 64 * 512
              else p.sbuf("hslab_big", [128, 64 * 512], BF16))
    ost, ost_u = emit_ffn_down(p, h_dram, 8192, TOK, wT["sh_down"], ps, "sdn",
                               h_ready, prep_thru["sh_down"], hs_big, stage,
                               "res", scr,
                               yr_dram=yr_dram, x_dram=x_in, xout_dram=x1_out,
                               ev_pre_wait=comb_mark, st_pre=yr_ready)

    @p.block.gpsimd
    def _(g):
        ost.wait_all(g)
    return p


BASE_LR_INV = math.log(math.expm1(0.001))
CV_EPS_L2 = 5


def hview(ap2d, d, j0, j1):
    return ap2d.rearrange("p (h d) -> p h d", d=d)[:, :, j0:j1]


def build_launch2(TOK=2048):
    p = Prog()
    ps = p.psum_pool()
    p.evq = Sem(p, "evq")

    x_in = p.dram("x_in", [H, TOK], F32, kind="ExternalInput")
    qkv_w = p.dram("qkv_w", [3 * H, H], F32, kind="ExternalInput")
    bnames = ["qnb", "knb", "s0b", "s1b", "o0b", "o1b"]
    bcast_in = {n: p.dram(f"{n}_in", [128, H], F32, kind="ExternalInput")
                for n in bnames}
    lrb_in = p.dram("lrb_in", [128, 12], F32, kind="ExternalInput")
    lr_w = p.dram("lr_w", [16, H], F32, kind="ExternalInput")
    ctab = p.dram("ctab", [TOK, 4 * 1024], BF16, kind="ExternalInput")

    onames = ["qa", "ka", "va", "fq", "fk", "fv"]
    outs = {n: p.dram(f"{n}_out", [TOK, H], BF16, kind="ExternalOutput")
            for n in onames}
    lr_out = p.dram("lr_out", [TOK, 12], F32, kind="ExternalOutput")

    qkvbf = p.dram("qkvbf", [3 * H, H], BF16)
    qkvT = p.dram("qkvT", [H, 3 * H], BF16)

    ones_bf, ones_f, _ = emit_consts(p)

    @p.block.vector
    def _(v):
        v.memset(p.cvals[:, CV_EPS_L2:CV_EPS_L2 + 1], 1e-12)

    lrbb = p.sbuf("lrbb", [128, 12], F32)
    lrT16 = p.sbuf("lrT16", [128, NCH * 16], BF16)
    lrwbf = p.dram("lrwbf", [16, H], BF16)

    ild = p.stream("identld", 1)

    @p.block.gpsimd
    def _(g):
        ild.issue(g.dma_start(lrwbf.ap(), lr_w.ap()))
        ild.issue(g.dma_start(lrbb[:], lrb_in.ap()))
    rows_ld = ild.k - 1

    ltr = p.stream("lrtr", 1)

    @p.block.sync
    def _(s):
        ild.wait_all(s, rows_ld)
        for c in range(NCH):
            ltr.issue(s.dma_start(lrT16[:, c * 16:(c + 1) * 16],
                                  lrwbf.ap()[:, c * 128:(c + 1) * 128],
                                  transpose=True))
    lrt_ld = ltr.k - 1

    @p.block.vector
    def _(v):
        ild.wait_all(v, rows_ld)
        p.x1.inc(v.tensor_scalar_add(lrbb[:], lrbb[:], BASE_LR_INV))
    lrbb_mark = p.x1.n

    xln_d = p.dram("xln_d", [H, TOK], BF16)
    Gln = min(1024, TOK)
    xm = 0
    for t0 in range(0, TOK, Gln):
        xm = emit_ln_fm(p, x_in, None, TOK, ps, ones_bf, ones_f, "l2", t0=t0,
                        G=Gln, entry_ev=xm, xln_dram=xln_d)
    xln_mark = xm
    xst_all = p.streams["lnxst"]
    A_grp = [p.sbuf(f"Agrp{j}", [128, NCH * 512], BF16) for j in range(2)]
    astr = p.stream("agld", 2)

    bt = {n: p.sbuf(f"bt_{n}", [128, H], BF16) for n in bnames}
    bstr = p.stream("bcld", 1)

    @p.block.gpsimd
    def _(g):
        for n in bnames:
            bstr.issue(g.dma_start(bt[n][:], bcast_in[n].ap()))
    bts_ld = bstr.k - 1

    stage = [p.sbuf(f"stage{j}", [128, 8192], BF16) for j in range(2)]
    cast_m, tr_m, wbs = emit_weight_prep(
        p, [(qkv_w.ap(), qkvbf, qkvT, 3 * H, H)], "l2", stage)
    prep_all = (wbs, wbs.k - 1)

    qg = p.sbuf("qg", [128, 4 * 6144], BF16)
    ctile = p.sbuf("ctile", [128, 4 * 1024], BF16)
    dvq = p.sbuf("dvq", [128, H], BF16)
    dvk = p.sbuf("dvk", [128, H], BF16)
    dv3 = p.sbuf("dv3", [128, H], BF16)
    dv4 = p.sbuf("dv4", [128, H], BF16)
    qa_st = p.sbuf("qa_st", [128, H], BF16)
    dcol = p.sbuf("dcol", [128, 16], F32)
    lrs = p.sbuf("lrs", [128, 48], F32)

    wstr = p.stream("gusl", 2)
    cstr = p.stream("ctabld", 1)
    st_s = {n: p.stream(f"st_{n}", 1) for n in onames + ["lr"]}

    ngrp = TOK // 512
    mm0 = p.mm.n
    evq0 = p.evq.n
    wld_use = {}
    lrmm = {}
    M = {}

    ag_use = {}

    @p.block.sync
    def _(s):
        wrs, wru = prep_all
        wrs.wait_all(s, wru)
        xst_all.wait_all(s)
        k = 0
        for g in range(ngrp):
            if g >= 2:
                s.wait_ge(p.mm.h, mm0 + (g - 1) * 52)
            ag_use[g] = slab_load_split(
                astr, g % 2, s, A_grp[g % 2],
                xln_d.ap()[:, g * 512:(g + 1) * 512], NCH, 512)
            for os_ in range(12):
                if k >= 2:
                    kp = k - 2
                    s.wait_ge(p.mm.h,
                              mm0 + (kp // 12) * 52 + (kp % 12 + 1) * 4)
                wld_use[(g, os_)] = slab_load_split(
                    wstr, k % 2, s, stage[k % 2],
                    qkvT.ap()[:, os_ * 512:(os_ + 1) * 512], NCH, 512)
                k += 1

    @p.block.tensor
    def _(t):
        ltr.wait_all(t, lrt_ld)
        k = 0
        for g in range(ngrp):
            astr.wait(t, ag_use[g])
            for os_ in range(12):
                wstr.wait(t, wld_use[(g, os_)])
                for tb in range(4):
                    idx = os_ * 4 + tb
                    if g * 52 + idx >= 2:
                        t.wait_ge(p.evq.h, evq0 + g * 52 + idx - 1)
                    bank = ps[idx % 2]
                    last = None
                    for kc in range(NCH):
                        last = t.matmul(
                            bank[:, :],
                            A_grp[g % 2][:, kc * 512 + tb * 128:
                                         kc * 512 + (tb + 1) * 128],
                            stage[k % 2][:, kc * 512:(kc + 1) * 512],
                            start=(kc == 0), stop=(kc == NCH - 1))
                    p.mm.inc(last)
                k += 1
            for tb in range(4):
                t.wait_ge(p.evq.h, evq0 + g * 52 + 48 + tb)
                last = None
                for kc in range(NCH):
                    last = t.matmul(
                        ps[4][:, 0:12],
                        A_grp[g % 2][:, kc * 512 + tb * 128:
                                     kc * 512 + (tb + 1) * 128],
                        lrT16[:, kc * 16:kc * 16 + 12],
                        start=(kc == 0), stop=(kc == NCH - 1))
                lrmm[(g, tb)] = p.mm.inc(last)

    @p.block.gpsimd
    def _(g_):
        cstr.issue(g_.dma_start(ctile[:], ctab.ap()[0:128, :]))

    for g in range(ngrp):
        @p.block.vector
        def _(v, g=g):
            for os_ in range(12):
                for tb in range(4):
                    idx = os_ * 4 + tb
                    v.wait_ge(p.mm.h, mm0 + g * 52 + idx + 1)
                    if g > 0 and idx == 0:
                        v.wait_ge(p.ac.h, M[("fv", g - 1, 3)])
                        st_s["va"].wait(v, st_s["va"].k - 1)
                        st_s["fv"].wait(v, st_s["fv"].k - 1)
                        st_s["ka"].wait(v, st_s["ka"].k - 1)
                    bank = ps[idx % 2]
                    p.evq.inc(v.tensor_copy(
                        qg[:, tb * 6144 + os_ * 512:tb * 6144 + (os_ + 1) * 512],
                        bank[:, :]))

        for tb in range(4):
            q_sl = qg[:, tb * 6144:tb * 6144 + 2048]
            k_sl = qg[:, tb * 6144 + 2048:tb * 6144 + 4096]
            v_sl = qg[:, tb * 6144 + 4096:tb * 6144 + 6144]

            @p.block.scalar
            def _(sc, g=g, tb=tb, q_sl=q_sl, k_sl=k_sl):
                sc.wait_ge(p.evq.h, evq0 + g * 52 + 48)
                p.chain(sc, sc.activation(dv3[:], q_sl, AF.Square,
                                          accum_out=dcol[:, 0:1]), "s")
                p.chain(sc, sc.activation(dv3[:], k_sl, AF.Square,
                                          accum_out=dcol[:, 1:2]), "s")
                p.chain(sc, sc.activation(
                    dcol[:, 2:4], dcol[:, 0:2], AF.Ln, scale=1.0 / H,
                    bias=p.cvals[:, CV_EPS_RMS:CV_EPS_RMS + 1]), "s")
                p.ac.inc(sc.activation(dcol[:, 4:6], dcol[:, 2:4], AF.Exp,
                                       scale=-0.5))
            M[("rstd", g, tb)] = p.ac.n

            @p.block.vector
            def _(v, g=g, tb=tb, q_sl=q_sl, k_sl=k_sl, k_sl_v=k_sl):
                v.wait_ge(p.ac.h, M[("rstd", g, tb)])
                bstr.wait_all(v, bts_ld)
                if g + tb > 0:
                    st_s["fq"].wait(v, st_s["fq"].k - 1)
                    st_s["fk"].wait(v, st_s["fk"].k - 1)
                p.chain(v, v.scalar_tensor_tensor(dvq[:], q_sl, dcol[:, 4:5],
                                                  bt["qnb"][:], ALU.mult,
                                                  ALU.mult))
                p.chain(v, v.scalar_tensor_tensor(dvk[:], k_sl, dcol[:, 5:6],
                                                  bt["knb"][:], ALU.mult,
                                                  ALU.mult))
                cstr.wait(v, 4 * g + tb)
                if g + tb > 0:
                    st_s["qa"].wait(v, st_s["qa"].k - 1)
                    st_s["ka"].wait(v, st_s["ka"].k - 1)
                T = hview(dv3[:, 0:1024], 64, 0, 64)
                for (src, cof, dst) in ((dvq, 0, qa_st), (dvk, 2048, k_sl_v)):
                    x1 = hview(src[:], 128, 0, 64)
                    x2 = hview(src[:], 128, 64, 128)
                    o1 = hview(dst[:], 128, 0, 64)
                    o2 = hview(dst[:], 128, 64, 128)
                    cc = hview(ctile[:, cof:cof + 1024], 64, 0, 64)
                    ss = hview(ctile[:, cof + 1024:cof + 2048], 64, 0, 64)
                    p.chain(v, v.tensor_mul(o1, x1, cc))
                    p.chain(v, v.tensor_mul(T, x2, ss))
                    p.chain(v, v.tensor_sub(o1, o1, T))
                    p.chain(v, v.tensor_mul(o2, x2, cc))
                    p.chain(v, v.tensor_mul(T, x1, ss))
                    p.chain(v, v.tensor_add(o2, o2, T))
                p.ev.inc(v.engine_nop())
                M[("rope", g, tb)] = p.chv.n
                p.chain(v, v.tensor_mul(dv3[:], dvq[:], bt["s0b"][:]))
                p.ev.inc(v.tensor_add(dv4[:], dv3[:], bt["o0b"][:]))
            M[("ropeev", g, tb)] = p.ev.n - 1
            M[("fqpre", g, tb)] = p.ev.n

            @p.block.scalar
            def _(sc, g=g, tb=tb):
                sc.wait_ge(p.ev.h, M[("fqpre", g, tb)])
                p.chain(sc, sc.activation(dv3[:], dv4[:], AF.Silu), "s")
                for j in range(4):
                    p.chain(sc, sc.activation(
                        dv4[:, j * 512:(j + 1) * 512],
                        dv3[:, j * 512:(j + 1) * 512], AF.Square,
                        accum_out=dcol[:, 8 + j:9 + j]), "s")
                p.chain(sc, sc.activation(
                    dcol[:, 12:16], dcol[:, 8:12], AF.Ln,
                    bias=p.cvals[:, CV_EPS_L2:CV_EPS_L2 + 1]), "s")
                p.ac.inc(sc.activation(dcol[:, 8:12], dcol[:, 12:16], AF.Exp,
                                       scale=-0.5))
            M[("fqr", g, tb)] = p.ac.n

            @p.block.vector
            def _(v, g=g, tb=tb):
                v.wait_ge(p.ac.h, M[("fqr", g, tb)])
                for j in range(4):
                    p.chain(v, v.tensor_scalar_mul(
                        dvq[:, j * 512:(j + 1) * 512],
                        dv3[:, j * 512:(j + 1) * 512], dcol[:, 8 + j:9 + j]))
                p.ev.inc(v.engine_nop())
                M[("fq", g, tb)] = p.ev.n
                p.chain(v, v.tensor_mul(dv4[:], dvk[:], bt["s1b"][:]))
                p.ev.inc(v.tensor_add(dv3[:], dv4[:], bt["o1b"][:]))
            M[("fkpre", g, tb)] = p.ev.n

            @p.block.scalar
            def _(sc, g=g, tb=tb, v_sl=v_sl, q_sl_=q_sl):
                sc.wait_ge(p.ev.h, M[("fkpre", g, tb)])
                p.chain(sc, sc.activation(dv4[:], dv3[:], AF.Silu), "s")
                for j in range(4):
                    p.chain(sc, sc.activation(
                        dv3[:, j * 512:(j + 1) * 512],
                        dv4[:, j * 512:(j + 1) * 512], AF.Square,
                        accum_out=dcol[:, 8 + j:9 + j]), "s")
                p.chain(sc, sc.activation(
                    dcol[:, 12:16], dcol[:, 8:12], AF.Ln,
                    bias=p.cvals[:, CV_EPS_L2:CV_EPS_L2 + 1]), "s")
                p.chain(sc, sc.activation(dcol[:, 8:12], dcol[:, 12:16],
                                          AF.Exp, scale=-0.5), "s")
                if g + tb > 0:
                    st_s["fv"].wait(sc, st_s["fv"].k - 1)
                p.ac.inc(sc.activation(q_sl_, v_sl, AF.Silu))
            M[("fkr", g, tb)] = p.ac.n
            M[("fv", g, tb)] = p.ac.n

            @p.block.vector
            def _(v, g=g, tb=tb):
                v.wait_ge(p.ac.h, M[("fkr", g, tb)])
                for j in range(4):
                    p.chain(v, v.tensor_scalar_mul(
                        dvk[:, j * 512:(j + 1) * 512],
                        dv4[:, j * 512:(j + 1) * 512], dcol[:, 8 + j:9 + j]))
                p.ev.inc(v.engine_nop())
                M[("fk", g, tb)] = p.ev.n
                v.wait_ge(p.mm.h, lrmm[(g, tb)])
                if g + tb > 0:
                    st_s["lr"].wait(v, st_s["lr"].k - 1)
                p.evq.inc(v.tensor_add(lrs[:, 0:12], ps[4][:, 0:12],
                                       lrbb[:]))
                v.wait_ge(p.evq.h, p.evq.n)
                p.chain(v, v.tensor_scalar_max(lrs[:, 16:28], lrs[:, 0:12],
                                               0.0))
                p.ev.inc(v.engine_nop())
            M[("lrm", g, tb)] = p.ev.n

            @p.block.scalar
            def _(sc, g=g, tb=tb):
                sc.wait_ge(p.ev.h, M[("lrm", g, tb)])
                p.chain(sc, sc.activation(lrs[:, 32:44], lrs[:, 0:12], AF.Abs),
                        "s")
                p.chain(sc, sc.activation(lrs[:, 0:12], lrs[:, 32:44], AF.Exp,
                                          scale=-1.0), "s")
                p.ac.inc(sc.activation(lrs[:, 32:44], lrs[:, 0:12], AF.Ln,
                                       bias=p.cvals[:, CV_ONE:CV_ONE + 1]))
            M[("lrl", g, tb)] = p.ac.n

            @p.block.vector
            def _(v, g=g, tb=tb):
                v.wait_ge(p.ac.h, M[("lrl", g, tb)])
                p.ev.inc(v.tensor_add(lrs[:, 0:12], lrs[:, 16:28],
                                      lrs[:, 32:44]))
            M[("lr", g, tb)] = p.ev.n

            @p.block.gpsimd
            def _(g_, g=g, tb=tb, v_sl=v_sl):
                r0 = g * 512 + tb * 128
                g_.wait_ge(p.ev.h, M[("ropeev", g, tb)])
                st_s["qa"].issue(g_.dma_start(outs["qa"].ap()[r0:r0 + 128, :],
                                              qa_st[:]))
                st_s["ka"].issue(g_.dma_start(outs["ka"].ap()[r0:r0 + 128, :],
                                              qg[:, tb * 6144 + 2048:
                                                 tb * 6144 + 4096]))
                st_s["va"].issue(g_.dma_start(outs["va"].ap()[r0:r0 + 128, :],
                                              v_sl))
                g_.wait_ge(p.ev.h, M[("fq", g, tb)])
                st_s["fq"].issue(g_.dma_start(outs["fq"].ap()[r0:r0 + 128, :],
                                              dvq[:]))
                g_.wait_ge(p.ev.h, M[("fk", g, tb)])
                g_.wait_ge(p.ac.h, M[("fv", g, tb)])
                st_s["fk"].issue(g_.dma_start(outs["fk"].ap()[r0:r0 + 128, :],
                                              dvk[:]))
                st_s["fv"].issue(g_.dma_start(outs["fv"].ap()[r0:r0 + 128, :],
                                              qg[:, tb * 6144:tb * 6144 + 2048]))
                g_.wait_ge(p.ev.h, M[("lr", g, tb)])
                st_s["lr"].issue(g_.dma_start(lr_out.ap()[r0:r0 + 128, :],
                                              lrs[:, 0:12]))
                if (g, tb) != (ngrp - 1, 3):
                    nt_ = (g, tb + 1) if tb < 3 else (g + 1, 0)
                    g_.wait_ge(p.chv.h, M[("rope", g, tb)])
                    cstr.issue(g_.dma_start(
                        ctile[:], ctab.ap()[nt_[0] * 512 + nt_[1] * 128:
                                            nt_[0] * 512 + (nt_[1] + 1) * 128, :]))

    @p.block.gpsimd
    def _(g_):
        for n in onames + ["lr"]:
            st_s[n].wait_all(g_)
    return p


def build_launch3(S=8192, QTOK=2048, CHUNK=512):
    FD = 512
    p = Prog()
    ps = [p.ctx.enter_context(p.nc.psum_tensor(f"pp{i}", [128, 512], F32))
          for i in range(6)]
    pb = [p.ctx.enter_context(p.nc.psum_tensor(f"pbt{i}", [128, 128], BF16))
          for i in range(2)]

    fq_u = p.dram("fq_u", [S, FD], BF16, kind="ExternalInput")
    fk_u = p.dram("fk_u", [S, FD], BF16, kind="ExternalInput")
    fv_u = p.dram("fv_u", [S, FD], BF16, kind="ExternalInput")
    lr_u = p.dram("lr_u", [S, 4], F32, kind="ExternalInput")
    w0_in = p.dram("w0_in", [FD, FD], F32, kind="ExternalInput")
    w1_in = p.dram("w1_in", [FD, FD], F32, kind="ExternalInput")
    w2_in = p.dram("w2_in", [FD, FD], F32, kind="ExternalInput")
    tttnb_in = p.dram("tttnb_in", [128, FD], F32, kind="ExternalInput")
    ident_in = p.dram("ident_in", [128, 128], F32, kind="ExternalInput")
    qa_own = p.dram("qa_own", [QTOK, H], BF16, kind="ExternalInput")
    ka_ctx = p.dram("ka_ctx", [2 * QTOK, H], BF16, kind="ExternalInput")
    va_ctx = p.dram("va_ctx", [2 * QTOK, H], BF16, kind="ExternalInput")
    lo_in = p.dram("lo_in", [128, 1], F32, kind="ExternalInput")

    ottt_out = p.dram("ottt_out", [S, FD], BF16, kind="ExternalOutput")
    attn_out = p.dram("attn_out", [QTOK, H], BF16, kind="ExternalOutput")

    ones_bf, ones_f, _ = emit_consts(p)
    ident_f = p.sbuf("ident_f", [128, 128], F32)
    ident_b = p.sbuf("ident_b", [128, 128], BF16)
    tttnb = p.sbuf("tttnb", [128, FD], F32)
    lo_sb = p.sbuf("lo_sb", [128, 4], F32)

    ild = p.stream("identld", 1)

    @p.block.gpsimd
    def _(g):
        ild.issue(g.dma_start(ident_f[:], ident_in.ap()))
        ild.issue(g.dma_start(ident_b[:], ident_in.ap()))
        ild.issue(g.dma_start(tttnb[:], tttnb_in.ap()))
        ild.issue(g.dma_start(lo_sb[:, 0:1], lo_in.ap()))
    consts_ld = ild.k - 1

    w0f = p.sbuf("w0f", [128, 4 * FD], F32)
    w2f = p.sbuf("w2f", [128, 4 * FD], F32)
    w1f_ed = p.sbuf("w1f_ed", [128, 4 * FD], F32)
    w1f_de = p.sbuf("w1f_de", [128, 4 * FD], F32)
    w0b = p.sbuf("w0b", [128, 4 * FD], BF16)
    w2b = p.sbuf("w2b", [128, 4 * FD], BF16)
    w1b_ed = p.sbuf("w1b_ed", [128, 4 * FD], BF16)
    w1b_de = p.sbuf("w1b_de", [128, 4 * FD], BF16)
    cbuf = [p.sbuf(f"cb{j}", [128, 5 * 4 * FD], BF16) for j in range(2)]
    lrc = [p.sbuf(f"lrc{j}", [128, 4 * 4], F32) for j in range(2)]
    hqT = p.sbuf("hqT", [128, 4 * FD], BF16)
    gate = p.sbuf("gate", [128, 4 * FD], BF16)
    sig = p.sbuf("sig", [128, 4 * FD], BF16)
    g1sb = p.sbuf("g1sb", [128, 4 * FD], BF16)
    g2sb = p.sbuf("g2sb", [128, 4 * FD], BF16)
    dhsb = p.sbuf("dhsb", [128, 4 * FD], BF16)
    hid = p.sbuf("hid", [128, 4 * FD], BF16)
    fbuf = p.sbuf("fbuf", [128, 4 * FD], BF16)
    d1l = p.sbuf("d1l", [128, 4 * FD], BF16)
    d2l = p.sbuf("d2l", [128, 4 * FD], BF16)
    l1v = p.sbuf("l1v", [128, 4 * FD], BF16)
    tsb = p.sbuf("tsb", [128, FD], BF16)
    osb = p.sbuf("osb", [128, 2 * FD], BF16)
    dcol = p.sbuf("dcol", [128, 8], F32)

    NC_ = S // CHUNK

    wstage = p.sbuf("wstage", [128, 4 * FD], F32)
    NC_ = S // CHUNK

    wld = p.stream("wld", 1)
    ost = p.stream("ottt_st", 2)
    cld = p.stream("chld", 2)

    w1ld = p.stream("w1ld", 1)

    @p.block.sync
    def _(s):
        w1ld.issue(s.dma_start(w1f_de[:].rearrange("p (a c) -> p a c", a=4),
                               rearr_rows(w1_in.ap())))
    w1de_ld = w1ld.k - 1

    for mi, (win, carry, bcopy) in enumerate(
            [(w0_in, w0f, w0b), (w2_in, w2f, w2b), (w1_in, w1f_ed, w1b_ed)]):
        ev_tr0 = p.ev.n

        @p.block.sync
        def _(s, win=win, mi=mi):
            if mi == 0:
                ild.wait_all(s, consts_ld)
            else:
                s.wait_ge(p.ev.h, ev_tr0)
            wld.issue(s.dma_start(wstage[:].rearrange("p (a c) -> p a c", a=4),
                                  rearr_rows(win.ap())))
        ldm = wld.k - 1
        trm = []

        @p.block.tensor
        def _(t, ldm=ldm):
            wld.wait(t, ldm)
            ild.wait_all(t, consts_ld)
            for blk in range(16):
                i_, j_ = blk // 4, blk % 4
                if blk >= 2:
                    t.wait_ge(p.ev.h, ev_tr0 + blk - 1)
                trm.append(p.mm.inc(t.transpose(
                    ps[blk % 2][:, 0:128],
                    wstage[:, i_ * 512 + j_ * 128:i_ * 512 + (j_ + 1) * 128],
                    ident_f[:])))

        @p.block.vector
        def _(v, carry=carry, bcopy=bcopy, trm=trm):
            for blk in range(16):
                i_, j_ = blk // 4, blk % 4
                v.wait_ge(p.mm.h, trm[blk])
                dst = carry[:, j_ * 512 + i_ * 128:j_ * 512 + (i_ + 1) * 128]
                p.chain(v, v.tensor_copy(dst, ps[blk % 2][:, 0:128]))
                p.ev.inc(v.tensor_copy(
                    bcopy[:, j_ * 512 + i_ * 128:j_ * 512 + (i_ + 1) * 128],
                    dst))

    @p.block.vector
    def _(v):
        w1ld.wait(v, w1de_ld)
        p.ev.inc(v.tensor_copy(w1b_de[:], w1f_de[:]))
    init_done = p.ev.n

    ch_done_mm = {}
    ch_ld = {}

    def emit_chunk_loads(c):
        @p.block.sync
        def _(s):
            sl = c % 2
            if c >= 2:
                s.wait_ge(p.mm.h, ch_done_mm[c - 2])
            base = c * CHUNK
            for j, src_t in enumerate((fq_u, fk_u, fv_u)):
                for dch in range(4):
                    cld.issue_slot(sl, s.dma_start(
                        cbuf[sl][:, j * 2048 + dch * 512:
                                 j * 2048 + (dch + 1) * 512],
                        src_t.ap()[base:base + CHUNK,
                                   dch * 128:(dch + 1) * 128],
                        transpose=True))
            cld.issue_slot(sl, s.dma_start(
                cbuf[sl][:, 3 * 2048:4 * 2048].rearrange("p (a c) -> p a c", a=4),
                rearr_rows(fk_u.ap()[base:base + CHUNK, :])))
            cld.issue_slot(sl, s.dma_start(
                cbuf[sl][:, 4 * 2048:5 * 2048].rearrange("p (a c) -> p a c", a=4),
                rearr_rows(fv_u.ap()[base:base + CHUNK, :])))
            cld.issue_slot(sl, s.dma_start(
                lrc[sl][:].rearrange("p (a c) -> p a c", a=4),
                rearr_rows(lr_u.ap()[base:base + CHUNK, :])))
        return cld.k - 1

    ch_ld[0] = emit_chunk_loads(0)
    if NC_ > 1:
        ch_ld[1] = emit_chunk_loads(1)

    bfm = {-1: init_done}
    for c in range(NC_):
        sl = c % 2
        qcT = cbuf[sl][:, 0:2048]
        kcT = cbuf[sl][:, 2048:4096]
        vcT = cbuf[sl][:, 4096:6144]
        kc = cbuf[sl][:, 6144:8192]
        vc = cbuf[sl][:, 8192:10240]
        lr3 = lrc[sl]

        mmA0, acA0, evA0 = p.mm.n, p.ac.n, p.ev.n

        @p.block.tensor
        def _(t, c=c, qcT=qcT, mmA0=mmA0, acA0=acA0, evA0=evA0):
            cld.wait_all(t, ch_ld[c])
            t.wait_ge(p.ev.h, bfm[c - 1])
            for eb in range(4):
                if eb >= 1:
                    t.wait_ge(p.ac.h, acA0 + eb)
                    t.wait_ge(p.ev.h, evA0 + eb)
                for wmat, bank in ((w0b, ps[0]), (w2b, ps[1])):
                    last = None
                    for j in range(4):
                        last = t.matmul(
                            bank[:, :],
                            wmat[:, j * 512 + eb * 128:j * 512 + (eb + 1) * 128],
                            qcT[:, j * 512:(j + 1) * 512],
                            start=(j == 0), stop=(j == 3))
                    p.mm.inc(last)

        @p.block.scalar
        def _(sc, mmA0=mmA0, evA0=evA0):
            for eb in range(4):
                sc.wait_ge(p.mm.h, mmA0 + 2 * eb + 1)
                if eb >= 1:
                    sc.wait_ge(p.ev.h, evA0 + eb)
                p.ac.inc(sc.activation(tsb[:], ps[0][:, :], AF.Silu))

        @p.block.vector
        def _(v, mmA0=mmA0, acA0=acA0):
            for eb in range(4):
                v.wait_ge(p.mm.h, mmA0 + 2 * eb + 2)
                v.wait_ge(p.ac.h, acA0 + eb + 1)
                p.ev.inc(v.tensor_mul(hqT[:, eb * 512:(eb + 1) * 512],
                                      tsb[:], ps[1][:, :]))

        mmG0, acG0, evG0 = p.mm.n, p.ac.n, p.ev.n

        @p.block.tensor
        def _(t, kcT=kcT, vcT=vcT, acG0=acG0, evG0=evG0):
            for cb in range(4):
                if cb >= 1:
                    t.wait_ge(p.ac.h, acG0 + 2 * cb)
                    t.wait_ge(p.ev.h, evG0 + 3 * cb)
                for lmat, rmat, bank in ((kcT, w0b, ps[2]), (kcT, w2b, ps[3]),
                                         (vcT, w1b_de, ps[4])):
                    last = None
                    for j in range(4):
                        last = t.matmul(
                            bank[:, :],
                            lmat[:, j * 512 + cb * 128:j * 512 + (cb + 1) * 128],
                            rmat[:, j * 512:(j + 1) * 512],
                            start=(j == 0), stop=(j == 3))
                    p.mm.inc(last)

        @p.block.scalar
        def _(sc, mmG0=mmG0):
            for cb in range(4):
                sc.wait_ge(p.mm.h, mmG0 + 3 * cb + 1)
                p.ac.inc(sc.activation(gate[:, cb * 512:(cb + 1) * 512],
                                       ps[2][:, :], AF.Silu))
                p.ac.inc(sc.activation(sig[:, cb * 512:(cb + 1) * 512],
                                       ps[2][:, :], AF.Sigmoid))

        @p.block.vector
        def _(v, mmG0=mmG0, acG0=acG0):
            for cb in range(4):
                v.wait_ge(p.mm.h, mmG0 + 3 * cb + 3)
                v.wait_ge(p.ac.h, acG0 + 2 * cb + 2)
                p.ev.inc(v.tensor_copy(g1sb[:, cb * 512:(cb + 1) * 512],
                                       ps[2][:, :]))
                p.ev.inc(v.tensor_copy(g2sb[:, cb * 512:(cb + 1) * 512],
                                       ps[3][:, :]))
                p.ev.inc(v.tensor_copy(dhsb[:, cb * 512:(cb + 1) * 512],
                                       ps[4][:, :]))

        @p.block.vector
        def _(v, vc=vc, lr3=lr3, acG0=acG0, evG0=evG0):
            v.wait_ge(p.ac.h, acG0 + 8)
            v.wait_ge(p.ev.h, evG0 + 12)
            p.chain(v, v.tensor_mul(hid[:], gate[:], g2sb[:]))
            p.chain(v, v.tensor_scalar(fbuf[:], sig[:], -1.0, 1.0,
                                       ALU.mult, ALU.add))
            p.chain(v, v.tensor_mul(fbuf[:], g1sb[:], fbuf[:]))
            p.chain(v, v.tensor_scalar_add(fbuf[:], fbuf[:], 1.0))
            p.chain(v, v.tensor_mul(fbuf[:], sig[:], fbuf[:]))
            p.chain(v, v.tensor_mul(d1l[:], dhsb[:], g2sb[:]))
            for cb in range(4):
                slc = slice(cb * 512, (cb + 1) * 512)
                p.chain(v, v.scalar_tensor_tensor(
                    d1l[:, slc], d1l[:, slc], lr3[:, cb * 4:cb * 4 + 1],
                    fbuf[:, slc], ALU.mult, ALU.mult))
                p.chain(v, v.scalar_tensor_tensor(
                    d2l[:, slc], dhsb[:, slc], lr3[:, cb * 4 + 2:cb * 4 + 3],
                    gate[:, slc], ALU.mult, ALU.mult))
                p.chain(v, v.tensor_scalar_mul(
                    l1v[:, slc], vc[:, slc], lr3[:, cb * 4 + 1:cb * 4 + 2]))
            p.ev.inc(v.engine_nop())
        e_done = p.ev.n

        mmO0, acO0, evO0 = p.mm.n, p.ac.n, p.ev.n
        ostk0 = ost.k

        @p.block.tensor
        def _(t, acO0=acO0, evO0=evO0):
            for cb in range(4):
                if cb >= 1:
                    t.wait_ge(p.ac.h, acO0 + cb)
                    t.wait_ge(p.ev.h, evO0 + cb)
                last = None
                for j in range(4):
                    last = t.matmul(
                        ps[5][:, :],
                        hqT[:, j * 512 + cb * 128:j * 512 + (cb + 1) * 128],
                        w1b_ed[:, j * 512:(j + 1) * 512],
                        start=(j == 0), stop=(j == 3))
                p.mm.inc(last)

        @p.block.scalar
        def _(sc, mmO0=mmO0):
            for cb in range(4):
                sc.wait_ge(p.mm.h, mmO0 + cb + 1)
                p.chain(sc, sc.activation(tsb[:], ps[5][:, :], AF.Square,
                                          accum_out=dcol[:, 0:1]), "s")
                p.chain(sc, sc.activation(
                    dcol[:, 1:2], dcol[:, 0:1], AF.Ln, scale=1.0 / FD,
                    bias=p.cvals[:, CV_EPS_RMS:CV_EPS_RMS + 1]), "s")
                p.ac.inc(sc.activation(dcol[:, 2:3], dcol[:, 1:2], AF.Exp,
                                       scale=-0.5))

        @p.block.vector
        def _(v, acO0=acO0, ostk0=ostk0):
            for cb in range(4):
                v.wait_ge(p.ac.h, acO0 + cb + 1)
                if ostk0 + cb >= 2:
                    ost.wait_idx(v, ostk0 + cb - 2)
                p.ev.inc(v.scalar_tensor_tensor(
                    osb[:, (cb % 2) * FD:(cb % 2 + 1) * FD], ps[5][:, :],
                    dcol[:, 2:3], tttnb[:], ALU.mult, ALU.mult))

        @p.block.gpsimd
        def _(g_, c=c, evO0=evO0):
            for cb in range(4):
                g_.wait_ge(p.ev.h, evO0 + cb + 1)
                r0 = c * CHUNK + cb * 128
                ost.issue(g_.dma_start(
                    ottt_out.ap()[r0:r0 + 128, :],
                    osb[:, (cb % 2) * FD:(cb % 2 + 1) * FD]))

        mmU0, evU0 = p.mm.n, p.ev.n
        upd = [
            (hid, l1v, w1f_ed, w1b_ed),
            (l1v, hid, w1f_de, w1b_de),
            (kc, d1l, w0f, w0b),
            (kc, d2l, w2f, w2b),
        ]

        @p.block.tensor
        def _(t, upd=upd, e_done=e_done, evO0=evO0, evU0=evU0):
            t.wait_ge(p.ev.h, e_done)
            t.wait_ge(p.ev.h, evO0 + 4)
            u = 0
            for (lmat, rmat, cf, cbw) in upd:
                for blk in range(4):
                    if u >= 1:
                        t.wait_ge(p.ev.h, evU0 + 2 * u)
                    last = None
                    for j in range(4):
                        last = t.matmul(
                            ps[5][:, :],
                            lmat[:, j * 512 + blk * 128:j * 512 + (blk + 1) * 128],
                            rmat[:, j * 512:(j + 1) * 512],
                            start=(j == 0), stop=(j == 3))
                    p.mm.inc(last)
                    u += 1

        @p.block.vector
        def _(v, upd=upd, mmU0=mmU0):
            u = 0
            for (lmat, rmat, cf, cbw) in upd:
                for blk in range(4):
                    v.wait_ge(p.mm.h, mmU0 + u + 1)
                    slc = slice(blk * 512, (blk + 1) * 512)
                    p.chain(v, v.tensor_add(cf[:, slc], cf[:, slc],
                                            ps[5][:, :]))
                    p.ev.inc(v.engine_nop())
                    p.ev.inc(v.tensor_copy(cbw[:, slc], cf[:, slc]))
                    u += 1
        bfm[c] = p.ev.n
        ch_done_mm[c] = p.mm.n
        if c + 2 < NC_:
            ch_ld[c + 2] = emit_chunk_loads(c + 2)

    ttt_done_ev = p.ev.n
    ttt_done_mm = p.mm.n

    NQB = QTOK // 128
    WTILES = {}
    for qb in range(NQB):
        lo_t = (qb * 128 + 1) // 512
        hi_t = (qb * 128 + 2048 + 127) // 512
        WTILES[qb] = (lo_t, min(hi_t, 2 * QTOK // 512 - 1))

    s_sb2 = p.sbuf("s_sb", [128, 2 * 5 * 512], F32)
    psb2 = p.sbuf("psb", [128, 2 * 5 * 512], BF16)
    pT_sb = p.sbuf("pT_sb", [128, 2 * 128], BF16)
    msk = [p.sbuf(f"msk{j}", [128, 5 * 512], BF16) for j in range(2)]
    attn_col = p.sbuf("attn_col", [128, NQB * 128], BF16)
    mcol2 = p.sbuf("mcol", [128, 32], F32)
    mcol = mcol2[:, 0:16]
    Jt = p.sbuf("Jt", [128, 512], F32)
    icol = p.sbuf("icol", [128, 4], F32)
    mbld = p.sbuf("mbld", [128, 2 * 512], F32)

    mask_d = p.dram("mask_d", [QTOK, 5 * 512], BF16)

    kld = p.stream("attnkld", 2)
    mld = p.stream("attnmld", 2)
    ast = p.stream("attnst", 2)
    mst = p.stream("maskst", 2)

    @p.block.gpsimd
    def _(g):
        g.iota(Jt[:], [[1, 512]], channel_multiplier=0,
               allow_small_or_imprecise_dtypes=True)
        p.chg.inc(g.iota(icol[:, 0:1], [[1, 1]], channel_multiplier=1,
                         allow_small_or_imprecise_dtypes=True))
        g.wait_ge(p.chg.h, p.chg.n)
    iota_m = p.chg.n

    mq_marks = {}

    @p.block.vector
    def _(v):
        v.wait_ge(p.chg.h, iota_m)
        ild.wait_all(v, consts_ld)
        for qb in range(NQB):
            lo_t, hi_t = WTILES[qb]
            for kt in range(lo_t, hi_t + 1):
                kti = kt - lo_t
                if kti == 0 and qb >= 1:
                    mst.wait_idx(v, qb - 1)
                slc = slice(kti * 512, (kti + 1) * 512)
                p.chain(v, v.tensor_scalar_add(mcol[:, 0:1], icol[:, 0:1],
                                               float(qb * 128 - kt * 512)))
                p.chain(v, v.tensor_scalar_add(mcol[:, 2:3], mcol[:, 0:1],
                                               float(2048.0)))
                p.chain(v, v.tensor_scalar_add(mcol[:, 1:2], lo_sb[:, 0:1],
                                               float(-kt * 512)))
                p.chain(v, v.tensor_max(mcol[:, 0:1], mcol[:, 0:1],
                                        mcol[:, 1:2]))
                p.chain(v, v.tensor_scalar(mbld[:, 0:512], Jt[:],
                                           mcol[:, 0:1], None, ALU.is_gt))
                p.chain(v, v.tensor_scalar(mbld[:, 512:1024], Jt[:],
                                           mcol[:, 2:3], None, ALU.is_le))
                p.chain(v, v.tensor_mul(mbld[:, 0:512], mbld[:, 0:512],
                                        mbld[:, 512:1024]))
                p.ev.inc(v.tensor_scalar(msk[0][:, slc], mbld[:, 0:512],
                                         1e9, -1e9, ALU.mult, ALU.add))
            mq_marks[qb] = p.ev.n

    @p.block.gpsimd
    def _(g):
        for qb in range(NQB):
            g.wait_ge(p.ev.h, mq_marks[qb])
            lo_t, hi_t = WTILES[qb]
            nkt = hi_t - lo_t + 1
            mst.issue(g.dma_start(
                mask_d.ap()[qb * 128:(qb + 1) * 128, 0:nkt * 512],
                msk[0][:, 0:nkt * 512]))


    kslab = cbuf
    kh_ld = {}

    def emit_head_loads(h):
        @p.block.sync
        def _(s):
            sl = h % 2
            if h < 2:
                s.wait_ge(p.mm.h, ttt_done_mm)
                s.wait_ge(p.ev.h, ttt_done_ev)
            else:
                s.wait_ge(p.mm.h, h_done_mm[h - 2])
            kld.issue_slot(sl, s.dma_start(
                kslab[sl][:, 0:2 * QTOK],
                ka_ctx.ap()[:, h * 128:(h + 1) * 128], transpose=True))
            kld.issue_slot(sl, s.dma_start(
                kslab[sl][:, 2 * QTOK:4 * QTOK].rearrange(
                    "p (a c) -> p a c", a=2 * QTOK // 128),
                rearr_rows(va_ctx.ap()[:, h * 128:(h + 1) * 128])))
            kld.issue_slot(sl, s.dma_start(
                kslab[sl][:, 4 * QTOK:5 * QTOK],
                qa_own.ap()[:, h * 128:(h + 1) * 128], transpose=True))
        return kld.k - 1

    h_done_mm = {}
    kh_ld[0] = emit_head_loads(0)
    if H // 128 > 1:
        kh_ld[1] = emit_head_loads(1)

    mt_ld = {}
    ac_attn0 = p.ac.n

    def emit_mask_loads(h):
        @p.block.gpsimd
        def _(g):
            if h == 0:
                mst.wait_all(g)
            for qb in range(NQB):
                it = h * NQB + qb
                lo_t, hi_t = WTILES[qb]
                nkt = hi_t - lo_t + 1
                if it >= 2:
                    g.wait_ge(p.ac.h, ac_attn0 + it - 1)
                mt_ld[(h, qb)] = mld.issue_slot(it % 2, g.dma_start(
                    msk[it % 2][:, 0:nkt * 512],
                    mask_d.ap()[qb * 128:(qb + 1) * 128, 0:nkt * 512]))

    it_g = 0
    sbanks = [ps[0], ps[1], ps[3], ps[4]]
    prev_evac_last = [0]
    trp_hist = [0, 0]
    prev_scale = [0]
    prev_pv_last = [0]
    for h in range(16):
        emit_mask_loads(h)
        sl = h % 2
        kaT = kslab[sl][:, 0:2 * QTOK]
        va_s = kslab[sl][:, 2 * QTOK:4 * QTOK]
        qaT = kslab[sl][:, 4 * QTOK:5 * QTOK]
        for qb in range(NQB):
            lo_t, hi_t = WTILES[qb]
            nkt = hi_t - lo_t + 1
            mmq0 = p.mm.n
            evq0 = p.ev.n
            u2 = it_g % 2
            s_sb = s_sb2[:, u2 * 2560:u2 * 2560 + 2560]
            psb = psb2[:, u2 * 2560:u2 * 2560 + 2560]
            mcol = mcol2[:, u2 * 16:u2 * 16 + 16]

            @p.block.tensor
            def _(t, h=h, qb=qb, lo_t=lo_t, nkt=nkt, kaT=kaT, qaT=qaT,
                  it_g=it_g, mmq0=mmq0, evq0=evq0):
                if qb == 0:
                    kld.wait_all(t, kh_ld[h])
                for kti in range(nkt):
                    kt = lo_t + kti
                    if kti >= 4:
                        t.wait_ge(p.ev.h, evq0 + kti - 3)
                    elif it_g > 0:
                        t.wait_ge(p.ev.h, prev_evac_last[0])
                    p.mm.inc(t.matmul(
                        sbanks[kti % 4][:, :], qaT[:, qb * 128:(qb + 1) * 128],
                        kaT[:, kt * 512:(kt + 1) * 512],
                        start=True, stop=True))

            @p.block.vector
            def _(v, h=h, qb=qb, nkt=nkt, it_g=it_g, mmq0=mmq0, evq0=evq0,
                  s_sb=s_sb, mcol=mcol):
                mld.wait(v, mt_ld[(h, qb)])
                if it_g > 1:
                    v.wait_ge(p.ac.h, ac_attn0 + it_g - 1)
                for kti in range(nkt):
                    v.wait_ge(p.mm.h, mmq0 + kti + 1)
                    p.chain(v, v.tensor_tensor(
                        s_sb[:, kti * 512:(kti + 1) * 512],
                        sbanks[kti % 4][:, :],
                        msk[it_g % 2][:, kti * 512:(kti + 1) * 512],
                        ALU.add))
                    p.ev.inc(v.reduce_max(mcol[:, 4 + kti:5 + kti],
                                          s_sb[:, kti * 512:(kti + 1) * 512],
                                          axis=AX.X))
                v.wait_ge(p.ev.h, evq0 + nkt)
                p.chain(v, v.reduce_max(mcol[:, 0:1], mcol[:, 4:4 + nkt],
                                        axis=AX.X))
                p.ev.inc(v.tensor_scalar_mul(mcol[:, 1:2], mcol[:, 0:1], -1.0))
            evac_last = evq0 + nkt
            negm_m = evq0 + nkt + 1
            prev_evac_last[0] = evac_last

            @p.block.scalar
            def _(sc, nkt=nkt, negm_m=negm_m, it_g=it_g, psb=psb, s_sb=s_sb,
                  mcol=mcol):
                sc.wait_ge(p.ev.h, negm_m)
                if it_g > 1:
                    sc.wait_ge(p.mm.h, trp_hist[0])
                p.ac.inc(sc.activation(psb[:, 0:nkt * 512], s_sb[:, 0:nkt * 512],
                                       AF.Exp, bias=mcol[:, 1:2],
                                       accum_out=mcol[:, 2:3]))
            exp_m = p.ac.n
            assert exp_m == ac_attn0 + it_g + 1, (exp_m, it_g)

            @p.block.vector
            def _(v, exp_m=exp_m, mcol=mcol):
                v.wait_ge(p.ac.h, exp_m)
                p.ev.inc(v.reciprocal(mcol[:, 3:4], mcol[:, 2:3]))
            rcol_m = p.ev.n

            trp = []
            pvm = []
            pv_ev0 = p.ev.n
            NKB = nkt * 4

            @p.block.tensor
            def _(t, nkt=nkt, lo_t=lo_t, va_s=va_s, exp_m=exp_m,
                  pv_ev0=pv_ev0, it_g=it_g, NKB=NKB, psb=psb):
                t.wait_ge(p.ac.h, exp_m)
                if it_g > 0:
                    t.wait_ge(p.ev.h, prev_scale[0])
                for kb in range(NKB):
                    if kb >= 2:
                        t.wait_ge(p.ev.h, pv_ev0 + kb - 1)
                    trp.append(p.mm.inc(t.transpose(
                        pb[kb % 2][:, :], psb[:, kb * 128:(kb + 1) * 128],
                        ident_b[:])))
                    if kb >= 2:
                        kbv = kb - 2
                        pvm.append(p.mm.inc(t.matmul(
                            ps[2][:, 0:128],
                            pT_sb[:, (kbv % 2) * 128:(kbv % 2 + 1) * 128],
                            va_s[:, (lo_t * 4 + kbv) * 128:(lo_t * 4 + kbv + 1) * 128],
                            start=(kbv == 0), stop=False)))
                for kbv in (NKB - 2, NKB - 1):
                    t.wait_ge(p.ev.h, pv_ev0 + kbv + 1)
                    pvm.append(p.mm.inc(t.matmul(
                        ps[2][:, 0:128],
                        pT_sb[:, (kbv % 2) * 128:(kbv % 2 + 1) * 128],
                        va_s[:, (lo_t * 4 + kbv) * 128:(lo_t * 4 + kbv + 1) * 128],
                        start=(kbv == 0), stop=(kbv == NKB - 1))))

            @p.block.vector
            def _(v, trp=trp, pvm=pvm, it_g=it_g, NKB=NKB):
                for kb in range(NKB):
                    v.wait_ge(p.mm.h, trp[kb])
                    if kb >= 2:
                        v.wait_ge(p.mm.h, pvm[kb - 2])
                    elif it_g > 0:
                        v.wait_ge(p.mm.h, prev_pv_last[0])
                    p.ev.inc(v.tensor_copy(
                        pT_sb[:, (kb % 2) * 128:(kb % 2 + 1) * 128],
                        pb[kb % 2][:, :]))
            pv_mm = pvm[-1]

            @p.block.vector
            def _(v, qb=qb, pv_mm=pv_mm, rcol_m=rcol_m, it_g=it_g, mcol=mcol):
                v.wait_ge(p.mm.h, pv_mm)
                v.wait_ge(p.ev.h, rcol_m)
                if it_g >= NQB:
                    ast.wait_idx(v, (it_g // NQB) - 1)
                p.ev.inc(v.tensor_scalar_mul(attn_col[:, qb * 128:(qb + 1) * 128],
                                             ps[2][:, 0:128], mcol[:, 3:4]))
            prev_scale[0] = p.ev.n
            trp_hist.append(trp[-1])
            trp_hist.pop(0)
            prev_pv_last[0] = pvm[-1]
            it_g += 1

        h_done_mm[h] = p.mm.n
        col_done = p.ev.n

        @p.block.sync
        def _(s_, h=h, col_done=col_done):
            s_.wait_ge(p.ev.h, col_done)
            ast.issue(s_.dma_start(
                rearr_rows(attn_out.ap()[:, h * 128:(h + 1) * 128]),
                attn_col[:].rearrange("p (a c) -> p a c", a=NQB)))
        if h + 2 < 16:
            kh_ld[h + 2] = emit_head_loads(h + 2)

    @p.block.gpsimd
    def _(g_):
        ost.wait_all(g_)
        ast.wait_all(g_)
    return p


def emit_down_resp(p, src_dram, KC, TOK, dT, ps, tag, src_ready, w_ready,
                   hslab, wcol, scr, x_dram, xout_dram, src_is_tm=False):
    nt = TOK // 512
    no2 = H // 128
    mm0 = p.mm.n
    ev0 = p.ev.n
    hstr = p.stream(f"dr_h{tag}", 1)
    wstr = p.stream(f"dr_w{tag}", 2)
    xld = p.stream(f"dr_x{tag}", 2)
    ost = p.stream(f"dr_o{tag}", 2)
    hld_use = []
    wld_use = []

    @p.block.sync
    def _(s):
        src_ready[0].wait_all(s, src_ready[1])
        if w_ready is not None:
            w_ready[0].wait_all(s, w_ready[1])
        it = 0
        for t in range(nt):
            if t >= 1:
                s.wait_ge(p.mm.h, mm0 + t * no2)
            if src_is_tm:
                u = None
                for kc in range(KC):
                    u = hstr.issue_slot(0, s.dma_start(
                        hslab[:, kc * 512:(kc + 1) * 512],
                        src_dram.ap()[t * 512:(t + 1) * 512,
                                      kc * 128:(kc + 1) * 128],
                        transpose=True))
                hld_use.append(u)
            else:
                hld_use.append(slab_load_split(
                    hstr, 0, s, hslab,
                    src_dram.ap()[:, t * 512:(t + 1) * 512], KC, 512))
            for o2 in range(no2):
                if it >= 2:
                    s.wait_ge(p.mm.h, mm0 + it - 1)
                wld_use.append(slab_load_split(
                    wstr, it % 2, s, wcol[it % 2],
                    dT.ap()[:, o2 * 128:(o2 + 1) * 128], KC, 128))
                if it >= 2:
                    it_s = it - 2
                    t_s, o2_s = divmod(it_s, no2)
                    s.wait_ge(p.ev.h, ev0 + it_s + 1)
                    ost.issue(s.dma_start(
                        xout_dram.ap()[o2_s * 128:(o2_s + 1) * 128,
                                       t_s * 512:(t_s + 1) * 512],
                        scr[:, 1024:2048][:, (it_s % 2) * 512:
                                          (it_s % 2 + 1) * 512]))
                it += 1
        for it_s in (nt * no2 - 2, nt * no2 - 1):
            t_s, o2_s = divmod(it_s, no2)
            s.wait_ge(p.ev.h, ev0 + it_s + 1)
            ost.issue(s.dma_start(
                xout_dram.ap()[o2_s * 128:(o2_s + 1) * 128,
                               t_s * 512:(t_s + 1) * 512],
                scr[:, 1024:2048][:, (it_s % 2) * 512:(it_s % 2 + 1) * 512]))

    @p.block.gpsimd
    def _(g):
        it = 0
        for t in range(nt):
            for o2 in range(no2):
                if it >= 2:
                    g.wait_ge(p.ev.h, ev0 + it - 1)
                xld.issue(g.dma_start(
                    scr[:, 0:1024][:, (it % 2) * 512:(it % 2 + 1) * 512],
                    x_dram.ap()[o2 * 128:(o2 + 1) * 128,
                                t * 512:(t + 1) * 512]))
                it += 1

    @p.block.tensor
    def _(t_):
        it = 0
        for t in range(nt):
            hstr.wait_all(t_, hld_use[t])
            for o2 in range(no2):
                wstr.wait(t_, wld_use[it])
                if it >= 2:
                    t_.wait_ge(p.ev.h, ev0 + it - 1)
                elif ev0 > 0:
                    t_.wait_ge(p.ev.h, ev0)
                bank = ps[4 + (it % 2)]
                last = None
                for kc in range(KC):
                    last = t_.matmul(bank[:, :],
                                     wcol[it % 2][:, kc * 128:(kc + 1) * 128],
                                     hslab[:, kc * 512:(kc + 1) * 512],
                                     start=(kc == 0), stop=(kc == KC - 1))
                p.mm.inc(last)
                it += 1

    @p.block.vector
    def _(v):
        it = 0
        for t in range(nt):
            for o2 in range(no2):
                v.wait_ge(p.mm.h, mm0 + it + 1)
                xld.wait_idx(v, it)
                if it >= 2:
                    ost.wait_idx(v, it - 2)
                bank = ps[4 + (it % 2)]
                sl = slice((it % 2) * 512, (it % 2 + 1) * 512)
                p.ev.inc(v.tensor_add(scr[:, 1024:2048][:, sl], bank[:, :],
                                      scr[:, 0:1024][:, sl]))
                it += 1
    return ost, ost.k - 1


def emit_down_rms(p, src_dram, KC, TOK, dT, ps, tag, src_ready, w_ready,
                  hslab, wcol, scr, y_sb, x_dram, xout_dram, ones_bf, rowbuf,
                  x_pre=None):
    nt = TOK // 512
    no2 = 16
    mm0 = p.mm.n
    evq0 = p.evq.n
    hstr = p.stream(f"dm_h{tag}", 1)
    wstr = p.stream(f"dm_w{tag}", 2)
    xld = p.stream(f"dm_x{tag}", 4)
    ost = p.stream(f"dm_o{tag}", 2)
    hld_use = []
    wld_use = []
    ac_sq = {}
    ev_apply = {}
    rb_prev = [0]

    def mm_down(t, o2):
        return mm0 + 18 * t + o2 + 1

    def mm_statd(t):
        return mm0 + 18 * t + 17

    def mm_bc(t):
        return mm0 + 18 * t + 18

    @p.block.sync
    def _(s):
        src_ready[0].wait_all(s, src_ready[1])
        if w_ready is not None:
            w_ready[0].wait_all(s, w_ready[1])
        it = 0
        for t in range(nt):
            if t >= 1:
                s.wait_ge(p.mm.h, mm_statd(t - 1))
            hld_use.append(slab_load_split(
                hstr, 0, s, hslab, src_dram.ap()[:, t * 512:(t + 1) * 512],
                KC, 512))
            for o2 in range(no2):
                if it >= 2:
                    tp, op = divmod(it - 2, no2)
                    s.wait_ge(p.mm.h, mm_down(tp, op))
                wld_use.append(slab_load_split(
                    wstr, it % 2, s, wcol[it % 2],
                    dT.ap()[:, o2 * 128:(o2 + 1) * 128], KC, 128))
                it += 1

    @p.block.gpsimd
    def _(g):
        if x_pre is not None:
            x_pre[0].wait_all(g, x_pre[1])
        for o2 in range(4):
            xld.issue(g.dma_start(
                scr[:, 0:2048][:, o2 * 512:(o2 + 1) * 512],
                x_dram.ap()[o2 * 128:(o2 + 1) * 128, 0:512]))

    for t in range(nt):
        @p.block.tensor
        def _(t_, t=t):
            hstr.wait_all(t_, hld_use[t])
            for o2 in range(no2):
                it = t * no2 + o2
                wstr.wait(t_, wld_use[it])
                if it >= 2:
                    t_.wait_ge(p.evq.h, evq0 + it - 1)
                bank = ps[4 + (it % 2)]
                last = None
                for kc in range(KC):
                    last = t_.matmul(bank[:, :],
                                     wcol[it % 2][:, kc * 128:(kc + 1) * 128],
                                     hslab[:, kc * 512:(kc + 1) * 512],
                                     start=(kc == 0), stop=(kc == KC - 1))
                p.mm.inc(last)

        @p.block.vector
        def _(v, t=t):
            for o2 in range(no2):
                it = t * no2 + o2
                v.wait_ge(p.mm.h, mm_down(t, o2))
                if t > 0 and o2 == 0:
                    v.wait_ge(p.ac.h, ac_sq[t - 1])
                p.evq.inc(v.tensor_copy(y_sb[:, o2 * 512:(o2 + 1) * 512],
                                        ps[4 + (it % 2)][:, :]))

        @p.block.scalar
        def _(sc, t=t):
            sc.wait_ge(p.evq.h, evq0 + (t + 1) * no2)
            if t > 0:
                sc.wait_ge(p.mm.h, mm_statd(t - 1))
            p.ac.inc(sc.activation(hslab[:, 0:8192], y_sb[:], AF.Square))
        ac_sq[t] = p.ac.n

        @p.block.tensor
        def _(t_, t=t):
            t_.wait_ge(p.ac.h, ac_sq[t])
            if t > 0:
                t_.wait_ge(p.ev.h, rb_prev[0])
            last = None
            for c in range(16):
                last = t_.matmul(ps[2][0:1, :], ones_bf[:, 0:1],
                                 hslab[:, c * 512:(c + 1) * 512],
                                 start=(c == 0), stop=(c == 15))
            p.mm.inc(last)

        @p.block.vector
        def _(v, t=t):
            v.wait_ge(p.mm.h, mm_statd(t))
            p.ev.inc(v.tensor_copy(rowbuf[0:1, 0:512], ps[2][0:1, :]))
        srow_m = p.ev.n

        @p.block.scalar
        def _(sc, srow_m=srow_m):
            sc.wait_ge(p.ev.h, srow_m)
            p.chain(sc, sc.activation(
                rowbuf[0:1, 512:1024], rowbuf[0:1, 0:512], AF.Ln,
                scale=1.0 / H,
                bias=p.cvals[0:1, CV_EPS_RMS:CV_EPS_RMS + 1]), "s")
            p.ac.inc(sc.activation(rowbuf[0:1, 0:512], rowbuf[0:1, 512:1024],
                                   AF.Exp, scale=-0.5))
        rstd_m = p.ac.n

        @p.block.tensor
        def _(t_, rstd_m=rstd_m):
            t_.wait_ge(p.ac.h, rstd_m)
            p.mm.inc(t_.matmul(ps[2][:, :], p.ones_f_g[0:1, :],
                               rowbuf[0:1, 0:512], start=True, stop=True))

        @p.block.vector
        def _(v, t=t):
            v.wait_ge(p.mm.h, mm_bc(t))
            p.ev.inc(v.tensor_copy(scr[:, 3072:3584], ps[2][:, :]))
            rb_prev.append(p.ev.n)
            rb_prev.pop(0) if len(rb_prev) > 1 else None
            v.wait_ge(p.ev.h, rb_prev[0])
            for o2 in range(no2):
                it = t * no2 + o2
                xld.wait_idx(v, it)
                if it >= 2:
                    ost.wait_idx(v, it - 2)
                osl = scr[:, 2048:3072][:, (it % 2) * 512:(it % 2 + 1) * 512]
                p.chain(v, v.tensor_mul(osl, y_sb[:, o2 * 512:(o2 + 1) * 512],
                                        scr[:, 3072:3584]))
                p.ev.inc(v.scalar_tensor_tensor(
                    osl, osl, 0.5,
                    scr[:, 0:2048][:, (it % 4) * 512:(it % 4 + 1) * 512],
                    ALU.mult, ALU.add))
                ev_apply[(t, o2)] = p.ev.n

        @p.block.gpsimd
        def _(g, t=t):
            for o2 in range(no2):
                it = t * no2 + o2
                g.wait_ge(p.ev.h, ev_apply[(t, o2)])
                ost.issue(g.dma_start(
                    xout_dram.ap()[o2 * 128:(o2 + 1) * 128,
                                   t * 512:(t + 1) * 512],
                    scr[:, 2048:3072][:, (it % 2) * 512:(it % 2 + 1) * 512]))
                itn = it + 4
                if itn < nt * no2:
                    tn, on = divmod(itn, no2)
                    xld.issue(g.dma_start(
                        scr[:, 0:2048][:, (itn % 4) * 512:(itn % 4 + 1) * 512],
                        x_dram.ap()[on * 128:(on + 1) * 128,
                                    tn * 512:(tn + 1) * 512]))
    return ost, ost.k - 1


def emit_add_tm(p, a_dram, b_dram, out_dram, TOK, arena, tag):
    nr = TOK // 128
    lda = p.stream(f"ad_a{tag}", 2)
    ldb = p.stream(f"ad_b{tag}", 2)
    ost = p.stream(f"ad_o{tag}", 2)
    abuf = arena[:, 0:2 * H]
    bbuf = arena[:, 2 * H:4 * H]
    obuf = arena[:, 4 * H:6 * H]
    ev0 = p.ev.n

    @p.block.gpsimd
    def _(g):
        for r in range(nr):
            if r >= 2:
                g.wait_ge(p.ev.h, ev0 + r - 1)
            sl = slice((r % 2) * H, (r % 2 + 1) * H)
            lda.issue(g.dma_start(abuf[:, sl],
                                  a_dram.ap()[r * 128:(r + 1) * 128, :]))
            ldb.issue(g.dma_start(bbuf[:, sl],
                                  b_dram.ap()[r * 128:(r + 1) * 128, :]))

    @p.block.vector
    def _(v):
        for r in range(nr):
            lda.wait(v, r)
            ldb.wait(v, r)
            if r >= 2:
                ost.wait_idx(v, r - 2)
            sl = slice((r % 2) * H, (r % 2 + 1) * H)
            p.ev.inc(v.tensor_add(obuf[:, sl], abuf[:, sl], bbuf[:, sl]))

    @p.block.sync
    def _(s):
        for r in range(nr):
            s.wait_ge(p.ev.h, ev0 + r + 1)
            sl = slice((r % 2) * H, (r % 2 + 1) * H)
            ost.issue(s.dma_start(out_dram.ap()[r * 128:(r + 1) * 128, :],
                                  obuf[:, sl]))
    return ost, ost.k - 1


class DqCtx:
    def __init__(self, p, tag="dq", vf=None, ob=None):
        self.p = p
        self.pk = p.sbuf(f"{tag}_pk", [128, 2 * 704], I32)
        self.vf = vf if vf is not None else p.sbuf(f"{tag}_vf", [128, 2048], F32)
        self.ti = p.sbuf(f"{tag}_ti", [128, 256], I32)
        self.t1 = p.sbuf(f"{tag}_t1", [128, 256], F32)
        self.t2 = p.sbuf(f"{tag}_t2", [128, 256], F32)
        self.t3 = p.sbuf(f"{tag}_t3", [128, 256], F32)
        self.ob = ob if ob is not None else p.sbuf(f"{tag}_ob", [128, 2 * 2048], BF16)
        self.sch = p.sbuf(f"{tag}_sch", [128, 2 * 44], F16)
        self.scf = p.sbuf(f"{tag}_scf", [128, 2 * 44], F32)
        self.pld = p.stream(f"{tag}_pld", 2)
        self.sld = p.stream(f"{tag}_sld", 2)
        self.ost = p.stream(f"{tag}_ost", 2)
        self.tile_i = 0
        self.ck_i = 0
        self.tile_done = {}


def emit_dequant(c, packed_in, scl_in, R, IN, wbf_dram):
    p = c.p
    W = IN // 8
    NG = IN // 128
    CW = W
    while CW > 256:
        assert CW % 2 == 0
        CW //= 2
    nck = W // CW
    GPC = CW * 8 // 128
    ntile = R // 128
    ev_scale = {}

    for tt in range(ntile):
        ti = c.tile_i
        c.tile_i += 1
        ck0 = c.ck_i
        pslot = slice((ti % 2) * 704, (ti % 2) * 704 + W)
        sslot = slice((ti % 2) * 44, (ti % 2) * 44 + NG)
        uses = []

        @p.block.gpsimd
        def _(g, tt=tt, ti=ti):
            if ti >= 2:
                g.wait_ge(p.ev.h, c.tile_done[ti - 2])
            uses.append(c.pld.issue_slot(ti % 2, g.dma_start(
                c.pk[:, pslot], packed_in.ap()[tt * 128:(tt + 1) * 128, :])))
            uses.append(c.sld.issue_slot(ti % 2, g.dma_start(
                c.sch[:, sslot], scl_in.ap()[tt * 128:(tt + 1) * 128, :])))

        @p.block.vector
        def _(v, tt=tt, ti=ti, ck0=ck0):
            c.pld.wait(v, uses[0])
            c.sld.wait(v, uses[1])
            p.chain(v, v.tensor_copy(c.scf[:, sslot], c.sch[:, sslot]))
            for ck in range(nck):
                cki = ck0 + ck
                pk_sl = c.pk[:, (ti % 2) * 704 + ck * CW:
                             (ti % 2) * 704 + (ck + 1) * CW]
                vf = c.vf[:, 0:CW * 8]
                tiv = c.ti[:, 0:CW]
                a = c.t1[:, 0:CW]
                m = c.t2[:, 0:CW]
                b = c.t3[:, 0:CW]
                for j in range(8):
                    out_sl = vf[:, j:CW * 8:8]
                    p.chain(v, v.tensor_scalar(tiv, pk_sl, 4 * j, 15,
                                               ALU.logical_shift_right,
                                               ALU.bitwise_and))
                    p.chain(v, v.tensor_copy(a, tiv))
                    p.chain(v, v.tensor_scalar(m, a, 8.0, None, ALU.is_ge))
                    p.chain(v, v.scalar_tensor_tensor(b, m, -8.0, a,
                                                      ALU.mult, ALU.add))
                    p.chain(v, v.tensor_scalar(a, b, -4.0, 0.0,
                                               ALU.add, ALU.max))
                    p.chain(v, v.tensor_tensor(a, a, b, ALU.add))
                    p.chain(v, v.tensor_scalar(b, b, 7.0, None, ALU.is_ge))
                    v.scalar_tensor_tensor(a, b, 2.0, a, ALU.mult, ALU.add)
                    p.chain(v, v.tensor_scalar(m, m, -1.0, 0.5,
                                               ALU.mult, ALU.add))
                    p.chain(v, v.tensor_tensor(out_sl, a, m, ALU.mult))
                if cki >= 2:
                    c.ost.wait_idx(v, cki - 2)
                last = None
                for g2 in range(GPC):
                    gabs = ck * GPC + g2
                    last = v.tensor_scalar(
                        c.ob[:, (cki % 2) * 2048 + g2 * 128:
                             (cki % 2) * 2048 + (g2 + 1) * 128],
                        vf[:, g2 * 128:(g2 + 1) * 128],
                        c.scf[:, (ti % 2) * 44 + gabs:
                              (ti % 2) * 44 + gabs + 1],
                        None, ALU.mult)
                p.ev.inc(last)
                ev_scale[cki] = p.ev.n
                c.ck_i += 1
        c.tile_done[ti] = ev_scale[ck0 + nck - 1]

        @p.block.sync
        def _(s, tt=tt, ck0=ck0):
            for ck in range(nck):
                cki = ck0 + ck
                s.wait_ge(p.ev.h, ev_scale[cki])
                c.ost.issue(s.dma_start(
                    wbf_dram.ap()[tt * 128:(tt + 1) * 128,
                                  ck * CW * 8:(ck + 1) * CW * 8],
                    c.ob[:, (cki % 2) * 2048:(cki % 2) * 2048 + CW * 8]))
    return c.ost, c.ost.k - 1


def build_launch4(TOK=2048, RGU=1408, RDN=256):
    p = Prog()
    ps = p.psum_pool()

    x1_in = p.dram("x1_in", [H, TOK], F32, kind="ExternalInput")
    attn_in = p.dram("attn_in", [TOK, H], BF16, kind="ExternalInput")
    ottt_in = p.dram("ottt_in", [TOK, H], BF16, kind="ExternalInput")
    oproj_in = p.dram("w_oproj", [H, H], F32, kind="ExternalInput")
    ident_in = p.dram("ident_in", [128, 128], F32, kind="ExternalInput")
    w_names = ["r_gate", "r_up", "r_down", "sh_gate", "sh_up", "sh_down"]
    w_shapes = {"sh_gate": (8192, H), "sh_up": (8192, H), "sh_down": (H, 8192),
                "r_gate": (1024, H), "r_up": (1024, H), "r_down": (H, 1024)}
    w_in = {n: p.dram(f"w_{n}", list(w_shapes[n]), F32, kind="ExternalInput")
            for n in w_names}
    router_in = p.dram("w_router", [8, H], F32, kind="ExternalInput")
    gu_pk_in = p.dram("gu_pk", [RGU, H // 8], I32, kind="ExternalInput")
    gu_sc_in = p.dram("gu_sc", [RGU, H // 128], F16, kind="ExternalInput")
    dn_pk_in = p.dram("dn_pk", [RDN, 5632 // 8], I32, kind="ExternalInput")
    dn_sc_in = p.dram("dn_sc", [RDN, 5632 // 128], F16, kind="ExternalInput")

    x3_out = p.dram("x3_out", [H, TOK], F32, kind="ExternalOutput")
    wguT_out = p.dram("wguT_out", [H, RGU], BF16, kind="ExternalOutput")
    wdnT_out = p.dram("wdnT_out", [5632, RDN], BF16, kind="ExternalOutput")

    wbf = {n: p.dram(f"wbf_{n}", list(w_shapes[n]), BF16) for n in w_names}
    wbf["oproj"] = p.dram("wbf_oproj", [H, H], BF16)
    wT = {n: p.dram(f"wT_{n}", [w_shapes[n][1], w_shapes[n][0]], BF16)
          for n in w_names}
    wT["oproj"] = p.dram("wT_oproj", [H, H], BF16)
    comb_d = p.dram("comb_d", [TOK, H], BF16)
    x2_d = p.dram("x2_d", [H, TOK], F32)
    h_dram = p.dram("h_d", [8192, TOK], BF16)
    hr_dram = p.dram("hr_d", [1024, TOK], BF16)
    yr_dram = p.dram("yr_d", [H, TOK], F32)
    wbf_gu_d = p.dram("wbf_gu_d", [RGU, H], BF16)
    wbf_dn_d = p.dram("wbf_dn_d", [RDN, 5632], BF16)

    ones_bf, ones_f, _ = emit_consts(p)
    ident_f = p.sbuf("ident_f", [128, 128], F32)
    ild = p.stream("identld", 1)

    @p.block.gpsimd
    def _(g):
        ild.issue(g.dma_start(ident_f[:], ident_in.ap()))

    A = p.sbuf("A", [128, max(NCH * TOK, 64 * 512)], BF16)

    add_done = emit_add_tm(p, attn_in, ottt_in, comb_d, TOK, A, "cmb")

    stage = [p.sbuf(f"stage{j}", [128, 8192], BF16) for j in range(2)]
    bounce = p.sbuf("wpbounce", [128, 2 * 2048], BF16)
    bslots = [bounce[:, 0:2048], bounce[:, 2048:4096]]
    prep_names = ["oproj"] + w_names
    shp = dict(w_shapes, oproj=(H, H))
    w_all_in = dict(w_in, oproj=oproj_in)
    specs = [(w_all_in[n].ap(), wbf[n], wT[n], *shp[n]) for n in prep_names]
    cast_m, tr_m, wbs = emit_weight_prep(p, specs, "l4", bslots, piece=2048)
    prep_oproj = (wbs, tr_m[0])
    prep_all = (wbs, wbs.k - 1)
    prep_thru = {n: (wbs, tr_m[j]) for j, n in enumerate(prep_names)}

    owcol = [p.sbuf(f"owcol{j}", [128, NCH * 128], BF16) for j in range(2)]
    op_scr = (p.named_buf("lnx", [128, 2 * min(2048, TOK)], F32)
              if 2 * min(2048, TOK) >= 2048
              else p.sbuf("op_scr", [128, 2048], F32))
    op_ost, op_u = emit_down_resp(p, comb_d, NCH, TOK, wT["oproj"], ps, "opj",
                                  add_done, prep_oproj,
                                  A[:, 6 * H:6 * H + 64 * 128],
                                  owcol, op_scr, x1_in, x2_d, src_is_tm=True)

    xln_mark = emit_ln_fm(p, x2_d, A, TOK, ps, ones_bf, ones_f, "l4",
                          entry_stream=(op_ost, op_u))

    comb_b = p.lnmu
    comb_mark = emit_router_comb(p, A, TOK, router_in.ap(), ps, ones_f, comb_b,
                                 "l4", xln_mark, ident_f, ident_ready=(ild, 0))

    tmp = p.sbuf("gu_tmp", [128, 2 * 512], BF16)
    htile = p.sbuf("gu_ht", [128, 2 * 512], BF16)
    hr_ready = emit_ffn_gu(p, A, TOK, wT["r_gate"], wT["r_up"], hr_dram, 1024,
                           ps, "rgu", prep_thru["r_up"], xln_mark, stage, tmp,
                           htile)

    hrslab = (p.lnsq if 2 * min(2048, TOK) >= 8 * 512
              else p.sbuf("hrslab", [128, 8 * 512], BF16))
    rwcol = [p.sbuf(f"rwcol{j}", [128, 8 * 128], BF16) for j in range(2)]
    scr = (p.lnx if 2 * min(2048, TOK) >= 3072
           else p.sbuf("scr", [128, 3072], F32))
    yr_ready = emit_ffn_down(p, hr_dram, 1024, TOK, wT["r_down"], ps, "rdn",
                             hr_ready, prep_thru["r_down"], hrslab, rwcol,
                             "yr", scr, comb_b=comb_b, yr_dram=yr_dram,
                             ev_pre_wait=comb_mark)

    h_ready = emit_ffn_gu(p, A, TOK, wT["sh_gate"], wT["sh_up"], h_dram, 8192,
                          ps, "sgu", prep_thru["sh_up"], xln_mark, stage, tmp,
                          htile)

    ost, ost_u = emit_ffn_down(p, h_dram, 8192, TOK, wT["sh_down"], ps, "sdn",
                               h_ready, prep_thru["sh_down"], A, stage, "res",
                               scr,
                               yr_dram=yr_dram, x_dram=x2_d, xout_dram=x3_out,
                               ev_pre_wait=comb_mark, st_pre=yr_ready,
                               x_pre=(op_ost, op_u))

    dq = DqCtx(p, vf=p.lnrows[:, 0:2048], ob=A[:, 0:4096])
    dq_ost1 = emit_dequant(dq, gu_pk_in, gu_sc_in, RGU, H, wbf_gu_d)
    dq_ost2 = emit_dequant(dq, dn_pk_in, dn_sc_in, RDN, 5632, wbf_dn_d)
    dq_specs = [(None, wbf_gu_d, wguT_out, RGU, H),
                (None, wbf_dn_d, wdnT_out, RDN, 5632)]
    _, dq_tr, dq_wbs = emit_weight_prep(
        p, dq_specs, "dq", bslots, piece=2048,
        ready_map={0: [dq_ost1, prep_all], 1: [dq_ost2]})

    @p.block.gpsimd
    def _(g):
        ost.wait_all(g)
        dq_wbs.wait_all(g)
    return p


def emit_proj_fm(p, xln_sb, TOK, wT, ps, tag, w_ready, x_mark, wcol,
                 vmin, ebuf, obuf, out_dram, fmap=True, pre_ev=0):
    nt = TOK // 512
    no2 = NCH
    n_it = no2 * nt
    mm0 = p.mm.n
    ev0 = p.ev.n
    ac0 = p.ac.n
    EPI = 2 if fmap else 1
    wstr = p.stream(f"pf_w{tag}", 2)
    ost = p.stream(f"pf_o{tag}", 2)
    wld_use = []

    def mmk(it):
        return mm0 + it + 1

    def ev_min(it):
        return ev0 + EPI * it + 1

    def ev_out(it):
        return ev0 + EPI * it + EPI

    @p.block.sync
    def _(s):
        if w_ready is not None:
            w_ready[0].wait_all(s, w_ready[1])
        for o2 in range(no2):
            if o2 >= 2:
                s.wait_ge(p.mm.h, mm0 + (o2 - 1) * nt)
            wld_use.append(slab_load_split(
                wstr, o2 % 2, s, wcol[o2 % 2],
                wT.ap()[:, o2 * 128:(o2 + 1) * 128], NCH, 128))

    @p.block.tensor
    def _(t_):
        t_.wait_ge(p.ev.h, x_mark)
        if pre_ev:
            t_.wait_ge(p.ev.h, pre_ev)
        it = 0
        for o2 in range(no2):
            wstr.wait(t_, wld_use[o2])
            for t in range(nt):
                if it >= 2:
                    t_.wait_ge(p.ev.h, ev_out(it - 2))
                bank = ps[4 + (it % 2)]
                last = None
                for kc in range(NCH):
                    last = t_.matmul(
                        bank[:, :], wcol[o2 % 2][:, kc * 128:(kc + 1) * 128],
                        xln_sb[:, kc * TOK + t * 512:kc * TOK + (t + 1) * 512],
                        start=(kc == 0), stop=(kc == NCH - 1))
                p.mm.inc(last)
                it += 1

    if fmap:
        @p.block.scalar
        def _(sc):
            for it in range(n_it):
                sc.wait_ge(p.ev.h, ev_min(it))
                if it >= 2:
                    sc.wait_ge(p.ev.h, ev_out(it - 2))
                sl = slice((it % 2) * 512, (it % 2 + 1) * 512)
                p.ac.inc(sc.activation(ebuf[:, sl], vmin[:, sl], AF.Exp))

    @p.block.vector
    def _(v):
        for it in range(n_it):
            sl = slice((it % 2) * 512, (it % 2 + 1) * 512)
            bank = ps[4 + (it % 2)]
            v.wait_ge(p.mm.h, mmk(it))
            if fmap:
                if it >= 2:
                    v.wait_ge(p.ac.h, ac0 + it - 1)
                p.ev.inc(v.tensor_scalar(vmin[:, sl], bank[:, :], 0.0, None,
                                         ALU.min))
                v.wait_ge(p.ac.h, ac0 + it + 1)
                if it >= 2:
                    ost.wait_idx(v, it - 2)
                p.ev.inc(v.scalar_tensor_tensor(obuf[:, sl], bank[:, :], 0.0,
                                                ebuf[:, sl], ALU.max, ALU.add))
            else:
                if it >= 2:
                    ost.wait_idx(v, it - 2)
                p.ev.inc(v.tensor_copy(obuf[:, sl], bank[:, :]))

    @p.block.gpsimd
    def _(g):
        it = 0
        for o2 in range(no2):
            for t in range(nt):
                g.wait_ge(p.ev.h, ev_out(it))
                ost.issue(g.dma_start(
                    out_dram.ap()[o2 * 128:(o2 + 1) * 128,
                                  t * 512:(t + 1) * 512],
                    obuf[:, (it % 2) * 512:(it % 2 + 1) * 512]))
                it += 1
    return ost, ost.k - 1


def emit_proj_tm(p, xln_sb, TOK, wT, ps, tag, w_ready, x_mark, wsl,
                 vmin, ebuf, obuf, out_dram, fmap=True, pre_mm=0, pre_ev=0):
    ntb = TOK // 128
    nog = H // 512
    n_it = nog * ntb
    mm0 = p.mm.n
    ev0 = p.ev.n
    ac0 = p.ac.n
    EPI = 2 if fmap else 1
    wstr = p.stream(f"pt_w{tag}", 2)
    ost = p.stream(f"pt_o{tag}", 2)
    wld_use = []

    def ev_min(it):
        return ev0 + EPI * it + 1

    def ev_out(it):
        return ev0 + EPI * it + EPI

    @p.block.sync
    def _(s):
        if w_ready is not None:
            w_ready[0].wait_all(s, w_ready[1])
        if pre_mm:
            s.wait_ge(p.mm.h, pre_mm)
        for og in range(nog):
            if og >= 2:
                s.wait_ge(p.mm.h, mm0 + (og - 1) * ntb)
            wld_use.append(slab_load_split(
                wstr, og % 2, s, wsl[og % 2],
                wT.ap()[:, og * 512:(og + 1) * 512], NCH, 512))

    @p.block.tensor
    def _(t_):
        t_.wait_ge(p.ev.h, x_mark)
        if pre_ev:
            t_.wait_ge(p.ev.h, pre_ev)
        it = 0
        for og in range(nog):
            wstr.wait(t_, wld_use[og])
            for tb in range(ntb):
                if it >= 2:
                    t_.wait_ge(p.ev.h, ev_out(it - 2))
                bank = ps[4 + (it % 2)]
                last = None
                for kc in range(NCH):
                    last = t_.matmul(
                        bank[:, :],
                        xln_sb[:, kc * TOK + tb * 128:kc * TOK + (tb + 1) * 128],
                        wsl[og % 2][:, kc * 512:(kc + 1) * 512],
                        start=(kc == 0), stop=(kc == NCH - 1))
                p.mm.inc(last)
                it += 1

    if fmap:
        @p.block.scalar
        def _(sc):
            for it in range(n_it):
                sc.wait_ge(p.ev.h, ev_min(it))
                if it >= 2:
                    sc.wait_ge(p.ev.h, ev_out(it - 2))
                sl = slice((it % 2) * 512, (it % 2 + 1) * 512)
                p.ac.inc(sc.activation(ebuf[:, sl], vmin[:, sl], AF.Exp))

    @p.block.vector
    def _(v):
        for it in range(n_it):
            sl = slice((it % 2) * 512, (it % 2 + 1) * 512)
            bank = ps[4 + (it % 2)]
            v.wait_ge(p.mm.h, mm0 + it + 1)
            if fmap:
                if it >= 2:
                    v.wait_ge(p.ac.h, ac0 + it - 1)
                p.ev.inc(v.tensor_scalar(vmin[:, sl], bank[:, :], 0.0, None,
                                         ALU.min))
                v.wait_ge(p.ac.h, ac0 + it + 1)
                if it >= 2:
                    ost.wait_idx(v, it - 2)
                p.ev.inc(v.scalar_tensor_tensor(obuf[:, sl], bank[:, :], 0.0,
                                                ebuf[:, sl], ALU.max, ALU.add))
            else:
                if it >= 2:
                    ost.wait_idx(v, it - 2)
                p.ev.inc(v.tensor_copy(obuf[:, sl], bank[:, :]))

    @p.block.gpsimd
    def _(g):
        it = 0
        for og in range(nog):
            for tb in range(ntb):
                g.wait_ge(p.ev.h, ev_out(it))
                ost.issue(g.dma_start(
                    out_dram.ap()[tb * 128:(tb + 1) * 128,
                                  og * 512:(og + 1) * 512],
                    obuf[:, (it % 2) * 512:(it % 2 + 1) * 512]))
                it += 1
    return ost, ost.k - 1


def emit_kvks(p, k_d, v_d, TOK, ps, tag, kv_ready, ones_bf, kvks_sb,
              kcol, vcol, out_dram):
    ntb = TOK // 128
    NHD = 16
    mm0 = p.mm.n
    ev0 = p.ev.n
    kst = p.stream(f"kv_k{tag}", 2)
    vst = p.stream(f"kv_v{tag}", 2)
    ost = p.stream(f"kv_o{tag}", 1)
    kuse, vuse = [], []

    @p.block.gpsimd
    def _(g):
        for pr in kv_ready:
            pr[0].wait_all(g, pr[1])
        for h in range(NHD):
            if h >= 2:
                g.wait_ge(p.ev.h, ev0 + 2 * (h - 2) + 2)
            kuse.append(slab_load_split(kst, h % 2, g, kcol[h % 2],
                                        k_d.ap()[:, h * 128:(h + 1) * 128],
                                        ntb, 128))
            vuse.append(slab_load_split(vst, h % 2, g, vcol[h % 2],
                                        v_d.ap()[:, h * 128:(h + 1) * 128],
                                        ntb, 128))

    @p.block.tensor
    def _(t_):
        for h in range(NHD):
            kst.wait(t_, kuse[h])
            vst.wait(t_, vuse[h])
            if h >= 2:
                t_.wait_ge(p.ev.h, ev0 + 2 * (h - 2) + 2)
            bkv = ps[0 + (h % 2)]
            bks = ps[2 + (h % 2)]
            last = None
            for tb in range(ntb):
                last = t_.matmul(bkv[:, 0:128],
                                 kcol[h % 2][:, tb * 128:(tb + 1) * 128],
                                 vcol[h % 2][:, tb * 128:(tb + 1) * 128],
                                 start=(tb == 0), stop=(tb == ntb - 1))
            p.mm.inc(last)
            for tb in range(ntb):
                last = t_.matmul(bks[:, 0:1],
                                 kcol[h % 2][:, tb * 128:(tb + 1) * 128],
                                 ones_bf[:, 0:1],
                                 start=(tb == 0), stop=(tb == ntb - 1))
            p.mm.inc(last)

    @p.block.vector
    def _(v):
        for h in range(NHD):
            v.wait_ge(p.mm.h, mm0 + 2 * h + 2)
            bkv = ps[0 + (h % 2)]
            bks = ps[2 + (h % 2)]
            p.ev.inc(v.tensor_copy(kvks_sb[:, h * 129:h * 129 + 128],
                                   bkv[:, 0:128]))
            p.ev.inc(v.tensor_copy(kvks_sb[:, h * 129 + 128:h * 129 + 129],
                                   bks[:, 0:1]))

    @p.block.sync
    def _(s):
        s.wait_ge(p.ev.h, ev0 + 2 * NHD)
        ost.issue(s.dma_start(out_dram.ap(), kvks_sb[:, 0:NHD * 129]))
    return ost, ost.k - 1


def emit_linattn(p, q_in, kvp_in, TOK, ps, tag, kv_f, ptmp, kv_b, qrow, zb,
                 obuf, o_d):
    NHD = 16
    ntb = TOK // 128
    n_it = NHD * ntb
    mm0 = p.mm.n
    pld = p.stream(f"la_p{tag}", 2)
    qld = p.stream(f"la_q{tag}", 2)
    ost = p.stream(f"la_o{tag}", 2)

    ev_add0 = p.ev.n

    @p.block.gpsimd
    def _(g):
        pld.issue_slot(0, g.dma_start(kv_f[:, 0:2064],
                                      kvp_in.ap()[0:128, :]))
        for j in range(1, 4):
            if j >= 2:
                g.wait_ge(p.ev.h, ev_add0 + j - 1)
            pld.issue_slot(j % 2, g.dma_start(
                ptmp[:, 0:2064], kvp_in.ap()[j * 128:(j + 1) * 128, :]))

    @p.block.vector
    def _(v):
        pld.wait(v, 0)
        for j in range(1, 4):
            pld.wait(v, j)
            p.ev.inc(v.tensor_add(kv_f[:, 0:2064], kv_f[:, 0:2064],
                                  ptmp[:, 0:2064]))
            v.wait_ge(p.ev.h, ev_add0 + j)
        p.ev.inc(v.tensor_copy(kv_b[:, 0:2064], kv_f[:, 0:2064]))
    kvb_m = p.ev.n
    ev0 = p.ev.n

    def ev_out(it):
        return ev0 + it + 1

    @p.block.sync
    def _(s):
        for h in range(NHD):
            if h >= 2:
                s.wait_ge(p.mm.h, mm0 + (h - 1) * ntb)
            qld.issue(s.dma_start(qrow[:, (h % 2) * TOK:(h % 2 + 1) * TOK],
                                  q_in.ap()[h * 128:(h + 1) * 128, :]))

    @p.block.tensor
    def _(t_):
        t_.wait_ge(p.ev.h, kvb_m)
        it = 0
        for h in range(NHD):
            qld.wait(t_, h)
            for tb in range(ntb):
                if it >= 2:
                    t_.wait_ge(p.ev.h, ev_out(it - 2))
                bank = ps[4 + (it % 2)]
                p.mm.inc(t_.matmul(
                    bank[:, 0:129],
                    qrow[:, (h % 2) * TOK + tb * 128:
                         (h % 2) * TOK + (tb + 1) * 128],
                    kv_b[:, h * 129:(h + 1) * 129],
                    start=True, stop=True))
                it += 1

    @p.block.vector
    def _(v):
        it = 0
        for h in range(NHD):
            for tb in range(ntb):
                v.wait_ge(p.mm.h, mm0 + it + 1)
                if it >= 2:
                    ost.wait_idx(v, it - 2)
                bank = ps[4 + (it % 2)]
                zs = slice((it % 2) * 2, (it % 2) * 2 + 1)
                zs2 = slice((it % 2) * 2 + 1, (it % 2) * 2 + 2)
                p.chain(v, v.tensor_scalar(zb[:, zs], bank[:, 128:129],
                                           1e-6, None, ALU.add))
                p.chain(v, v.reciprocal(zb[:, zs2], zb[:, zs]))
                p.ev.inc(v.tensor_scalar(
                    obuf[:, (it % 2) * 128:(it % 2 + 1) * 128],
                    bank[:, 0:128], zb[:, zs2], None, ALU.mult))
                it += 1

    @p.block.gpsimd
    def _(g):
        it = 0
        for h in range(NHD):
            for tb in range(ntb):
                g.wait_ge(p.ev.h, ev_out(it))
                ost.issue(g.dma_start(
                    o_d.ap()[tb * 128:(tb + 1) * 128, h * 128:(h + 1) * 128],
                    obuf[:, (it % 2) * 128:(it % 2 + 1) * 128]))
                it += 1
    return ost, ost.k - 1


def build_launch5(TOK=2048):
    assert TOK == 2048
    p = Prog()
    ps = p.psum_pool()
    p.evq = Sem(p, "evq")

    x3_in = p.dram("x3_in", [H, TOK], F32, kind="ExternalInput")
    wguT_in = p.dram("wguT_in", [H, 11264], BF16, kind="ExternalInput")
    wdnT_in = p.dram("wdnT_in", [5632, H], BF16, kind="ExternalInput")
    wq_in = p.dram("w_q", [H, H], F32, kind="ExternalInput")
    wk_in = p.dram("w_k", [H, H], F32, kind="ExternalInput")
    wv_in = p.dram("w_v", [H, H], F32, kind="ExternalInput")

    x4_out = p.dram("x4_out", [H, TOK], F32, kind="ExternalOutput")
    q_out = p.dram("q_out", [H, TOK], BF16, kind="ExternalOutput")
    kvks_out = p.dram("kvks_out", [128, 16 * 129], F32, kind="ExternalOutput")

    wbf = {n: p.dram(f"wbf_{n}", [H, H], BF16) for n in ["q", "k", "v"]}
    wT = {n: p.dram(f"wT_{n}", [H, H], BF16) for n in ["q", "k", "v"]}
    h1_d = p.dram("h1_d", [5632, TOK], BF16)
    k_d = p.dram("k_d", [TOK, H], BF16)
    v_d = p.dram("v_d", [TOK, H], BF16)

    ones_bf, ones_f, _ = emit_consts(p)
    p.ones_f_g = ones_f

    A = p.sbuf("A", [128, NCH * TOK], BF16)
    stage = [p.sbuf(f"stage{j}", [128, 8192], BF16) for j in range(2)]
    qstage = [p.sbuf(f"qstage{j}", [128, H], BF16) for j in range(2)]

    specs = [(w.ap(), wbf[n], wT[n], H, H)
             for n, w in [("q", wq_in), ("k", wk_in), ("v", wv_in)]]
    cast_m, tr_m, wbs = emit_weight_prep(p, specs, "l5", qstage)
    prep_all = (wbs, wbs.k - 1)

    xln_mark = emit_ln_fm(p, x3_in, A, TOK, ps, ones_bf, ones_f, "l5a")

    tmp = p.sbuf("gu_tmp", [128, 2 * 512], BF16)
    htile = p.sbuf("gu_ht", [128, 2 * 512], BF16)
    h_ready = emit_ffn_gu(p, A, TOK, wguT_in, wguT_in, h1_d, 5632, ps, "f1a",
                          None, xln_mark, stage, tmp, htile,
                          g_off=0, u_off=5632)

    rms_ost, rms_u = emit_down_rms(p, h1_d, 44, TOK, wdnT_in, ps, "r1a",
                                   h_ready, None, A[:, 0:44 * 512], stage,
                                   p.lnx, A[:, 44 * 512:44 * 512 + 8192],
                                   x3_in, x4_out, ones_bf, p.lnrows)

    xln2_mark = emit_ln_fm(p, x4_out, A, TOK, ps, ones_bf, ones_f, "l5b",
                           entry_stream=(rms_ost, rms_u))

    wcol_q = [p.sbuf(f"qwcol{j}", [128, NCH * 128], BF16) for j in range(2)]
    vmin = p.lnx[:, 0:1024]
    ebuf = p.lnx[:, 1024:2048]
    obq = p.lnxb_t[:, 0:1024]
    obk = p.lnxb_t[:, 1024:2048]
    obv = p.lnxb_t[:, 2048:3072]
    q_ost, q_u = emit_proj_fm(p, A, TOK, wT["q"], ps, "q", prep_all,
                              xln2_mark, wcol_q, vmin, ebuf, obq, q_out,
                              fmap=True)
    k_ost, k_u = emit_proj_tm(p, A, TOK, wT["k"], ps, "k", prep_all,
                              xln2_mark, stage, vmin, ebuf, obk, k_d,
                              fmap=True, pre_mm=p.mm.n, pre_ev=p.ev.n)
    v_ost, v_u = emit_proj_tm(p, A, TOK, wT["v"], ps, "v", prep_all,
                              xln2_mark, stage, vmin, ebuf, obv, v_d,
                              fmap=False, pre_mm=p.mm.n, pre_ev=p.ev.n)

    kcol = [A[:, j * 2048:(j + 1) * 2048] for j in range(2)]
    vcol = [A[:, (2 + j) * 2048:(3 + j) * 2048] for j in range(2)]
    kvks_sb = p.lnrows[:, 0:2064]
    kv_ost, kv_u = emit_kvks(p, k_d, v_d, TOK, ps, "kv",
                             [(k_ost, k_u), (v_ost, v_u)], ones_bf,
                             kvks_sb, kcol, vcol, kvks_out)

    @p.block.gpsimd
    def _(g):
        rms_ost.wait_all(g)
        q_ost.wait_all(g)
        kv_ost.wait_all(g)
    return p


def build_launch6(TOK=2048):
    assert TOK == 2048
    p = Prog()
    ps = p.psum_pool()
    p.evq = Sem(p, "evq")

    x4_in = p.dram("x4_in", [H, TOK], F32, kind="ExternalInput")
    q_in = p.dram("q_in", [H, TOK], BF16, kind="ExternalInput")
    kvp_in = p.dram("kvp_in", [512, 2064], F32, kind="ExternalInput")
    wo_in = p.dram("w_o", [H, H], F32, kind="ExternalInput")
    wguT_in = p.dram("wguT_in", [H, 11264], BF16, kind="ExternalInput")
    wdnT_in = p.dram("wdnT_in", [5632, H], BF16, kind="ExternalInput")

    x6_out = p.dram("x6_out", [H, TOK], F32, kind="ExternalOutput")

    wbf_o = p.dram("wbf_o", [H, H], BF16)
    wT_o = p.dram("wT_o", [H, H], BF16)
    o_d = p.dram("o_d", [TOK, H], BF16)
    x5_d = p.dram("x5_d", [H, TOK], F32)
    h2_d = p.dram("h2_d", [5632, TOK], BF16)

    ones_bf, ones_f, _ = emit_consts(p)
    p.ones_f_g = ones_f

    A = p.sbuf("A", [128, NCH * TOK], BF16)
    stage = [p.sbuf(f"stage{j}", [128, 8192], BF16) for j in range(2)]
    qstage = [p.sbuf(f"qstage{j}", [128, H], BF16) for j in range(2)]

    specs = [(wo_in.ap(), wbf_o, wT_o, H, H)]
    cast_m, tr_m, wbs = emit_weight_prep(p, specs, "l6", qstage)
    prep_o = (wbs, wbs.k - 1)

    lnx = p.named_buf("lnx", [128, 2 * 2048], F32)
    lnrows = p.named_buf("lnrows", [128, 2 * 2048], F32)
    lnxb = p.named_buf("lnxb", [128, 2 * 2048], BF16)
    kv_f = lnrows[:, 0:2064]
    ptmp = lnx[:, 0:2064]
    kv_b = lnxb[:, 0:2064]
    qrow = p.sbuf("qrow", [128, 2 * TOK], BF16)
    zb = p.sbuf("zb", [128, 4], F32)
    obuf = lnxb[:, 2064:2064 + 256]
    la_ost, la_u = emit_linattn(p, q_in, kvp_in, TOK, ps, "la", kv_f, ptmp,
                                kv_b, qrow, zb, obuf, o_d)

    owcol = [p.sbuf(f"owcol{j}", [128, NCH * 128], BF16) for j in range(2)]
    op_scr = p.sbuf("op_scr", [128, 2048], F32)
    wo_ost, wo_u = emit_down_resp(p, o_d, NCH, TOK, wT_o, ps, "wo",
                                  (la_ost, la_u), prep_o, A[:, 0:8192],
                                  owcol, op_scr, x4_in, x5_d, src_is_tm=True)

    xln_mark = emit_ln_fm(p, x5_d, A, TOK, ps, ones_bf, ones_f, "l6",
                          entry_stream=(wo_ost, wo_u))

    tmp = p.sbuf("gu_tmp", [128, 2 * 512], BF16)
    htile = p.sbuf("gu_ht", [128, 2 * 512], BF16)
    h_ready = emit_ffn_gu(p, A, TOK, wguT_in, wguT_in, h2_d, 5632, ps, "f1b",
                          None, xln_mark, stage, tmp, htile,
                          g_off=0, u_off=5632)

    rms_ost, rms_u = emit_down_rms(p, h2_d, 44, TOK, wdnT_in, ps, "r1b",
                                   h_ready, None, A[:, 0:44 * 512], stage,
                                   p.lnx, A[:, 44 * 512:44 * 512 + 8192],
                                   x5_d, x6_out, ones_bf, p.lnrows,
                                   x_pre=(wo_ost, wo_u))

    @p.block.gpsimd
    def _(g):
        rms_ost.wait_all(g)
    return p


import ml_dtypes
from concourse.bass_utils import run_bass_kernel_spmd

BFNP = ml_dtypes.bfloat16
F32NP = np.float32
SEQ = 8192
TOKC = 2048


def _ctab(qt):
    pos = np.arange(TOKC, dtype=np.float64) + qt * TOKC
    inv = 1.0 / (500000.0 ** (np.arange(64, dtype=np.float64) / 64))
    f = (pos[:, None] * inv[None, :]).astype(F32NP)
    cq = np.tile(np.cos(f) / np.sqrt(128.0), (1, 16))
    sq = np.tile(np.sin(f) / np.sqrt(128.0), (1, 16))
    ck = np.tile(np.cos(f), (1, 16))
    sk = np.tile(np.sin(f), (1, 16))
    return np.concatenate([cq, sq, ck, sk], 1).astype(BFNP)


def _tile(r):
    return np.ascontiguousarray(
        np.tile(np.asarray(r, F32NP).reshape(1, -1), (128, 1)))


def kernel(**inp):
    x = np.asarray(inp["L0_attn_hidden_states"], F32NP)
    cores = list(range(8))
    ident = np.eye(128, dtype=F32NP)

    ffn0_w = {
        "w_router": np.asarray(inp["L0_ffn_router_weight"], F32NP),
        "w_r_gate": np.asarray(inp["L0_ffn_r_gate"], F32NP),
        "w_r_up": np.asarray(inp["L0_ffn_r_up"], F32NP),
        "w_r_down": np.asarray(inp["L0_ffn_r_down"], F32NP),
        "w_sh_gate": np.asarray(inp["L0_ffn_sh_gate"], F32NP),
        "w_sh_up": np.asarray(inp["L0_ffn_sh_up"], F32NP),
        "w_sh_down": np.asarray(inp["L0_ffn_sh_down"], F32NP),
    }

    p = build_launch1(TOKC)
    nc = p.finish()
    ins = []
    for c in cores:
        b, qt = c // 4, c % 4
        d = dict(ffn0_w)
        d["ident_in"] = ident
        d["x_in"] = np.ascontiguousarray(
            x[b, qt * TOKC:(qt + 1) * TOKC, :].T)
        ins.append(d)
    r1 = run_bass_kernel_spmd(nc, ins, core_ids=cores).results
    x1 = [r1[c]["x1_out"] for c in cores]

    qks = np.asarray(inp["L0_attn_qk_scale"], F32NP)
    qko = np.asarray(inp["L0_attn_qk_offset"], F32NP)
    p = build_launch2(TOKC)
    nc = p.finish()
    ins = []
    for c in cores:
        b, qt = c // 4, c % 4
        ins.append(dict(
            x_in=x1[c],
            qkv_w=np.asarray(inp["L0_attn_qkv_w"], F32NP),
            qnb_in=_tile(inp["L0_attn_q_norm_w"]),
            knb_in=_tile(inp["L0_attn_k_norm_w"]),
            s0b_in=_tile(qks[:, 0]), s1b_in=_tile(qks[:, 1]),
            o0b_in=_tile(qko[:, 0]), o1b_in=_tile(qko[:, 1]),
            lrb_in=_tile(inp["L0_attn_lr_b"]),
            lr_w=np.pad(np.asarray(inp["L0_attn_lr_w"], F32NP),
                        ((0, 4), (0, 0))),
            ctab=_ctab(qt)))
    r2 = run_bass_kernel_spmd(nc, ins, core_ids=cores).results

    w0 = np.asarray(inp["L0_attn_w0"], F32NP)
    w1 = np.asarray(inp["L0_attn_w1"], F32NP)
    w2 = np.asarray(inp["L0_attn_w2"], F32NP)
    tttnb = _tile(inp["L0_attn_ttt_norm_w"])
    p = build_launch3(S=SEQ, QTOK=TOKC, CHUNK=512)
    nc = p.finish()
    ins = []
    for c in cores:
        b, fw = c // 4, c % 4
        qt = c % 4
        fq_u = np.concatenate(
            [r2[4 * b + q]["fq_out"][:, fw * 512:(fw + 1) * 512]
             for q in range(4)], 0)
        fk_u = np.concatenate(
            [r2[4 * b + q]["fk_out"][:, fw * 512:(fw + 1) * 512]
             for q in range(4)], 0)
        fv_u = np.concatenate(
            [r2[4 * b + q]["fv_out"][:, fw * 512:(fw + 1) * 512]
             for q in range(4)], 0)
        lr_u = np.concatenate(
            [np.ascontiguousarray(
                np.pad(r2[4 * b + q]["lr_out"][:, [fw, 4 + fw, 8 + fw]],
                       ((0, 0), (0, 1))))
             for q in range(4)], 0).astype(F32NP)
        ka_own = r2[c]["ka_out"]
        va_own = r2[c]["va_out"]
        if qt == 0:
            ka_prev = np.zeros_like(ka_own)
            va_prev = np.zeros_like(va_own)
            lo = np.full((128, 1), 2047.0, F32NP)
        else:
            ka_prev = r2[c - 1]["ka_out"]
            va_prev = r2[c - 1]["va_out"]
            lo = np.full((128, 1), -1e30, F32NP)
        ins.append(dict(
            fq_u=np.ascontiguousarray(fq_u),
            fk_u=np.ascontiguousarray(fk_u),
            fv_u=np.ascontiguousarray(fv_u), lr_u=lr_u,
            w0_in=np.ascontiguousarray(w0[fw]),
            w1_in=np.ascontiguousarray(w1[fw]),
            w2_in=np.ascontiguousarray(w2[fw]),
            tttnb_in=tttnb, ident_in=ident,
            qa_own=r2[c]["qa_out"],
            ka_ctx=np.concatenate([ka_prev, ka_own], 0),
            va_ctx=np.concatenate([va_prev, va_own], 0),
            lo_in=lo))
    r3 = run_bass_kernel_spmd(nc, ins, core_ids=cores).results

    gu_pk = np.asarray(inp["L1_ffn_gate_up_packed"])
    gu_sc = np.asarray(inp["L1_ffn_gate_up_scales"])
    dn_pk = np.asarray(inp["L1_ffn_down_packed"])
    dn_sc = np.asarray(inp["L1_ffn_down_scales"])
    p = build_launch4(TOKC)
    nc = p.finish()
    ins = []
    for c in cores:
        b, qt = c // 4, c % 4
        ottt = np.concatenate(
            [r3[4 * b + fw]["ottt_out"][qt * TOKC:(qt + 1) * TOKC, :]
             for fw in range(4)], 1)
        d = dict(ffn0_w)
        d.update(
            x1_in=x1[c], attn_in=r3[c]["attn_out"],
            ottt_in=np.ascontiguousarray(ottt),
            w_oproj=np.asarray(inp["L0_attn_o_proj_w"], F32NP),
            ident_in=ident,
            gu_pk=np.ascontiguousarray(gu_pk[c * 1408:(c + 1) * 1408, :]),
            gu_sc=np.ascontiguousarray(gu_sc[:, c * 1408:(c + 1) * 1408].T),
            dn_pk=np.ascontiguousarray(dn_pk[c * 256:(c + 1) * 256, :]),
            dn_sc=np.ascontiguousarray(dn_sc[:, c * 256:(c + 1) * 256].T))
        ins.append(d)
    r4 = run_bass_kernel_spmd(nc, ins, core_ids=cores).results
    wguT = np.concatenate([r4[c]["wguT_out"] for c in cores], 1)
    wdnT = np.concatenate([r4[c]["wdnT_out"] for c in cores], 1)
    wguT = np.ascontiguousarray(wguT)
    wdnT = np.ascontiguousarray(wdnT)

    p = build_launch5(TOKC)
    nc = p.finish()
    ins = []
    for c in cores:
        ins.append(dict(
            x3_in=r4[c]["x3_out"], wguT_in=wguT, wdnT_in=wdnT,
            w_q=np.asarray(inp["L1_attn_Wq"], F32NP),
            w_k=np.asarray(inp["L1_attn_Wk"], F32NP),
            w_v=np.asarray(inp["L1_attn_Wv"], F32NP)))
    r5 = run_bass_kernel_spmd(nc, ins, core_ids=cores).results

    p = build_launch6(TOKC)
    nc = p.finish()
    ins = []
    for c in cores:
        b = c // 4
        kvp = np.concatenate([r5[4 * b + j]["kvks_out"] for j in range(4)], 0)
        ins.append(dict(
            x4_in=r5[c]["x4_out"], q_in=r5[c]["q_out"],
            kvp_in=np.ascontiguousarray(kvp),
            w_o=np.asarray(inp["L1_attn_Wo"], F32NP),
            wguT_in=wguT, wdnT_in=wdnT))
    r6 = run_bass_kernel_spmd(nc, ins, core_ids=cores).results

    out = np.empty((2, SEQ, H), F32NP)
    for c in cores:
        b, qt = c // 4, c % 4
        out[b, qt * TOKC:(qt + 1) * TOKC, :] = r6[c]["x6_out"].T
    return out
